# version 03
# baseline (speedup 1.0000x reference)
import numpy as np
import concourse.bass as bass
import concourse.mybir as mybir
from concourse.bass_utils import run_bass_kernel_spmd

F32 = mybir.dt.float32
AF = mybir.ActivationFunctionType
ALU = mybir.AluOpType
AX = mybir.AxisListType

D = 2048
NK = 16
SEQ = 4096
NG = 8
GT = 512
CH = 128
HL = 16
P = 64
HP = HL * P
DS = 128
EPS = 1e-5
NSLOT = 3
NE = 8

SEMS = ["uld", "uld1", "wld", "wld1", "wld2", "fmld", "fmld1", "cst", "gcst", "vcst", "scst", "csq", "tsq",
        "tstat", "vp", "sp", "tp", "gp", "vun", "ufree", "tfm", "sfree",
        "fmfree", "vzx", "tyd", "vssd", "typ", "vyf", "vpo", "post", "cc",
        "cld", "vx1", "sq2", "tsq2", "trt", "vrt", "grt", "vg", "vfo", "outd"]


def build(taps=(), stop_after=None):
    nc = bass.Bass("TRN2", target_bir_lowering=False, num_devices=8)

    uT = nc.dram_tensor("uT", [D, SEQ], F32, kind="ExternalInput").ap()
    uq = nc.dram_tensor("uq", [D, 1024], F32, kind="ExternalInput").ap()
    w_tmT = nc.dram_tensor("w_tmT", [D, 2048], F32, kind="ExternalInput").ap()
    w_fmT = nc.dram_tensor("w_fmT", [D, 512], F32, kind="ExternalInput").ap()
    w_outT = nc.dram_tensor("w_outT", [HP, D], F32, kind="ExternalInput").ap()
    w_fc = nc.dram_tensor("w_fc", [NE, D, D], F32, kind="ExternalInput").ap()
    w_pj = nc.dram_tensor("w_pj", [NE, D, D], F32, kind="ExternalInput").ap()
    cblk = nc.dram_tensor("cblk", [128, 512], F32, kind="ExternalInput").ap()
    dfull = nc.dram_tensor("dfull", [128, HP], F32, kind="ExternalInput").ap()

    part_out = nc.dram_tensor("part_out", [4, D, 1024], F32)
    rs_out = nc.dram_tensor("rs_out", [D, 1024], F32)
    out = nc.dram_tensor("out", [D, 1024], F32, kind="ExternalOutput").ap()

    tap_shapes = {"un": [128, NK * GT], "zx": [128, 2 * 2048], "bn": [128, GT],
                  "cn": [128, GT], "da": [HL, GT], "y0": [128, HP],
                  "yf": [128, 8 * GT], "po": [128, 8 * GT], "x1": [128, NK * GT],
                  "ln": [128, NK * GT], "lg": [128, 32], "wt": [128, 32],
                  "sst": [128, HP], "u0": [128, NK * GT], "rows": [1, 8 * GT], "pbc": [128, 2 * GT], "all": [128, 51200], "lnw": [128, 2 * NK], "braw": [128, GT], "brow": [1, GT]}
    tap_t = {t: nc.dram_tensor("tap_" + t, tap_shapes[t], F32,
                               kind="ExternalOutput").ap() for t in taps}

    def sb(name, shape, off):
        assert off + int(np.prod(shape[1:])) * 4 <= 224 * 1024, (name, off)
        return nc.alloc_sbuf_tensor_at(name, shape, F32, offset=off)

    k_id = sb("k_id", [128, 128], 16512)
    k_tril = sb("k_tril", [128, 128], 17024)
    k_ones = sb("k_ones", [128, 1], 17536)
    k_ones1 = sb("k_ones1", [1, 128], 17568)
    c_blkt = sb("c_blk", [128, 512], 18080)
    c_blk = c_blkt[:, :]
    c_lnw = c_blkt[:, 0:16]
    c_lnb = c_blkt[:, 16:32]
    c_bnw = c_blkt[:, 32:33]
    c_cnw = c_blkt[:, 33:34]
    c_dtb = c_blkt[0:HL, 34:35]
    c_alog = c_blkt[0:HL, 35:36]
    c_wrt = c_blkt[:, 128:256]
    c_aneg = sb("c_aneg", [HL, 1], 20160)
    c_dfull = sb("c_dfull", [128, HP], 20224)
    S_state = sb("S_state", [128, HP], 24320)
    r_w = sb("r_w", [128, NSLOT * 2048], 28416)
    s_rows = sb("s_rows", [1, 8 * GT], 52992)

    A0 = 68 * 1024
    s_u = sb("s_u", [128, NK * GT], A0)
    s_un = sb("s_un", [128, NK * GT], A0 + 32 * 1024)
    s_zx = sb("s_zx", [128, 2 * 2048], A0 + 64 * 1024)
    s_B = sb("s_B", [128, GT], A0 + 80 * 1024)
    s_C = sb("s_C", [128, GT], A0 + 82 * 1024)
    s_sq = sb("s_sq", [128, 2 * GT], A0 + 84 * 1024)
    s_dA = sb("s_dA", [HL, GT], A0 + 88 * 1024)
    s_dts = sb("s_dts", [HL, GT], A0 + 90 * 1024)
    s_Acs = sb("s_Acs", [HL, CH], A0 + 92 * 1024)
    s_AcsT = sb("s_AcsT", [128, HL], A0 + 92 * 1024 + 512)
    s_d1 = sb("s_d1", [128, HL], A0 + 92 * 1024 + 576)
    s_d2 = sb("s_d2", [128, HL], A0 + 92 * 1024 + 640)
    s_f = sb("s_f", [128, HL], A0 + 92 * 1024 + 704)
    s_e1 = sb("s_e1", [128, HL], A0 + 92 * 1024 + 768)
    s_E2 = sb("s_E2", [128, HL], A0 + 92 * 1024 + 832)
    s_ds = sb("s_ds", [128, HL], A0 + 92 * 1024 + 896)
    s_Dvb = sb("s_Dvb", [128, HL], A0 + 92 * 1024 + 960)
    s_mr = sb("s_mr", [1, 32], A0 + 93 * 1024)
    s_mc = sb("s_mc", [HL, 1], A0 + 93 * 1024 + 128)
    s_scm = sb("s_scm", [128, CH], A0 + 94 * 1024)
    s_Btm = sb("s_Btm", [128, DS], A0 + 94 * 1024 + 512)
    s_Mall = sb("s_Mall", [128, HL * CH], A0 + 95 * 1024)
    s_xd = sb("s_xd", [128, HP], A0 + 103 * 1024)
    s_y = sb("s_y", [128, HP], A0 + 107 * 1024)
    s_g1 = sb("s_g1", [128, HP], A0 + 111 * 1024)
    s_sz = sb("s_sz", [128, HP], A0 + 115 * 1024)
    s_yfm = sb("s_yfm", [128, 8 * GT], A0 + 119 * 1024)
    s_po = sb("s_po", [128, 8 * GT], A0 + 135 * 1024)
    s_wfm = sb("s_wfm", [128, 2 * 512], A0 + 151 * 1024)

    s_fo = sb("s_fo", [128, NK * GT], A0)
    s_ln = sb("s_ln", [128, NK * GT], A0 + 32 * 1024)
    s_uq = sb("s_uq", [128, NK * GT], A0 + 64 * 1024)
    s_g = sb("s_g", [128, NK * GT], A0 + 96 * 1024)
    s_sq2 = sb("s_sq2", [128, 2 * GT], A0 + 128 * 1024)
    s_lg = sb("s_lg", [8, GT], A0 + 132 * 1024)
    s_wrow = sb("s_wrow", [1, GT], A0 + 134 * 1024)
    s_lgt = sb("s_lgt", [128, 32], A0 + 136 * 1024)
    s_wtk = sb("s_wtk", [128, 32], A0 + 136 * 1024 + 128)
    s_lg2 = sb("s_lg2", [128, 32], A0 + 136 * 1024 + 256)
    s_m1 = sb("s_m1", [128, 1], A0 + 136 * 1024 + 384)
    s_m2 = sb("s_m2", [128, 1], A0 + 136 * 1024 + 416)
    s_p1 = sb("s_p1", [128, 1], A0 + 136 * 1024 + 448)
    s_p2 = sb("s_p2", [128, 1], A0 + 136 * 1024 + 480)
    s_dd = sb("s_dd", [128, 1], A0 + 136 * 1024 + 512)
    s_ee = sb("s_ee", [128, 1], A0 + 136 * 1024 + 544)
    s_wfm8 = sb("s_wfm8", [8, GT], A0 + 137 * 1024)
    s_wbc = sb("s_wbc", [128, NE * GT], A0 + 139 * 1024)

    PS = [nc.alloc_psum_tensor(f"ps{i}", [128, 512], F32) for i in range(8)]

    sems = {n: nc.alloc_semaphore("sem_" + n) for n in SEMS}
    tables = {}
    state = {"rec": True}

    strip_sched = []
    for gi in range(NG):
        for ps in range(2):
            for k in range(NK):
                strip_sched.append(("tm", k))
        for half in range(2):
            for e in range(8):
                strip_sched.append(("out", e))
    for tg in range(2):
        for e in range(NE):
            for mh in range(2):
                for k in range(NK):
                    strip_sched.append(("fc", e, k))
            for mh in range(2):
                for k in range(NK):
                    strip_sched.append(("pj", e, k))
    N_STRIPS = len(strip_sched)

    def strip_src(ent):
        if ent[0] == "tm":
            return w_tmT[ent[1] * 128:(ent[1] + 1) * 128, :]
        if ent[0] == "out":
            return w_outT[ent[1] * 128:(ent[1] + 1) * 128, :]
        if ent[0] == "fc":
            return w_fc[ent[1], ent[2] * 128:(ent[2] + 1) * 128, :]
        return w_pj[ent[1], ent[2] * 128:(ent[2] + 1) * 128, :]

    def emit(eng, E):
        cnt = {n: 0 for n in SEMS}
        rec = state["rec"]
        TE = E if eng == "tensor" else None
        VE = E if eng == "vector" else None
        SC = E if eng == "scalar" else None
        GP = E if eng == "gpsimd" else None
        SY = E if eng == "sync" else None
        R = s_rows

        def inc(bi, ch, n=1):
            cnt[ch] += n
            if bi is not None:
                bi.then_inc(sems[ch], n)
            return cnt[ch]

        def wt(Eng, ch, val):
            if Eng is not None and val is not None and val > 0:
                Eng.wait_ge(sems[ch], val)

        def rec_ev(key, ch):
            if rec:
                tables[key] = (ch, cnt[ch])

        def wt_ev(Eng, key):
            ch, val = tables.get(key, (None, None))
            if Eng is not None and ch is not None and val > 0:
                Eng.wait_ge(sems[ch], val)

        def do_tap(name, buf, ch=None, val=None):
            if name not in tap_t:
                return
            if ch is not None:
                wt(GP, ch, val)
            bi = GP.dma_start(out=tap_t[name], in_=buf) if GP else None
            inc(bi, "gp", 16)
            for Eng in (TE, VE, SC, SY):
                wt(Eng, "gp", cnt["gp"])

        ULD = ["uld", "uld1"]
        WLD = ["wld", "wld1", "wld2"]
        FMLD = ["fmld", "fmld1"]

        strips_issued = [0]
        sidx = [0]

        def strips_issue_upto(n):
            while strips_issued[0] < min(n, N_STRIPS):
                i = strips_issued[0]
                if i >= NSLOT:
                    wt_ev(SY, f"sd:{i - NSLOT}")
                src = strip_src(strip_sched[i])
                bi = SY.dma_start(
                    out=r_w[:, (i % NSLOT) * 2048:(i % NSLOT) * 2048 + src.shape[1]],
                    in_=src) if SY else None
                inc(bi, WLD[i % NSLOT], 16)
                strips_issued[0] += 1

        def strip_take():
            i = sidx[0]
            sidx[0] += 1
            strips_issue_upto(i + NSLOT + 1)
            return i, (i % NSLOT) * 2048

        bi = SC.dma_start(out=c_blk, in_=cblk) if SC else None
        inc(bi, "cst", 16)
        bi = SC.dma_start(out=c_dfull[:, :], in_=dfull) if SC else None
        ncst = inc(bi, "cst", 16)

        bi = GP.memset(k_id[:, :], 1.0) if GP else None
        inc(bi, "gp")
        wt(GP, "gp", cnt["gp"])
        bi = GP.affine_select(out=k_id[:, :], in_=k_id[:, :],
                              compare_op=ALU.is_equal, fill=0.0, base=0,
                              pattern=[[-1, 128]], channel_multiplier=1) if GP else None
        inc(bi, "gp")
        bi = GP.memset(k_tril[:, :], 1.0) if GP else None
        inc(bi, "gp")
        wt(GP, "gp", cnt["gp"])
        bi = GP.affine_select(out=k_tril[:, :], in_=k_tril[:, :],
                              compare_op=ALU.is_ge, fill=0.0, base=0,
                              pattern=[[1, 128]], channel_multiplier=-1) if GP else None
        inc(bi, "gp")
        bi = GP.memset(k_ones[:, :], 1.0) if GP else None
        inc(bi, "gp")
        bi = GP.memset(k_ones1[:, :], 1.0) if GP else None
        inc(bi, "gp")
        bi = GP.memset(S_state[:, :], 0.0) if GP else None
        inc(bi, "gcst")

        wt(SC, "cst", ncst)
        bi = SC.activation(c_aneg[:, :], c_alog, AF.Exp) if SC else None
        inc(bi, "scst")
        wt(VE, "scst", 1)
        bi = VE.tensor_scalar_mul(c_aneg[:, :], c_aneg[:, :], -1.0) if VE else None
        inc(bi, "vcst")

        wt(TE, "gcst", 1)
        wt(TE, "cst", ncst)
        wt(VE, "gcst", 1)
        wt(VE, "cst", ncst)
        wt(SC, "gcst", 1)

        for k in range(NK):
            bi = SC.dma_start(out=s_u[:, k * GT:(k + 1) * GT],
                              in_=uT[k * 128:(k + 1) * 128, 0:GT]) if SC else None
            inc(bi, "uld", 16)
        for kk in range(2):
            bi = SC.dma_start(out=s_wfm[:, (kk % 2) * 512:(kk % 2) * 512 + 512],
                              in_=w_fmT[kk * 128:(kk + 1) * 128, :]) if SC else None
            inc(bi, FMLD[kk % 2], 16)
        strips_issue_upto(NSLOT)

        caps = {}

        for gi in range(NG):
            if gi > 0:
                wt(TE, "vpo", caps[("vpo", gi - 1)])
                wt(TE, "sp", caps[("sp", gi - 1)])

            for k in range(NK):
                if k == 0:
                    wt(TE, ULD[gi % 2], 256 * (gi // 2 + 1))
                bi = TE.matmul(PS[0][0:1, :], k_ones[:, :],
                               s_u[:, k * GT:(k + 1) * GT],
                               start=(k == 0), stop=(k == NK - 1)) if TE else None
                if k == NK - 1:
                    inc(bi, "tstat")
                if k == 0:
                    wt(SC, ULD[gi % 2], 256 * (gi // 2 + 1))
                wt(SC, "tsq", 16 * gi + k - 1)
                sbi = SC.activation(s_sq[:, (k % 2) * GT:(k % 2 + 1) * GT],
                                    s_u[:, k * GT:(k + 1) * GT],
                                    AF.Square) if SC else None
                inc(sbi, "csq")
                wt(TE, "csq", 16 * gi + k + 1)
                bi = TE.matmul(PS[1][0:1, :], k_ones[:, :],
                               s_sq[:, (k % 2) * GT:(k % 2 + 1) * GT],
                               start=(k == 0), stop=(k == NK - 1)) if TE else None
                inc(bi, "tsq")

            if gi > 0:
                wt(VE, "sfree", caps[("sfree", gi - 1)])
            wt(VE, "tstat", cnt["tstat"])
            wt(VE, "tsq", cnt["tsq"])
            bi = VE.tensor_scalar_mul(R[0:1, 0 * GT:1 * GT], PS[0][0:1, :], 1.0 / D) if VE else None
            inc(bi, "vp")
            bi = VE.tensor_scalar_mul(R[0:1, 1 * GT:2 * GT], PS[1][0:1, :], 1.0 / D) if VE else None
            inc(bi, "vp")
            wt(VE, "vp", cnt["vp"])
            bi = VE.tensor_tensor(R[0:1, 2 * GT:3 * GT], R[0:1, 0 * GT:1 * GT], R[0:1, 0 * GT:1 * GT], ALU.mult) if VE else None
            inc(bi, "vp")
            wt(VE, "vp", cnt["vp"])
            bi = VE.tensor_tensor(R[0:1, 1 * GT:2 * GT], R[0:1, 1 * GT:2 * GT], R[0:1, 2 * GT:3 * GT], ALU.subtract) if VE else None
            inc(bi, "vp")
            wt(VE, "vp", cnt["vp"])
            bi = VE.tensor_scalar_add(R[0:1, 2 * GT:3 * GT], R[0:1, 1 * GT:2 * GT], EPS) if VE else None
            inc(bi, "vp")
            wt(SC, "vp", cnt["vp"])
            bi = SC.activation(R[0:1, 3 * GT:4 * GT], R[0:1, 2 * GT:3 * GT], AF.Ln) if SC else None
            inc(bi, "sp")
            wt(SC, "sp", cnt["sp"])
            bi = SC.activation(R[0:1, 3 * GT:4 * GT], R[0:1, 3 * GT:4 * GT], AF.Exp, scale=-0.5) if SC else None
            inc(bi, "sp")
            wt(VE, "sp", cnt["sp"])
            bi = VE.reciprocal(R[0:1, 4 * GT:5 * GT], R[0:1, 2 * GT:3 * GT]) if VE else None
            inc(bi, "vp")
            wt(VE, "vp", cnt["vp"])
            bi = VE.tensor_tensor(R[0:1, 4 * GT:5 * GT], R[0:1, 4 * GT:5 * GT], R[0:1, 1 * GT:2 * GT], ALU.mult) if VE else None
            inc(bi, "vp")
            wt(VE, "vp", cnt["vp"])
            bi = VE.tensor_scalar_add(R[0:1, 4 * GT:5 * GT], R[0:1, 4 * GT:5 * GT], EPS) if VE else None
            inc(bi, "vp")
            wt(SC, "vp", cnt["vp"])
            bi = SC.activation(R[0:1, 4 * GT:5 * GT], R[0:1, 4 * GT:5 * GT], AF.Ln) if SC else None
            inc(bi, "sp")
            wt(SC, "sp", cnt["sp"])
            bi = SC.activation(R[0:1, 4 * GT:5 * GT], R[0:1, 4 * GT:5 * GT], AF.Exp, scale=-0.5) if SC else None
            inc(bi, "sp")
            wt(VE, "sp", cnt["sp"])
            bi = VE.tensor_tensor(R[0:1, 5 * GT:6 * GT], R[0:1, 3 * GT:4 * GT], R[0:1, 4 * GT:5 * GT], ALU.mult) if VE else None
            inc(bi, "vp")
            wt(VE, "vp", cnt["vp"])
            bi = VE.scalar_tensor_tensor(R[0:1, 6 * GT:7 * GT], R[0:1, 0 * GT:1 * GT], -1.0, R[0:1, 5 * GT:6 * GT],
                                         ALU.mult, ALU.mult) if VE else None
            inc(bi, "vp")

            cap_rows = cnt["vp"]
            if gi == 0:
                do_tap("u0", s_u[:, :], "vp", cap_rows)
                do_tap("rows", s_rows[:, :], "vp", cap_rows)

            wt(TE, "vp", cap_rows)
            bi = TE.matmul(PS[2][:, :], k_ones1[:, :], R[0:1, 5 * GT:6 * GT],
                           start=True, stop=True) if TE else None
            inc(bi, "tp")
            bi = TE.matmul(PS[3][:, :], k_ones1[:, :], R[0:1, 6 * GT:7 * GT],
                           start=True, stop=True) if TE else None
            inc(bi, "tp")
            cap_bc = cnt["tp"]

            wt(VE, "tp", cap_bc)
            bi = VE.tensor_copy(s_sq[:, 0:GT], PS[2][:, :]) if VE else None
            inc(bi, "vp")
            bi = VE.tensor_copy(s_sq[:, GT:2 * GT], PS[3][:, :]) if VE else None
            inc(bi, "vp")
            wt(VE, "vp", cnt["vp"])
            for k in range(NK):
                bi = VE.tensor_tensor(s_un[:, k * GT:(k + 1) * GT],
                                      s_u[:, k * GT:(k + 1) * GT],
                                      s_sq[:, 0:GT], ALU.mult) if VE else None
                inc(bi, "vp")
                wt(VE, "vp", cnt["vp"])
                bi = VE.tensor_tensor(s_un[:, k * GT:(k + 1) * GT],
                                      s_un[:, k * GT:(k + 1) * GT],
                                      s_sq[:, GT:2 * GT], ALU.add) if VE else None
                inc(bi, "ufree")
                wt(VE, "ufree", cnt["ufree"])
                bi = VE.tensor_scalar(s_un[:, k * GT:(k + 1) * GT],
                                      s_un[:, k * GT:(k + 1) * GT],
                                      c_blkt[:, k:k + 1], c_blkt[:, 16 + k:17 + k],
                                      ALU.mult, ALU.add) if VE else None
                inc(bi, "vun")
            if gi == 0:
                if "pbc" in tap_t:
                    wt(VE, "vun", 16)
                    bi = VE.tensor_copy(s_sq[:, 0:GT], PS[2][:, :]) if VE else None
                    inc(bi, "vp")
                    bi = VE.tensor_copy(s_sq[:, GT:2 * GT], PS[3][:, :]) if VE else None
                    inc(bi, "vp")
                    do_tap("pbc", s_sq[:, :], "vp", cnt["vp"])
                do_tap("un", s_un[:, :], "vun", 16)
                if "lnw" in tap_t:
                    bi = VE.tensor_copy(s_sq[:, 0:NK], c_lnw) if VE else None
                    inc(bi, "vp")
                    bi = VE.tensor_copy(s_sq[:, NK:2 * NK], c_lnb) if VE else None
                    inc(bi, "vp")
                    do_tap("lnw", s_sq[:, 0:2 * NK], "vp", cnt["vp"])
                if "all" in tap_t:
                    s_all = nc.alloc_sbuf_tensor_at("s_all", [128, 51200], F32,
                                                    offset=17024)
                    do_tap("all", s_all[:, :], "vun", 16)
                if "zx" in tap_t:
                    wt(VE, "vun", 16)
                    bi = VE.tensor_copy(s_zx[:, 0:GT], s_un[:, 8 * GT:9 * GT]) if VE else None
                    inc(bi, "vp")
                    bi = VE.tensor_copy(s_zx[:, GT:2 * GT], s_un[:, 0:GT]) if VE else None
                    inc(bi, "vp")
                    do_tap("zx", s_zx[:, :], "vp", cnt["vp"])
            if stop_after == "a5" and gi == 0:
                return cnt
            if gi + 1 < NG:
                for k in range(NK):
                    wt(SC, "ufree", 16 * gi + k + 1)
                    bi = SC.dma_start(
                        out=s_u[:, k * GT:(k + 1) * GT],
                        in_=uT[k * 128:(k + 1) * 128,
                               (gi + 1) * GT:(gi + 2) * GT]) if SC else None
                    inc(bi, ULD[(gi + 1) % 2], 16)

            for k in range(NK):
                gidx = 16 * gi + k
                wt(TE, FMLD[gidx % 2], 16 * (gidx // 2 + 1))
                wt(TE, "vun", 16 * gi + k + 1)
                slot = gidx % 2
                wf = s_wfm[:, slot * 512:slot * 512 + 512]
                bi = TE.matmul(PS[6][:, :], wf[:, 0:128],
                               s_un[:, k * GT:(k + 1) * GT],
                               start=(k == 0), stop=(k == NK - 1)) if TE else None
                if k == NK - 1:
                    inc(bi, "tfm")
                bi2 = TE.matmul(PS[7][:, :], wf[:, 128:256],
                                s_un[:, k * GT:(k + 1) * GT],
                                start=(k == 0), stop=(k == NK - 1)) if TE else None
                if k == NK - 1:
                    inc(bi2, "tfm")
                bi3 = TE.matmul(PS[0][0:HL, :], wf[:, 256:272],
                                s_un[:, k * GT:(k + 1) * GT],
                                start=(k == 0), stop=(k == NK - 1)) if TE else None
                inc(bi3, "fmfree")
                nxt = gidx + 2
                if nxt < 16 * NG:
                    wt(SC, "fmfree", nxt - 1)
                    bi = SC.dma_start(
                        out=s_wfm[:, (nxt % 2) * 512:(nxt % 2) * 512 + 512],
                        in_=w_fmT[(nxt % 16) * 128:(nxt % 16) * 128 + 128, :]) if SC else None
                    inc(bi, FMLD[nxt % 2], 16)

            wt(VE, "tfm", cnt["tfm"])
            bi = VE.tensor_copy(s_B[:, :], PS[6][:, :]) if VE else None
            inc(bi, "vp")
            bi = VE.tensor_copy(s_C[:, :], PS[7][:, :]) if VE else None
            inc(bi, "vp")
            cap_bc_evac = cnt["vp"]
            if gi == 0:
                do_tap("braw", s_B[:, :], "vp", cap_bc_evac)
            for X, cx in ((s_B, c_bnw), (s_C, c_cnw)):
                wt(SC, "vp", cap_bc_evac)
                sqslot = 0 if X is s_B else GT
                bi = SC.activation(s_sq[:, sqslot:sqslot + GT], X[:, :],
                                   AF.Square) if SC else None
                inc(bi, "sp")
                if gi == 0 and X is s_B:
                    do_tap("brow", R[0:1, 2 * GT:3 * GT], "sp", cnt["sp"])
                wt(TE, "sp", cnt["sp"])
                if X is s_C:
                    wt(TE, "vp", cnt["vp"])
                bi = TE.matmul(PS[2][0:1, :], k_ones[:, :], s_sq[:, sqslot:sqslot + GT],
                               start=True, stop=True) if TE else None
                inc(bi, "tp")
                wt(VE, "tp", cnt["tp"])
                bi = VE.tensor_scalar(R[0:1, 2 * GT:3 * GT], PS[2][0:1, :], 1.0 / DS, EPS,
                                      ALU.mult, ALU.add) if VE else None
                inc(bi, "vp")
                wt(SC, "vp", cnt["vp"])
                bi = SC.activation(R[0:1, 2 * GT:3 * GT], R[0:1, 2 * GT:3 * GT], AF.Ln) if SC else None
                inc(bi, "sp")
                wt(SC, "sp", cnt["sp"])
                bi = SC.activation(R[0:1, 2 * GT:3 * GT], R[0:1, 2 * GT:3 * GT], AF.Exp, scale=-0.5) if SC else None
                inc(bi, "sp")
                wt(TE, "sp", cnt["sp"])
                bi = TE.matmul(PS[3][:, :], k_ones1[:, :], R[0:1, 2 * GT:3 * GT],
                               start=True, stop=True) if TE else None
                inc(bi, "tp")
                wt(VE, "tp", cnt["tp"])
                bi = VE.tensor_tensor(X[:, :], X[:, :], PS[3][:, :], ALU.mult) if VE else None
                inc(bi, "vp")
                wt(VE, "vp", cnt["vp"])
                bi = VE.tensor_scalar_mul(X[:, :], X[:, :], cx) if VE else None
                inc(bi, "vp")
            cap_bcn = cnt["vp"]
            if stop_after == "a7" and gi == 0:
                do_tap("bn", s_B[:, :], "vp", cap_bcn)
                do_tap("cn", s_C[:, :], "vp", cap_bcn)
                return cnt
            wt(SC, "tfm", cnt["tfm"])
            bi = SC.activation(s_dts[:, :], PS[0][0:HL, :], AF.Exp,
                               bias=c_dtb) if SC else None
            inc(bi, "sp")
            wt(VE, "sp", cnt["sp"])
            bi = VE.tensor_scalar_add(s_dts[:, :], s_dts[:, :], 1.0) if VE else None
            inc(bi, "vp")
            wt(SC, "vp", cnt["vp"])
            bi = SC.activation(s_dts[:, :], s_dts[:, :], AF.Ln) if SC else None
            inc(bi, "sp")
            wt(VE, "sp", cnt["sp"])
            wt(VE, "vcst", 1)
            bi = VE.tensor_scalar_mul(s_dA[:, :], s_dts[:, :], c_aneg[:, :]) if VE else None
            inc(bi, "vp")
            cap_da = cnt["vp"]
            cap_sp_a7 = cnt["sp"]
            if stop_after == "a7b" and gi == 0:
                do_tap("bn", s_B[:, :], "vp", cap_da)
                do_tap("cn", s_C[:, :], "vp", cap_da)
                do_tap("da", s_dA[:, :], "vp", cap_da)
                return cnt

            if gi == 0:
                do_tap("bn", s_B[:, :], "vp", cap_bcn)
                do_tap("cn", s_C[:, :], "vp", cap_bcn)
                do_tap("da", s_dA[:, :], "vp", cap_da)

            for ps in range(2):
                wt(TE, "sp", cap_sp_a7)
                wt(TE, "vp", cap_bcn)
                wt(TE, "vssd", caps.get(("vssd_last",), 0))
                wt(TE, "vyf", cnt["vyf"])
                for k in range(NK):
                    si, soff = strip_take()
                    wt(TE, WLD[si % NSLOT], 16 * (si // NSLOT + 1))
                    last = None
                    for tt2 in range(2):
                        tglob = ps * 2 + tt2
                        for ns in range(4):
                            last = TE.matmul(
                                PS[tt2 * 4 + ns][:, :],
                                s_un[:, k * GT + tglob * 128:k * GT + tglob * 128 + 128],
                                r_w[:, soff + ns * 512:soff + (ns + 1) * 512],
                                start=(k == 0), stop=(k == NK - 1)) if TE else None
                    inc(last, "sfree")
                    rec_ev(f"sd:{si}", "sfree")
                cap_zx_mm = cnt["sfree"]
                wt(VE, "sfree", cap_zx_mm)
                for tt2 in range(2):
                    for ns in range(4):
                        bi = VE.tensor_copy(
                            s_zx[:, tt2 * 2048 + ns * 512:tt2 * 2048 + (ns + 1) * 512],
                            PS[tt2 * 4 + ns][:, :]) if VE else None
                        inc(bi, "vzx")
                cap_vzx = cnt["vzx"]
                if gi == 0 and ps == 0:
                    do_tap("zx", s_zx[:, :], "vzx", cap_vzx)

                for tt2 in range(2):
                    cloc = ps * 2 + tt2
                    zoff = tt2 * 2048
                    wt(VE, "vp", cap_da)
                    bi = VE.tensor_tensor_scan(
                        s_Acs[:, :], k_ones[0:HL, 0:1].broadcast_to([HL, CH]),
                        s_dA[:, cloc * CH:(cloc + 1) * CH], 0.0,
                        ALU.mult, ALU.add) if VE else None
                    inc(bi, "vp")
                    wt(VE, "vp", cnt["vp"])
                    bi = VE.tensor_tensor(s_mc[:, :], s_Acs[:, 0:1],
                                          s_Acs[:, CH - 1:CH], ALU.add) if VE else None
                    inc(bi, "vp")
                    wt(VE, "vp", cnt["vp"])
                    bi = VE.tensor_scalar_mul(s_mc[:, :], s_mc[:, :], 0.5) if VE else None
                    inc(bi, "vp")
                    cap_scan = cnt["vp"]

                    wt(TE, "vzx", cap_vzx)
                    wt(TE, "vp", cap_scan)
                    bi = TE.matmul(PS[0][:, 0:128], s_B[:, cloc * 128:(cloc + 1) * 128],
                                   s_C[:, cloc * 128:(cloc + 1) * 128],
                                   start=True, stop=True) if TE else None
                    inc(bi, "tp")
                    bi = TE.transpose(PS[1][:, 0:128],
                                      s_B[:, cloc * 128:(cloc + 1) * 128],
                                      k_id[:, :]) if TE else None
                    inc(bi, "tp")
                    bi = TE.transpose(PS[0][:, 128:128 + HL], s_Acs[:, :],
                                      k_id[0:HL, 0:HL]) if TE else None
                    inc(bi, "tp")
                    bi = TE.transpose(PS[0][0:1, 176:176 + HL], s_mc[:, :],
                                      k_id[0:HL, 0:HL]) if TE else None
                    inc(bi, "tp")
                    bi = TE.transpose(PS[0][0:1, 192:192 + HL],
                                      s_Acs[:, CH - 1:CH],
                                      k_id[0:HL, 0:HL]) if TE else None
                    inc(bi, "tp")
                    cap_t1 = cnt["tp"]
                    wt(VE, "tp", cap_t1)
                    bi = VE.tensor_tensor(s_scm[:, :], PS[0][:, 0:128],
                                          k_tril[:, :], ALU.mult) if VE else None
                    inc(bi, "vp")
                    bi = VE.tensor_copy(s_Btm[:, :], PS[1][:, 0:128]) if VE else None
                    inc(bi, "vp")
                    bi = VE.tensor_copy(s_AcsT[:, :], PS[0][:, 128:128 + HL]) if VE else None
                    inc(bi, "vp")
                    bi = VE.tensor_copy(s_mr[0:1, 0:HL], PS[0][0:1, 176:176 + HL]) if VE else None
                    inc(bi, "vp")
                    bi = VE.tensor_copy(s_mr[0:1, 16:16 + HL], PS[0][0:1, 192:192 + HL]) if VE else None
                    inc(bi, "vp")
                    cap_ev1 = cnt["vp"]
                    wt(TE, "vp", cap_ev1)
                    bi = TE.matmul(PS[0][:, 144:144 + HL], k_ones1[:, :],
                                   s_mr[0:1, 0:HL], start=True, stop=True) if TE else None
                    inc(bi, "tp")
                    bi = TE.matmul(PS[0][:, 160:160 + HL], k_ones1[:, :],
                                   s_mr[0:1, 16:16 + HL], start=True, stop=True) if TE else None
                    inc(bi, "tp")
                    cap_t2 = cnt["tp"]
                    wt(VE, "tp", cap_t2)
                    bi = VE.tensor_tensor(s_d1[:, :], PS[0][:, 144:144 + HL],
                                          s_AcsT[:, :], ALU.subtract) if VE else None
                    inc(bi, "vp")
                    bi = VE.tensor_tensor(s_d2[:, :], PS[0][:, 160:160 + HL],
                                          s_AcsT[:, :], ALU.subtract) if VE else None
                    inc(bi, "vp")
                    cap_d = cnt["vp"]
                    wt(SC, "vp", cap_d)
                    bi = SC.activation(s_f[:, :], s_d1[:, :], AF.Exp) if SC else None
                    inc(bi, "sp")
                    bi = SC.activation(s_e1[:, :], s_d1[:, :], AF.Exp, scale=-1.0) if SC else None
                    inc(bi, "sp")
                    bi = SC.activation(s_E2[:, :], s_AcsT[:, :], AF.Exp) if SC else None
                    inc(bi, "sp")
                    bi = SC.activation(s_ds[:, :], s_d2[:, :], AF.Exp) if SC else None
                    inc(bi, "sp")
                    bi = SC.activation(s_Dvb[:, :], PS[0][:, 160:160 + HL], AF.Exp) if SC else None
                    inc(bi, "sp")
                    cap_exp = cnt["sp"]
                    wt(VE, "sp", cap_exp)
                    bi = VE.tensor_tensor(
                        s_Mall[:, :].rearrange("p (h l) -> p h l", h=HL),
                        s_scm[:, :].unsqueeze(1).broadcast_to([128, HL, CH]),
                        s_f[:, :].unsqueeze(2).broadcast_to([128, HL, CH]),
                        ALU.mult) if VE else None
                    inc(bi, "vp")
                    bi = VE.tensor_tensor(
                        s_xd[:, :].rearrange("p (h q) -> p h q", h=HL),
                        s_zx[:, zoff + 1024:zoff + 2048].rearrange("p (h q) -> p h q", h=HL),
                        s_ds[:, :].unsqueeze(2).broadcast_to([128, HL, P]),
                        ALU.mult) if VE else None
                    inc(bi, "vp")
                    cap_mall = cnt["vp"]

                    wt(TE, "vp", cap_mall)
                    for h in range(HL):
                        bank = PS[2] if h < 8 else PS[3]
                        co = (h % 8) * 64
                        bi = TE.matmul(bank[:, co:co + 64],
                                       s_Mall[:, h * CH:(h + 1) * CH],
                                       s_zx[:, zoff + 1024 + h * P:zoff + 1024 + (h + 1) * P],
                                       start=True, stop=True) if TE else None
                        inc(bi, "tp")
                    for half in range(2):
                        bi = TE.matmul(PS[4 + half][:, :],
                                       s_C[:, cloc * 128:(cloc + 1) * 128],
                                       S_state[:, half * 512:(half + 1) * 512],
                                       start=True, stop=True) if TE else None
                        inc(bi, "tp")
                    bi = TE.matmul(PS[6][:, :], s_Btm[:, :],
                                   s_xd[:, 0:512], start=True, stop=True) if TE else None
                    inc(bi, "tp")
                    bi = TE.matmul(PS[7][:, :], s_Btm[:, :],
                                   s_xd[:, 512:1024], start=True, stop=True) if TE else None
                    cap_yd = inc(bi, "tyd")

                    wt(VE, "tyd", cap_yd)
                    bi = VE.tensor_tensor(
                        S_state[:, :].rearrange("p (h q) -> p h q", h=HL),
                        S_state[:, :].rearrange("p (h q) -> p h q", h=HL),
                        s_Dvb[:, :].unsqueeze(2).broadcast_to([128, HL, P]),
                        ALU.mult) if VE else None
                    inc(bi, "vp")
                    wt(VE, "vp", cnt["vp"])
                    for half in range(2):
                        bi = VE.tensor_tensor(S_state[:, half * 512:(half + 1) * 512],
                                              S_state[:, half * 512:(half + 1) * 512],
                                              PS[6 + half][:, :], ALU.add) if VE else None
                        inc(bi, "vp")
                    for half in range(2):
                        bi = VE.tensor_tensor(
                            s_y[:, half * 512:(half + 1) * 512].rearrange(
                                "p (h q) -> p h q", h=8),
                            PS[2 + half][:, :].rearrange("p (h q) -> p h q", h=8),
                            s_e1[:, half * 8:(half + 1) * 8].unsqueeze(2).broadcast_to(
                                [128, 8, P]),
                            ALU.mult) if VE else None
                        inc(bi, "vp")
                        bi = VE.tensor_tensor(
                            s_g1[:, half * 512:(half + 1) * 512].rearrange(
                                "p (h q) -> p h q", h=8),
                            PS[4 + half][:, :].rearrange("p (h q) -> p h q", h=8),
                            s_E2[:, half * 8:(half + 1) * 8].unsqueeze(2).broadcast_to(
                                [128, 8, P]),
                            ALU.mult) if VE else None
                        inc(bi, "vp")
                    wt(VE, "vp", cnt["vp"])
                    bi = VE.tensor_tensor(s_y[:, :], s_y[:, :], s_g1[:, :], ALU.add) if VE else None
                    inc(bi, "vp")
                    bi = VE.tensor_tensor(s_g1[:, :], s_zx[:, zoff + 1024:zoff + 2048],
                                          c_dfull[:, :], ALU.mult) if VE else None
                    inc(bi, "vp")
                    wt(VE, "vp", cnt["vp"])
                    bi = VE.tensor_tensor(s_y[:, :], s_y[:, :], s_g1[:, :], ALU.add) if VE else None
                    inc(bi, "vp")
                    cap_ypre = cnt["vp"]
                    wt(SC, "vzx", cap_vzx)
                    bi = SC.activation(s_sz[:, :], s_zx[:, zoff:zoff + 1024],
                                       AF.Silu) if SC else None
                    inc(bi, "sp")
                    cap_silu = cnt["sp"]
                    wt(VE, "sp", cap_silu)
                    wt(VE, "vp", cap_ypre)
                    bi = VE.tensor_tensor(s_y[:, :], s_y[:, :], s_sz[:, :], ALU.mult) if VE else None
                    vssd_c = inc(bi, "vssd")
                    caps[("vssd_last",)] = vssd_c
                    if gi == 0 and cloc == 0:
                        do_tap("y0", s_y[:, :], "vssd", vssd_c)
                        do_tap("sst", S_state[:, :], "vssd", vssd_c)

                    wt(TE, "vssd", vssd_c)
                    for e8 in range(8):
                        bank = PS[2] if e8 < 4 else PS[3]
                        co = (e8 % 4) * 128
                        bi = TE.transpose(bank[:, co:co + 128],
                                          s_y[:, e8 * 128:(e8 + 1) * 128],
                                          k_id[:, :]) if TE else None
                        inc(bi, "typ")
                    cap_typ = cnt["typ"]
                    wt(VE, "typ", cap_typ)
                    for e8 in range(8):
                        bank = PS[2] if e8 < 4 else PS[3]
                        co = (e8 % 4) * 128
                        bi = VE.tensor_copy(
                            s_yfm[:, e8 * GT + cloc * 128:e8 * GT + (cloc + 1) * 128],
                            bank[:, co:co + 128]) if VE else None
                        inc(bi, "vyf")

            wt(TE, "vyf", cnt["vyf"])
            wt(TE, "sp", cnt["sp"])
            q = gi // 2
            th = gi % 2
            for half in range(2):
                if gi > 0 or half > 0:
                    wt(TE, "vpo", caps.get(("vpo_half",), 0))
                for e in range(8):
                    si, soff = strip_take()
                    wt(TE, WLD[si % NSLOT], 16 * (si // NSLOT + 1))
                    last = None
                    for m in range(8):
                        mg = half * 8 + m
                        last = TE.matmul(PS[m][:, :],
                                         r_w[:, soff + mg * 128:soff + (mg + 1) * 128],
                                         s_yfm[:, e * GT:(e + 1) * GT],
                                         start=(e == 0), stop=(e == 7)) if TE else None
                    inc(last, "sfree")
                    rec_ev(f"sd:{si}", "sfree")
                cap_op = cnt["sfree"]
                wt(VE, "sfree", cap_op)
                wt(VE, "post", caps.get(("post_half",), 0))
                for m in range(8):
                    bi = VE.tensor_copy(s_po[:, m * GT:(m + 1) * GT],
                                        PS[m][:, :]) if VE else None
                    inc(bi, "vpo")
                caps[("vpo_half",)] = cnt["vpo"]
                wt(GP, "vpo", cnt["vpo"])
                for m in range(8):
                    mg = half * 8 + m
                    bi = GP.dma_start(
                        out=part_out[q, mg * 128:(mg + 1) * 128, th * GT:(th + 1) * GT],
                        in_=s_po[:, m * GT:(m + 1) * GT]) if GP else None
                    inc(bi, "post", 16)
                caps[("post_half",)] = cnt["post"]
                if gi == 0 and half == 0:
                    do_tap("po", s_po[:, :], "vpo", caps[("vpo_half",)])
            if gi == 0:
                do_tap("yf", s_yfm[:, :], "vyf", cnt["vyf"])

            caps[("vpo", gi)] = cnt["vpo"]
            caps[("sp", gi)] = cnt["sp"]
            caps[("sfree", gi)] = cnt["sfree"]

        wt(GP, "post", cnt["post"])
        bi = GP.collective_compute(
            "ReduceScatter", ALU.add,
            replica_groups=[[0, 1, 2, 3], [4, 5, 6, 7]],
            ins=[part_out.ap().opt()],
            outs=[rs_out.ap().opt()]) if GP else None
        inc(bi, "cc")

        for Eng in (TE, VE, SC, GP, SY):
            wt(Eng, "cc", 1)

        for tg in range(2):
            if tg > 0:
                wt(SC, "sfree", caps[("c_sfree", 0)])
                wt(SC, "vx1", caps[("c_vx1", 0)])
                wt(SC, "vrt", caps[("c_vrt", 0)])
            base_cld = cnt["cld"]
            for k in range(NK):
                bi = SC.dma_start(out=s_uq[:, k * GT:(k + 1) * GT],
                                  in_=uq[k * 128:(k + 1) * 128,
                                         tg * GT:(tg + 1) * GT]) if SC else None
                inc(bi, "cld", 16)
                bi = SC.dma_start(out=s_g[:, k * GT:(k + 1) * GT],
                                  in_=rs_out[k * 128:(k + 1) * 128,
                                             tg * GT:(tg + 1) * GT]) if SC else None
                inc(bi, "cld", 16)

            if tg > 0:
                wt(TE, "vfo", caps[("c_vfo", 0)])
                wt(TE, "vg", cnt["vg"])
            for k in range(NK):
                if k == 0:
                    wt(TE, "cld", base_cld + 32 * NK)
                bi = TE.matmul(PS[0][0:1, :], k_ones[:, :],
                               s_uq[:, k * GT:(k + 1) * GT],
                               start=(k == 0), stop=(k == NK - 1)) if TE else None
                if k == NK - 1:
                    inc(bi, "tstat")
                if k == 0:
                    wt(SC, "cld", base_cld + 32 * NK)
                wt(SC, "tsq2", cnt["tsq2"] - 1)
                sbi = SC.activation(s_sq2[:, (k % 2) * GT:(k % 2 + 1) * GT],
                                    s_uq[:, k * GT:(k + 1) * GT], AF.Square) if SC else None
                inc(sbi, "sq2")
                wt(TE, "sq2", cnt["sq2"])
                bi = TE.matmul(PS[1][0:1, :], k_ones[:, :],
                               s_sq2[:, (k % 2) * GT:(k % 2 + 1) * GT],
                               start=(k == 0), stop=(k == NK - 1)) if TE else None
                inc(bi, "tsq2")
            wt(VE, "tstat", cnt["tstat"])
            wt(VE, "tsq2", cnt["tsq2"])
            if tg > 0:
                wt(VE, "outd", caps[("c_outd", 0)])
                wt(VE, "sfree", caps[("c_sfree", 0)])
            bi = VE.tensor_scalar_mul(R[0:1, 0 * GT:1 * GT], PS[0][0:1, :], 1.0 / D) if VE else None
            inc(bi, "vp")
            bi = VE.tensor_scalar_mul(R[0:1, 1 * GT:2 * GT], PS[1][0:1, :], 1.0 / D) if VE else None
            inc(bi, "vp")
            wt(VE, "vp", cnt["vp"])
            bi = VE.tensor_tensor(R[0:1, 2 * GT:3 * GT], R[0:1, 0 * GT:1 * GT], R[0:1, 0 * GT:1 * GT], ALU.mult) if VE else None
            inc(bi, "vp")
            wt(VE, "vp", cnt["vp"])
            bi = VE.tensor_tensor(R[0:1, 1 * GT:2 * GT], R[0:1, 1 * GT:2 * GT], R[0:1, 2 * GT:3 * GT], ALU.subtract) if VE else None
            inc(bi, "vp")
            wt(VE, "vp", cnt["vp"])
            bi = VE.tensor_scalar_add(R[0:1, 2 * GT:3 * GT], R[0:1, 1 * GT:2 * GT], EPS) if VE else None
            inc(bi, "vp")
            wt(SC, "vp", cnt["vp"])
            bi = SC.activation(R[0:1, 3 * GT:4 * GT], R[0:1, 2 * GT:3 * GT], AF.Ln) if SC else None
            inc(bi, "sp")
            wt(SC, "sp", cnt["sp"])
            bi = SC.activation(R[0:1, 3 * GT:4 * GT], R[0:1, 3 * GT:4 * GT], AF.Exp, scale=-0.5) if SC else None
            inc(bi, "sp")
            wt(VE, "sp", cnt["sp"])
            bi = VE.tensor_scalar_add(R[0:1, 4 * GT:5 * GT], R[0:1, 3 * GT:4 * GT], 1.0) if VE else None
            inc(bi, "vp")
            bi = VE.scalar_tensor_tensor(R[0:1, 5 * GT:6 * GT], R[0:1, 0 * GT:1 * GT], -1.0, R[0:1, 3 * GT:4 * GT],
                                         ALU.mult, ALU.mult) if VE else None
            inc(bi, "vp")
            cap_r = cnt["vp"]
            wt(TE, "vp", cap_r)
            bi = TE.matmul(PS[2][:, :], k_ones1[:, :], R[0:1, 4 * GT:5 * GT],
                           start=True, stop=True) if TE else None
            inc(bi, "tp")
            bi = TE.matmul(PS[3][:, :], k_ones1[:, :], R[0:1, 5 * GT:6 * GT],
                           start=True, stop=True) if TE else None
            inc(bi, "tp")
            cap_bc2 = cnt["tp"]
            wt(VE, "tp", cap_bc2)
            for k in range(NK):
                bi = VE.tensor_tensor(s_fo[:, k * GT:(k + 1) * GT],
                                      s_uq[:, k * GT:(k + 1) * GT],
                                      PS[2][:, :], ALU.mult) if VE else None
                inc(bi, "vp")
                wt(VE, "vp", cnt["vp"])
                bi = VE.tensor_tensor(s_fo[:, k * GT:(k + 1) * GT],
                                      s_fo[:, k * GT:(k + 1) * GT],
                                      PS[3][:, :], ALU.add) if VE else None
                inc(bi, "vp")
                wt(VE, "vp", cnt["vp"])
                if k == 0:
                    wt(VE, "cld", base_cld + 32 * NK)
                bi = VE.tensor_tensor(s_fo[:, k * GT:(k + 1) * GT],
                                      s_fo[:, k * GT:(k + 1) * GT],
                                      s_g[:, k * GT:(k + 1) * GT], ALU.add) if VE else None
                inc(bi, "vx1")
            cap_x1 = cnt["vx1"]
            caps[("c_vx1", tg)] = cap_x1
            if tg == 0:
                do_tap("x1", s_fo[:, :], "vx1", cap_x1)

            wt(TE, "vx1", cap_x1)
            for k in range(NK):
                bi = TE.matmul(PS[0][0:1, :], k_ones[:, :],
                               s_fo[:, k * GT:(k + 1) * GT],
                               start=(k == 0), stop=(k == NK - 1)) if TE else None
                if k == NK - 1:
                    inc(bi, "tstat")
                wt(SC, "vx1", cap_x1)
                wt(SC, "tsq2", cnt["tsq2"] - 1)
                sbi = SC.activation(s_sq2[:, (k % 2) * GT:(k % 2 + 1) * GT],
                                    s_fo[:, k * GT:(k + 1) * GT], AF.Square) if SC else None
                inc(sbi, "sq2")
                wt(TE, "sq2", cnt["sq2"])
                bi = TE.matmul(PS[1][0:1, :], k_ones[:, :],
                               s_sq2[:, (k % 2) * GT:(k % 2 + 1) * GT],
                               start=(k == 0), stop=(k == NK - 1)) if TE else None
                inc(bi, "tsq2")
            wt(VE, "tstat", cnt["tstat"])
            wt(VE, "tsq2", cnt["tsq2"])
            bi = VE.tensor_scalar_mul(R[0:1, 0 * GT:1 * GT], PS[0][0:1, :], 1.0 / D) if VE else None
            inc(bi, "vp")
            bi = VE.tensor_scalar_mul(R[0:1, 1 * GT:2 * GT], PS[1][0:1, :], 1.0 / D) if VE else None
            inc(bi, "vp")
            wt(VE, "vp", cnt["vp"])
            bi = VE.tensor_tensor(R[0:1, 2 * GT:3 * GT], R[0:1, 0 * GT:1 * GT], R[0:1, 0 * GT:1 * GT], ALU.mult) if VE else None
            inc(bi, "vp")
            wt(VE, "vp", cnt["vp"])
            bi = VE.tensor_tensor(R[0:1, 1 * GT:2 * GT], R[0:1, 1 * GT:2 * GT], R[0:1, 2 * GT:3 * GT], ALU.subtract) if VE else None
            inc(bi, "vp")
            wt(VE, "vp", cnt["vp"])
            bi = VE.tensor_scalar_add(R[0:1, 2 * GT:3 * GT], R[0:1, 1 * GT:2 * GT], EPS) if VE else None
            inc(bi, "vp")
            wt(SC, "vp", cnt["vp"])
            bi = SC.activation(R[0:1, 3 * GT:4 * GT], R[0:1, 2 * GT:3 * GT], AF.Ln) if SC else None
            inc(bi, "sp")
            wt(SC, "sp", cnt["sp"])
            bi = SC.activation(R[0:1, 3 * GT:4 * GT], R[0:1, 3 * GT:4 * GT], AF.Exp, scale=-0.5) if SC else None
            inc(bi, "sp")
            wt(VE, "sp", cnt["sp"])
            bi = VE.scalar_tensor_tensor(R[0:1, 5 * GT:6 * GT], R[0:1, 0 * GT:1 * GT], -1.0, R[0:1, 3 * GT:4 * GT],
                                         ALU.mult, ALU.mult) if VE else None
            inc(bi, "vp")
            cap_r2 = cnt["vp"]
            wt(TE, "vp", cap_r2)
            bi = TE.matmul(PS[2][:, :], k_ones1[:, :], R[0:1, 3 * GT:4 * GT],
                           start=True, stop=True) if TE else None
            inc(bi, "tp")
            bi = TE.matmul(PS[3][:, :], k_ones1[:, :], R[0:1, 5 * GT:6 * GT],
                           start=True, stop=True) if TE else None
            inc(bi, "tp")
            cap_bc3 = cnt["tp"]
            wt(VE, "tp", cap_bc3)
            for k in range(NK):
                bi = VE.tensor_tensor(s_ln[:, k * GT:(k + 1) * GT],
                                      s_fo[:, k * GT:(k + 1) * GT],
                                      PS[2][:, :], ALU.mult) if VE else None
                inc(bi, "vp")
                wt(VE, "vp", cnt["vp"])
                bi = VE.tensor_tensor(s_ln[:, k * GT:(k + 1) * GT],
                                      s_ln[:, k * GT:(k + 1) * GT],
                                      PS[3][:, :], ALU.add) if VE else None
                inc(bi, "vrt")
            cap_lnd = cnt["vrt"]
            if tg == 0:
                do_tap("ln", s_ln[:, :], "vrt", cap_lnd)

            wt(TE, "vrt", cap_lnd)
            for k in range(NK):
                bi = TE.matmul(PS[4][0:8, :], c_wrt[:, k * 8:(k + 1) * 8],
                               s_ln[:, k * GT:(k + 1) * GT],
                               start=(k == 0), stop=(k == NK - 1)) if TE else None
                if k == NK - 1:
                    inc(bi, "trt")
            wt(VE, "trt", cnt["trt"])
            bi = VE.tensor_copy(s_lg[:, :], PS[4][0:8, :]) if VE else None
            inc(bi, "vrt")
            cap_lgev = cnt["vrt"]
            wt(TE, "vrt", cap_lgev)
            for tc in range(4):
                bi = TE.transpose(PS[5][:, tc * 8:(tc + 1) * 8],
                                  s_lg[:, tc * 128:(tc + 1) * 128],
                                  k_id[0:8, 0:8]) if TE else None
                inc(bi, "trt")
            wt(VE, "trt", cnt["trt"])
            bi = VE.tensor_copy(s_lgt[:, :], PS[5][:, 0:32]) if VE else None
            inc(bi, "vrt")
            for tc in range(4):
                blk = s_lgt[:, tc * 8:(tc + 1) * 8]
                wt(VE, "vrt", cnt["vrt"])
                bi = VE.reduce_max(s_m1[:, :], blk, axis=AX.X) if VE else None
                inc(bi, "vrt")
                wt(VE, "vrt", cnt["vrt"])
                bi = VE.tensor_scalar(s_lg2[:, tc * 8:(tc + 1) * 8], blk,
                                      s_m1[:, :], None, ALU.is_ge) if VE else None
                inc(bi, "vrt")
                wt(VE, "vrt", cnt["vrt"])
                bi = VE.scalar_tensor_tensor(s_wtk[:, tc * 8:(tc + 1) * 8],
                                             s_lg2[:, tc * 8:(tc + 1) * 8],
                                             -1e30, blk, ALU.mult, ALU.add) if VE else None
                inc(bi, "vrt")
                wt(VE, "vrt", cnt["vrt"])
                bi = VE.reduce_max(s_m2[:, :], s_wtk[:, tc * 8:(tc + 1) * 8],
                                   axis=AX.X) if VE else None
                inc(bi, "vrt")
                wt(VE, "vrt", cnt["vrt"])
                bi = VE.tensor_tensor(s_dd[:, :], s_m2[:, :], s_m1[:, :],
                                      ALU.subtract) if VE else None
                inc(bi, "vrt")
                wt(SC, "vrt", cnt["vrt"])
                bi = SC.activation(s_ee[:, :], s_dd[:, :], AF.Exp) if SC else None
                inc(bi, "sp")
                wt(VE, "sp", cnt["sp"])
                bi = VE.tensor_scalar_add(s_p1[:, :], s_ee[:, :], 1.0) if VE else None
                inc(bi, "vrt")
                wt(VE, "vrt", cnt["vrt"])
                bi = VE.reciprocal(s_p1[:, :], s_p1[:, :]) if VE else None
                inc(bi, "vrt")
                wt(VE, "vrt", cnt["vrt"])
                bi = VE.tensor_tensor(s_p2[:, :], s_p1[:, :], s_ee[:, :],
                                      ALU.mult) if VE else None
                inc(bi, "vrt")
                wt(VE, "vrt", cnt["vrt"])
                bi = VE.tensor_scalar(s_lg2[:, tc * 8:(tc + 1) * 8],
                                      s_wtk[:, tc * 8:(tc + 1) * 8],
                                      s_m2[:, :], s_p2[:, :], ALU.is_ge,
                                      ALU.mult) if VE else None
                inc(bi, "vrt")
                wt(VE, "vrt", cnt["vrt"])
                bi = VE.tensor_scalar(s_wtk[:, tc * 8:(tc + 1) * 8], blk,
                                      s_m1[:, :], s_p1[:, :], ALU.is_ge,
                                      ALU.mult) if VE else None
                inc(bi, "vrt")
                wt(VE, "vrt", cnt["vrt"])
                bi = VE.tensor_tensor(s_wtk[:, tc * 8:(tc + 1) * 8],
                                      s_wtk[:, tc * 8:(tc + 1) * 8],
                                      s_lg2[:, tc * 8:(tc + 1) * 8], ALU.add) if VE else None
                inc(bi, "vrt")
            cap_wtk = cnt["vrt"]
            if tg == 0:
                do_tap("wt", s_wtk[:, :], "vrt", cap_wtk)
                do_tap("lg", s_lgt[:, :], "vrt", cap_wtk)

            wt(TE, "vrt", cap_wtk)
            for tc in range(4):
                bi = TE.transpose(PS[5][0:8, tc * 128:(tc + 1) * 128],
                                  s_wtk[:, tc * 8:(tc + 1) * 8],
                                  k_id[:, :]) if TE else None
                inc(bi, "trt")
            wt(VE, "trt", cnt["trt"])
            for tc in range(4):
                bi = VE.tensor_copy(s_wfm8[:, tc * 128:(tc + 1) * 128],
                                    PS[5][0:8, tc * 128:(tc + 1) * 128]) if VE else None
                inc(bi, "vrt")
            cap_wfm8 = cnt["vrt"]
            caps[("c_vrt", tg)] = cnt["vrt"]
            for e in range(NE):
                wt(GP, "vrt", cap_wfm8)
                wt(GP, "trt", cnt["trt"])
                bi = GP.dma_start(out=s_wrow[0:1, :], in_=s_wfm8[e:e + 1, :]) if GP else None
                inc(bi, "grt", 16)
                wt(TE, "grt", cnt["grt"])
                if e == 0:
                    wt(TE, "vrt", cap_wfm8)
                bi = TE.matmul(PS[6][:, :], k_ones1[:, :], s_wrow[0:1, :],
                               start=True, stop=True) if TE else None
                inc(bi, "trt")
                wt(VE, "trt", cnt["trt"])
                bi = VE.tensor_copy(s_wbc[:, e * GT:(e + 1) * GT], PS[6][:, :]) if VE else None
                inc(bi, "vg")
            cap_wbc = cnt["vg"]

            for e in range(NE):
                for mh in range(2):
                    for k in range(NK):
                        si, soff = strip_take()
                        wt(TE, WLD[si % NSLOT], 16 * (si // NSLOT + 1))
                        if k == 0:
                            wt(TE, "vg", cnt["vg"])
                            wt(TE, "vfo", cnt["vfo"])
                        last = None
                        for m in range(8):
                            mg = mh * 8 + m
                            last = TE.matmul(
                                PS[m][:, :],
                                r_w[:, soff + mg * 128:soff + (mg + 1) * 128],
                                s_ln[:, k * GT:(k + 1) * GT],
                                start=(k == 0), stop=(k == NK - 1)) if TE else None
                        inc(last, "sfree")
                        rec_ev(f"sd:{si}", "sfree")
                    cap_fc = cnt["sfree"]
                    wt(VE, "sfree", cap_fc)
                    for m in range(8):
                        mg = mh * 8 + m
                        bi = VE.tensor_copy(s_sq2[:, 0:GT], PS[m][:, :]) if VE else None
                        inc(bi, "vp")
                        wt(VE, "vp", cnt["vp"])
                        bi = VE.scalar_tensor_tensor(
                            s_g[:, mg * GT:(mg + 1) * GT], PS[m][:, :], 0.0,
                            s_sq2[:, 0:GT], ALU.max, ALU.mult) if VE else None
                        inc(bi, "vp")
                        wt(VE, "vp", cnt["vp"])
                        bi = VE.tensor_tensor(s_g[:, mg * GT:(mg + 1) * GT],
                                              s_g[:, mg * GT:(mg + 1) * GT],
                                              s_wbc[:, e * GT:(e + 1) * GT],
                                              ALU.mult) if VE else None
                        inc(bi, "vg")
                cap_g = cnt["vg"]
                for mh in range(2):
                    for k2 in range(NK):
                        si, soff = strip_take()
                        wt(TE, WLD[si % NSLOT], 16 * (si // NSLOT + 1))
                        if k2 == 0:
                            wt(TE, "vg", cap_g)
                            wt(TE, "vfo", cnt["vfo"])
                        last = None
                        for m in range(8):
                            mg = mh * 8 + m
                            last = TE.matmul(
                                PS[m][:, :],
                                r_w[:, soff + mg * 128:soff + (mg + 1) * 128],
                                s_g[:, k2 * GT:(k2 + 1) * GT],
                                start=(k2 == 0), stop=(k2 == NK - 1)) if TE else None
                        inc(last, "sfree")
                        rec_ev(f"sd:{si}", "sfree")
                    cap_pj = cnt["sfree"]
                    wt(VE, "sfree", cap_pj)
                    for m in range(8):
                        mg = mh * 8 + m
                        bi = VE.tensor_tensor(s_fo[:, mg * GT:(mg + 1) * GT],
                                              s_fo[:, mg * GT:(mg + 1) * GT],
                                              PS[m][:, :], ALU.add) if VE else None
                        inc(bi, "vfo")
            caps[("c_vfo", tg)] = cnt["vfo"]
            caps[("c_sfree", tg)] = cnt["sfree"]
            wt(GP, "vfo", cnt["vfo"])
            for m in range(NK):
                bi = GP.dma_start(out=out[m * 128:(m + 1) * 128, tg * GT:(tg + 1) * GT],
                                  in_=s_fo[:, m * GT:(m + 1) * GT]) if GP else None
                inc(bi, "outd", 16)
            caps[("c_outd", tg)] = cnt["outd"]

        wt(GP, "outd", cnt["outd"])
        return cnt

    emit("pre", None)
    state["rec"] = False

    with nc.Block() as block:
        block.sync(lambda e: emit("sync", e))
        block.tensor(lambda e: emit("tensor", e))
        block.vector(lambda e: emit("vector", e))
        block.scalar(lambda e: emit("scalar", e))
        block.gpsimd(lambda e: emit("gpsimd", e))

    return nc


def prep_core_inputs(ins, c):
    g, j = c // 4, c % 4
    u = np.asarray(ins["L0_attn_u"][g])
    W = np.asarray(ins["L0_attn_in_proj_w"])
    Wout = np.asarray(ins["L0_attn_out_proj_w"])
    zr = W[j * 1024:(j + 1) * 1024]
    xr = W[4096 + j * 1024:4096 + (j + 1) * 1024]
    fm = np.concatenate([W[8192:8320], W[8320:8448],
                         W[8448 + 16 * j:8448 + 16 * (j + 1)]], 0)
    fmT = np.zeros((2048, 512), np.float32)
    fmT[:, 0:272] = fm.T
    cb = np.zeros((128, 512), np.float32)
    cb[:, 0:16] = np.asarray(ins["L0_attn_ln_w"]).reshape(16, 128).T
    cb[:, 16:32] = np.asarray(ins["L0_attn_ln_b"]).reshape(16, 128).T
    cb[:, 32] = np.asarray(ins["L0_attn_bnorm_w"])
    cb[:, 33] = np.asarray(ins["L0_attn_cnorm_w"])
    cb[0:16, 34] = np.asarray(ins["L0_attn_dt_bias"][16 * j:16 * (j + 1)])
    cb[0:16, 35] = np.asarray(ins["L0_attn_A_log"][16 * j:16 * (j + 1)])
    cb[:, 128:256] = (np.asarray(ins["L0_ffn_w_router"]).reshape(16, 128, 8)
                      .transpose(1, 0, 2).reshape(128, 128))
    dv = np.asarray(ins["L0_attn_D"][16 * j:16 * (j + 1)])
    return {
        "uT": np.ascontiguousarray(u.T),
        "uq": np.ascontiguousarray(u[j * 1024:(j + 1) * 1024].T),
        "w_tmT": np.ascontiguousarray(np.concatenate([zr, xr], 0).T),
        "w_fmT": fmT,
        "w_outT": np.ascontiguousarray(Wout.T[j * 1024:(j + 1) * 1024]),
        "w_fc": np.ascontiguousarray(ins["L0_ffn_w_fc"]),
        "w_pj": np.ascontiguousarray(ins["L0_ffn_w_proj"]),
        "cblk": cb,
        "dfull": np.ascontiguousarray(
            np.broadcast_to(np.repeat(dv, 64)[None, :], (128, 1024))),
    }


def assemble(results):
    out = np.empty((2, 4096, 2048), dtype=np.float32)
    for c in range(8):
        g, j = c // 4, c % 4
        out[g, j * 1024:(j + 1) * 1024, :] = results[c]["out"].T
    return out


def kernel(**inputs):
    nc = build()
    in_maps = [prep_core_inputs(inputs, c) for c in range(8)]
    res = run_bass_kernel_spmd(nc, in_maps, core_ids=list(range(8)))
    return assemble(res.results)


# version 04
# speedup vs baseline: 1.0190x; 1.0190x over previous
import numpy as np
import concourse.bass as bass
import concourse.mybir as mybir
from concourse.bass_utils import run_bass_kernel_spmd

F32 = mybir.dt.float32
AF = mybir.ActivationFunctionType
ALU = mybir.AluOpType
AX = mybir.AxisListType

D = 2048
NK = 16
SEQ = 4096
NG = 8
GT = 512
CH = 128
HL = 16
P = 64
HP = HL * P
DS = 128
EPS = 1e-5
NSLOT = 3
NE = 8

SEMS = ["uld", "uld1", "wld", "wld1", "wld2", "fmld", "fmld1", "cst", "gcst", "vcst", "scst", "csq", "tsq",
        "tstat", "vp", "sp", "tp", "gp", "vun", "ufree", "tfm", "sfree",
        "fmfree", "vzx", "tyd", "vssd", "typ", "vyf", "vpo", "post", "cc",
        "cld", "vx1", "sq2", "tsq2", "trt", "vrt", "grt", "vg", "vfo", "outd"]


def build(taps=(), stop_after=None):
    nc = bass.Bass("TRN2", target_bir_lowering=False, num_devices=8)

    uT = nc.dram_tensor("uT", [D, SEQ], F32, kind="ExternalInput").ap()
    uq = nc.dram_tensor("uq", [D, 1024], F32, kind="ExternalInput").ap()
    w_tmT = nc.dram_tensor("w_tmT", [D, 2048], F32, kind="ExternalInput").ap()
    w_fmT = nc.dram_tensor("w_fmT", [D, 512], F32, kind="ExternalInput").ap()
    w_outT = nc.dram_tensor("w_outT", [HP, D], F32, kind="ExternalInput").ap()
    w_fc = nc.dram_tensor("w_fc", [NE, D, D], F32, kind="ExternalInput").ap()
    w_pj = nc.dram_tensor("w_pj", [NE, D, D], F32, kind="ExternalInput").ap()
    cblk = nc.dram_tensor("cblk", [128, 512], F32, kind="ExternalInput").ap()
    dfull = nc.dram_tensor("dfull", [128, HP], F32, kind="ExternalInput").ap()

    part_out = nc.dram_tensor("part_out", [4, D, 1024], F32)
    rs_out = nc.dram_tensor("rs_out", [D, 1024], F32)
    out = nc.dram_tensor("out", [D, 1024], F32, kind="ExternalOutput").ap()

    tap_shapes = {"un": [128, NK * GT], "zx": [128, 2 * 2048], "bn": [128, GT],
                  "cn": [128, GT], "da": [HL, GT], "y0": [128, HP],
                  "yf": [128, 8 * GT], "po": [128, 8 * GT], "x1": [128, NK * GT],
                  "ln": [128, NK * GT], "lg": [128, 32], "wt": [128, 32],
                  "sst": [128, HP], "u0": [128, NK * GT], "rows": [1, 8 * GT], "pbc": [128, 2 * GT], "all": [128, 51200], "lnw": [128, 2 * NK], "braw": [128, GT], "brow": [1, GT]}
    tap_t = {t: nc.dram_tensor("tap_" + t, tap_shapes[t], F32,
                               kind="ExternalOutput").ap() for t in taps}

    def sb(name, shape, off):
        assert off + int(np.prod(shape[1:])) * 4 <= 224 * 1024, (name, off)
        return nc.alloc_sbuf_tensor_at(name, shape, F32, offset=off)

    k_id = sb("k_id", [128, 128], 16512)
    k_tril = sb("k_tril", [128, 128], 17024)
    k_ones = sb("k_ones", [128, 1], 17536)
    k_ones1 = sb("k_ones1", [1, 128], 17568)
    c_blkt = sb("c_blk", [128, 512], 18080)
    c_blk = c_blkt[:, :]
    c_lnw = c_blkt[:, 0:16]
    c_lnb = c_blkt[:, 16:32]
    c_bnw = c_blkt[:, 32:33]
    c_cnw = c_blkt[:, 33:34]
    c_dtb = c_blkt[0:HL, 34:35]
    c_alog = c_blkt[0:HL, 35:36]
    c_wrt = c_blkt[:, 128:256]
    c_aneg = sb("c_aneg", [HL, 1], 20160)
    c_dfull = sb("c_dfull", [128, HP], 20224)
    S_state = sb("S_state", [128, HP], 24320)
    r_w = sb("r_w", [128, NSLOT * 2048], 28416)
    s_rows = sb("s_rows", [1, 8 * GT], 52992)

    A0 = 68 * 1024
    s_u = sb("s_u", [128, NK * GT], A0)
    s_un = sb("s_un", [128, NK * GT], A0 + 32 * 1024)
    s_zx = sb("s_zx", [128, 2 * 2048], A0 + 64 * 1024)
    s_B = sb("s_B", [128, GT], A0 + 80 * 1024)
    s_C = sb("s_C", [128, GT], A0 + 82 * 1024)
    s_sq = sb("s_sq", [128, 2 * GT], A0 + 84 * 1024)
    s_dA = sb("s_dA", [HL, GT], A0 + 88 * 1024)
    s_dts = sb("s_dts", [HL, GT], A0 + 90 * 1024)
    s_Acs = sb("s_Acs", [HL, CH], A0 + 92 * 1024)
    s_AcsT = sb("s_AcsT", [128, HL], A0 + 92 * 1024 + 512)
    s_d1 = sb("s_d1", [128, HL], A0 + 92 * 1024 + 576)
    s_d2 = sb("s_d2", [128, HL], A0 + 92 * 1024 + 640)
    s_f = sb("s_f", [128, HL], A0 + 92 * 1024 + 704)
    s_e1 = sb("s_e1", [128, HL], A0 + 92 * 1024 + 768)
    s_E2 = sb("s_E2", [128, HL], A0 + 92 * 1024 + 832)
    s_ds = sb("s_ds", [128, HL], A0 + 92 * 1024 + 896)
    s_Dvb = sb("s_Dvb", [128, HL], A0 + 92 * 1024 + 960)
    s_mr = sb("s_mr", [1, 32], A0 + 93 * 1024)
    s_mc = sb("s_mc", [HL, 1], A0 + 93 * 1024 + 128)
    s_scm = sb("s_scm", [128, CH], A0 + 94 * 1024)
    s_Btm = sb("s_Btm", [128, DS], A0 + 94 * 1024 + 512)
    s_Mall = sb("s_Mall", [128, HL * CH], A0 + 95 * 1024)
    s_xd = sb("s_xd", [128, HP], A0 + 103 * 1024)
    s_y = sb("s_y", [128, HP], A0 + 107 * 1024)
    s_g1 = sb("s_g1", [128, HP], A0 + 111 * 1024)
    s_sz = sb("s_sz", [128, HP], A0 + 115 * 1024)
    s_yfm = sb("s_yfm", [128, 8 * GT], A0 + 119 * 1024)
    s_po = sb("s_po", [128, 8 * GT], A0 + 135 * 1024)
    s_wfm = sb("s_wfm", [128, 2 * 512], A0 + 151 * 1024)

    s_fo = sb("s_fo", [128, NK * GT], A0)
    s_ln = sb("s_ln", [128, NK * GT], A0 + 32 * 1024)
    s_uq = sb("s_uq", [128, NK * GT], A0 + 64 * 1024)
    s_g = sb("s_g", [128, NK * GT], A0 + 96 * 1024)
    s_sq2 = sb("s_sq2", [128, 2 * GT], A0 + 128 * 1024)
    s_lg = sb("s_lg", [8, GT], A0 + 132 * 1024)
    s_wrow = sb("s_wrow", [1, GT], A0 + 134 * 1024)
    s_lgt = sb("s_lgt", [128, 32], A0 + 136 * 1024)
    s_wtk = sb("s_wtk", [128, 32], A0 + 136 * 1024 + 128)
    s_lg2 = sb("s_lg2", [128, 32], A0 + 136 * 1024 + 256)
    s_m1 = sb("s_m1", [128, 1], A0 + 136 * 1024 + 384)
    s_m2 = sb("s_m2", [128, 1], A0 + 136 * 1024 + 416)
    s_p1 = sb("s_p1", [128, 1], A0 + 136 * 1024 + 448)
    s_p2 = sb("s_p2", [128, 1], A0 + 136 * 1024 + 480)
    s_dd = sb("s_dd", [128, 1], A0 + 136 * 1024 + 512)
    s_ee = sb("s_ee", [128, 1], A0 + 136 * 1024 + 544)
    s_wfm8 = sb("s_wfm8", [8, GT], A0 + 137 * 1024)
    s_wbc = sb("s_wbc", [128, NE * GT], A0 + 139 * 1024)

    PS = [nc.alloc_psum_tensor(f"ps{i}", [128, 512], F32) for i in range(8)]

    sems = {n: nc.alloc_semaphore("sem_" + n) for n in SEMS}
    tables = {}
    state = {"rec": True}

    strip_sched = []
    for gi in range(NG):
        for ps in range(2):
            for k in range(NK):
                strip_sched.append(("tm", k))
        for half in range(2):
            for e in range(8):
                strip_sched.append(("out", e))
    for tg in range(2):
        for e in range(NE):
            for mh in range(2):
                for k in range(NK):
                    strip_sched.append(("fc", e, k))
            for mh in range(2):
                for k in range(NK):
                    strip_sched.append(("pj", e, k))
    N_STRIPS = len(strip_sched)

    def strip_src(ent):
        if ent[0] == "tm":
            return w_tmT[ent[1] * 128:(ent[1] + 1) * 128, :]
        if ent[0] == "out":
            return w_outT[ent[1] * 128:(ent[1] + 1) * 128, :]
        if ent[0] == "fc":
            return w_fc[ent[1], ent[2] * 128:(ent[2] + 1) * 128, :]
        return w_pj[ent[1], ent[2] * 128:(ent[2] + 1) * 128, :]

    def emit(eng, E):
        cnt = {n: 0 for n in SEMS}
        rec = state["rec"]
        TE = E if eng == "tensor" else None
        VE = E if eng == "vector" else None
        SC = E if eng == "scalar" else None
        GP = E if eng == "gpsimd" else None
        SY = E if eng == "sync" else None
        R = s_rows

        def inc(bi, ch, n=1):
            cnt[ch] += n
            if bi is not None:
                bi.then_inc(sems[ch], n)
            return cnt[ch]

        def wt(Eng, ch, val):
            if Eng is not None and val is not None and val > 0:
                Eng.wait_ge(sems[ch], val)

        def rec_ev(key, ch):
            if rec:
                tables[key] = (ch, cnt[ch])

        def wt_ev(Eng, key):
            ch, val = tables.get(key, (None, None))
            if Eng is not None and ch is not None and val > 0:
                Eng.wait_ge(sems[ch], val)

        def do_tap(name, buf, ch=None, val=None):
            if name not in tap_t:
                return
            if ch is not None:
                wt(GP, ch, val)
            bi = GP.dma_start(out=tap_t[name], in_=buf) if GP else None
            inc(bi, "gp", 16)
            for Eng in (TE, VE, SC, SY):
                wt(Eng, "gp", cnt["gp"])

        ULD = ["uld", "uld1"]
        WLD = ["wld", "wld1", "wld2"]
        FMLD = ["fmld", "fmld1"]

        strips_issued = [0]
        sidx = [0]

        def strips_issue_upto(n):
            while strips_issued[0] < min(n, N_STRIPS):
                i = strips_issued[0]
                if i >= NSLOT:
                    wt_ev(SY, f"sd:{i - NSLOT}")
                src = strip_src(strip_sched[i])
                bi = SY.dma_start(
                    out=r_w[:, (i % NSLOT) * 2048:(i % NSLOT) * 2048 + src.shape[1]],
                    in_=src) if SY else None
                inc(bi, WLD[i % NSLOT], 16)
                strips_issued[0] += 1

        def strip_take():
            i = sidx[0]
            sidx[0] += 1
            strips_issue_upto(i + NSLOT + 1)
            return i, (i % NSLOT) * 2048

        bi = SC.dma_start(out=c_blk, in_=cblk) if SC else None
        inc(bi, "cst", 16)
        bi = SC.dma_start(out=c_dfull[:, :], in_=dfull) if SC else None
        ncst = inc(bi, "cst", 16)

        bi = GP.memset(k_id[:, :], 1.0) if GP else None
        inc(bi, "gp")
        wt(GP, "gp", cnt["gp"])
        bi = GP.affine_select(out=k_id[:, :], in_=k_id[:, :],
                              compare_op=ALU.is_equal, fill=0.0, base=0,
                              pattern=[[-1, 128]], channel_multiplier=1) if GP else None
        inc(bi, "gp")
        bi = GP.memset(k_tril[:, :], 1.0) if GP else None
        inc(bi, "gp")
        wt(GP, "gp", cnt["gp"])
        bi = GP.affine_select(out=k_tril[:, :], in_=k_tril[:, :],
                              compare_op=ALU.is_ge, fill=0.0, base=0,
                              pattern=[[1, 128]], channel_multiplier=-1) if GP else None
        inc(bi, "gp")
        bi = GP.memset(k_ones[:, :], 1.0) if GP else None
        inc(bi, "gp")
        bi = GP.memset(k_ones1[:, :], 1.0) if GP else None
        inc(bi, "gp")
        bi = GP.memset(S_state[:, :], 0.0) if GP else None
        inc(bi, "gcst")

        wt(SC, "cst", ncst)
        bi = SC.activation(c_aneg[:, :], c_alog, AF.Exp) if SC else None
        inc(bi, "scst")
        wt(VE, "scst", 1)
        bi = VE.tensor_scalar_mul(c_aneg[:, :], c_aneg[:, :], -1.0) if VE else None
        inc(bi, "vcst")

        wt(TE, "gcst", 1)
        wt(TE, "cst", ncst)
        wt(VE, "gcst", 1)
        wt(VE, "cst", ncst)
        wt(SC, "gcst", 1)

        for k in range(NK):
            bi = SC.dma_start(out=s_u[:, k * GT:(k + 1) * GT],
                              in_=uT[k * 128:(k + 1) * 128, 0:GT]) if SC else None
            inc(bi, "uld", 16)
        for kk in range(2):
            bi = SC.dma_start(out=s_wfm[:, (kk % 2) * 512:(kk % 2) * 512 + 512],
                              in_=w_fmT[kk * 128:(kk + 1) * 128, :]) if SC else None
            inc(bi, FMLD[kk % 2], 16)
        strips_issue_upto(NSLOT)

        caps = {}

        for gi in range(NG):
            if gi > 0:
                wt(TE, "vpo", caps[("vpo", gi - 1)])
                wt(TE, "sp", caps[("sp", gi - 1)])

            for k in range(NK):
                if k == 0:
                    wt(TE, ULD[gi % 2], 256 * (gi // 2 + 1))
                bi = TE.matmul(PS[0][0:1, :], k_ones[:, :],
                               s_u[:, k * GT:(k + 1) * GT],
                               start=(k == 0), stop=(k == NK - 1)) if TE else None
                if k == NK - 1:
                    inc(bi, "tstat")
                if k == 0:
                    wt(SC, ULD[gi % 2], 256 * (gi // 2 + 1))
                wt(SC, "tsq", 16 * gi + k - 1)
                sbi = SC.activation(s_sq[:, (k % 2) * GT:(k % 2 + 1) * GT],
                                    s_u[:, k * GT:(k + 1) * GT],
                                    AF.Square) if SC else None
                inc(sbi, "csq")
                wt(TE, "csq", 16 * gi + k + 1)
                bi = TE.matmul(PS[1][0:1, :], k_ones[:, :],
                               s_sq[:, (k % 2) * GT:(k % 2 + 1) * GT],
                               start=(k == 0), stop=(k == NK - 1)) if TE else None
                inc(bi, "tsq")

            if gi > 0:
                wt(VE, "sfree", caps[("sfree", gi - 1)])
            wt(VE, "tstat", cnt["tstat"])
            wt(VE, "tsq", cnt["tsq"])
            bi = VE.tensor_scalar_mul(R[0:1, 0 * GT:1 * GT], PS[0][0:1, :], 1.0 / D) if VE else None
            inc(bi, "vp")
            bi = VE.tensor_scalar_mul(R[0:1, 1 * GT:2 * GT], PS[1][0:1, :], 1.0 / D) if VE else None
            inc(bi, "vp")
            wt(VE, "vp", cnt["vp"])
            bi = VE.tensor_tensor(R[0:1, 2 * GT:3 * GT], R[0:1, 0 * GT:1 * GT], R[0:1, 0 * GT:1 * GT], ALU.mult) if VE else None
            inc(bi, "vp")
            wt(VE, "vp", cnt["vp"])
            bi = VE.tensor_tensor(R[0:1, 1 * GT:2 * GT], R[0:1, 1 * GT:2 * GT], R[0:1, 2 * GT:3 * GT], ALU.subtract) if VE else None
            inc(bi, "vp")
            wt(VE, "vp", cnt["vp"])
            bi = VE.tensor_scalar_add(R[0:1, 2 * GT:3 * GT], R[0:1, 1 * GT:2 * GT], EPS) if VE else None
            inc(bi, "vp")
            wt(SC, "vp", cnt["vp"])
            bi = SC.activation(R[0:1, 3 * GT:4 * GT], R[0:1, 2 * GT:3 * GT], AF.Ln) if SC else None
            inc(bi, "sp")
            wt(SC, "sp", cnt["sp"])
            bi = SC.activation(R[0:1, 3 * GT:4 * GT], R[0:1, 3 * GT:4 * GT], AF.Exp, scale=-0.5) if SC else None
            inc(bi, "sp")
            wt(VE, "sp", cnt["sp"])
            bi = VE.reciprocal(R[0:1, 4 * GT:5 * GT], R[0:1, 2 * GT:3 * GT]) if VE else None
            inc(bi, "vp")
            wt(VE, "vp", cnt["vp"])
            bi = VE.tensor_tensor(R[0:1, 4 * GT:5 * GT], R[0:1, 4 * GT:5 * GT], R[0:1, 1 * GT:2 * GT], ALU.mult) if VE else None
            inc(bi, "vp")
            wt(VE, "vp", cnt["vp"])
            bi = VE.tensor_scalar_add(R[0:1, 4 * GT:5 * GT], R[0:1, 4 * GT:5 * GT], EPS) if VE else None
            inc(bi, "vp")
            wt(SC, "vp", cnt["vp"])
            bi = SC.activation(R[0:1, 4 * GT:5 * GT], R[0:1, 4 * GT:5 * GT], AF.Ln) if SC else None
            inc(bi, "sp")
            wt(SC, "sp", cnt["sp"])
            bi = SC.activation(R[0:1, 4 * GT:5 * GT], R[0:1, 4 * GT:5 * GT], AF.Exp, scale=-0.5) if SC else None
            inc(bi, "sp")
            wt(VE, "sp", cnt["sp"])
            bi = VE.tensor_tensor(R[0:1, 5 * GT:6 * GT], R[0:1, 3 * GT:4 * GT], R[0:1, 4 * GT:5 * GT], ALU.mult) if VE else None
            inc(bi, "vp")
            wt(VE, "vp", cnt["vp"])
            bi = VE.scalar_tensor_tensor(R[0:1, 6 * GT:7 * GT], R[0:1, 0 * GT:1 * GT], -1.0, R[0:1, 5 * GT:6 * GT],
                                         ALU.mult, ALU.mult) if VE else None
            inc(bi, "vp")

            cap_rows = cnt["vp"]
            if gi == 0:
                do_tap("u0", s_u[:, :], "vp", cap_rows)
                do_tap("rows", s_rows[:, :], "vp", cap_rows)

            wt(TE, "vp", cap_rows)
            bi = TE.matmul(PS[2][:, :], k_ones1[:, :], R[0:1, 5 * GT:6 * GT],
                           start=True, stop=True) if TE else None
            inc(bi, "tp")
            bi = TE.matmul(PS[3][:, :], k_ones1[:, :], R[0:1, 6 * GT:7 * GT],
                           start=True, stop=True) if TE else None
            inc(bi, "tp")
            cap_bc = cnt["tp"]

            wt(VE, "tp", cap_bc)
            bi = VE.tensor_copy(s_sq[:, 0:GT], PS[2][:, :]) if VE else None
            inc(bi, "vp")
            bi = VE.tensor_copy(s_sq[:, GT:2 * GT], PS[3][:, :]) if VE else None
            inc(bi, "vp")
            wt(VE, "vp", cnt["vp"])
            for k in range(NK):
                bi = VE.tensor_tensor(s_un[:, k * GT:(k + 1) * GT],
                                      s_u[:, k * GT:(k + 1) * GT],
                                      s_sq[:, 0:GT], ALU.mult) if VE else None
                inc(bi, "vp")
                wt(VE, "vp", cnt["vp"])
                bi = VE.tensor_tensor(s_un[:, k * GT:(k + 1) * GT],
                                      s_un[:, k * GT:(k + 1) * GT],
                                      s_sq[:, GT:2 * GT], ALU.add) if VE else None
                inc(bi, "ufree")
                wt(VE, "ufree", cnt["ufree"])
                bi = VE.tensor_scalar(s_un[:, k * GT:(k + 1) * GT],
                                      s_un[:, k * GT:(k + 1) * GT],
                                      c_blkt[:, k:k + 1], c_blkt[:, 16 + k:17 + k],
                                      ALU.mult, ALU.add) if VE else None
                inc(bi, "vun")
            if gi == 0:
                if "pbc" in tap_t:
                    wt(VE, "vun", 16)
                    bi = VE.tensor_copy(s_sq[:, 0:GT], PS[2][:, :]) if VE else None
                    inc(bi, "vp")
                    bi = VE.tensor_copy(s_sq[:, GT:2 * GT], PS[3][:, :]) if VE else None
                    inc(bi, "vp")
                    do_tap("pbc", s_sq[:, :], "vp", cnt["vp"])
                do_tap("un", s_un[:, :], "vun", 16)
                if "lnw" in tap_t:
                    bi = VE.tensor_copy(s_sq[:, 0:NK], c_lnw) if VE else None
                    inc(bi, "vp")
                    bi = VE.tensor_copy(s_sq[:, NK:2 * NK], c_lnb) if VE else None
                    inc(bi, "vp")
                    do_tap("lnw", s_sq[:, 0:2 * NK], "vp", cnt["vp"])
                if "all" in tap_t:
                    s_all = nc.alloc_sbuf_tensor_at("s_all", [128, 51200], F32,
                                                    offset=17024)
                    do_tap("all", s_all[:, :], "vun", 16)
                if "zx" in tap_t:
                    wt(VE, "vun", 16)
                    bi = VE.tensor_copy(s_zx[:, 0:GT], s_un[:, 8 * GT:9 * GT]) if VE else None
                    inc(bi, "vp")
                    bi = VE.tensor_copy(s_zx[:, GT:2 * GT], s_un[:, 0:GT]) if VE else None
                    inc(bi, "vp")
                    do_tap("zx", s_zx[:, :], "vp", cnt["vp"])
            if stop_after == "a5" and gi == 0:
                return cnt
            if gi + 1 < NG:
                for k in range(NK):
                    wt(SC, "ufree", 16 * gi + k + 1)
                    bi = SC.dma_start(
                        out=s_u[:, k * GT:(k + 1) * GT],
                        in_=uT[k * 128:(k + 1) * 128,
                               (gi + 1) * GT:(gi + 2) * GT]) if SC else None
                    inc(bi, ULD[(gi + 1) % 2], 16)

            for k in range(NK):
                gidx = 16 * gi + k
                wt(TE, FMLD[gidx % 2], 16 * (gidx // 2 + 1))
                wt(TE, "vun", 16 * gi + k + 1)
                slot = gidx % 2
                wf = s_wfm[:, slot * 512:slot * 512 + 512]
                bi = TE.matmul(PS[6][:, :], wf[:, 0:128],
                               s_un[:, k * GT:(k + 1) * GT],
                               start=(k == 0), stop=(k == NK - 1)) if TE else None
                if k == NK - 1:
                    inc(bi, "tfm")
                bi2 = TE.matmul(PS[7][:, :], wf[:, 128:256],
                                s_un[:, k * GT:(k + 1) * GT],
                                start=(k == 0), stop=(k == NK - 1)) if TE else None
                if k == NK - 1:
                    inc(bi2, "tfm")
                bi3 = TE.matmul(PS[0][0:HL, :], wf[:, 256:272],
                                s_un[:, k * GT:(k + 1) * GT],
                                start=(k == 0), stop=(k == NK - 1)) if TE else None
                inc(bi3, "fmfree")
                nxt = gidx + 2
                if nxt < 16 * NG:
                    wt(SC, "fmfree", nxt - 1)
                    bi = SC.dma_start(
                        out=s_wfm[:, (nxt % 2) * 512:(nxt % 2) * 512 + 512],
                        in_=w_fmT[(nxt % 16) * 128:(nxt % 16) * 128 + 128, :]) if SC else None
                    inc(bi, FMLD[nxt % 2], 16)

            wt(VE, "tfm", cnt["tfm"])
            bi = VE.tensor_copy(s_B[:, :], PS[6][:, :]) if VE else None
            inc(bi, "vp")
            bi = VE.tensor_copy(s_C[:, :], PS[7][:, :]) if VE else None
            inc(bi, "vp")
            cap_bc_evac = cnt["vp"]
            if gi == 0:
                do_tap("braw", s_B[:, :], "vp", cap_bc_evac)
            for X, cx in ((s_B, c_bnw), (s_C, c_cnw)):
                wt(SC, "vp", cap_bc_evac)
                sqslot = 0 if X is s_B else GT
                bi = SC.activation(s_sq[:, sqslot:sqslot + GT], X[:, :],
                                   AF.Square) if SC else None
                inc(bi, "sp")
                if gi == 0 and X is s_B:
                    do_tap("brow", R[0:1, 2 * GT:3 * GT], "sp", cnt["sp"])
                wt(TE, "sp", cnt["sp"])
                if X is s_C:
                    wt(TE, "vp", cnt["vp"])
                bi = TE.matmul(PS[2][0:1, :], k_ones[:, :], s_sq[:, sqslot:sqslot + GT],
                               start=True, stop=True) if TE else None
                inc(bi, "tp")
                wt(VE, "tp", cnt["tp"])
                bi = VE.tensor_scalar(R[0:1, 2 * GT:3 * GT], PS[2][0:1, :], 1.0 / DS, EPS,
                                      ALU.mult, ALU.add) if VE else None
                inc(bi, "vp")
                wt(SC, "vp", cnt["vp"])
                bi = SC.activation(R[0:1, 2 * GT:3 * GT], R[0:1, 2 * GT:3 * GT], AF.Ln) if SC else None
                inc(bi, "sp")
                wt(SC, "sp", cnt["sp"])
                bi = SC.activation(R[0:1, 2 * GT:3 * GT], R[0:1, 2 * GT:3 * GT], AF.Exp, scale=-0.5) if SC else None
                inc(bi, "sp")
                wt(TE, "sp", cnt["sp"])
                bi = TE.matmul(PS[3][:, :], k_ones1[:, :], R[0:1, 2 * GT:3 * GT],
                               start=True, stop=True) if TE else None
                inc(bi, "tp")
                wt(VE, "tp", cnt["tp"])
                bi = VE.tensor_tensor(X[:, :], X[:, :], PS[3][:, :], ALU.mult) if VE else None
                inc(bi, "vp")
                wt(VE, "vp", cnt["vp"])
                bi = VE.tensor_scalar_mul(X[:, :], X[:, :], cx) if VE else None
                inc(bi, "vp")
            cap_bcn = cnt["vp"]
            if stop_after == "a7" and gi == 0:
                do_tap("bn", s_B[:, :], "vp", cap_bcn)
                do_tap("cn", s_C[:, :], "vp", cap_bcn)
                return cnt
            wt(SC, "tfm", cnt["tfm"])
            bi = SC.activation(s_dts[:, :], PS[0][0:HL, :], AF.Exp,
                               bias=c_dtb) if SC else None
            inc(bi, "sp")
            wt(VE, "sp", cnt["sp"])
            bi = VE.tensor_scalar_add(s_dts[:, :], s_dts[:, :], 1.0) if VE else None
            inc(bi, "vp")
            wt(SC, "vp", cnt["vp"])
            bi = SC.activation(s_dts[:, :], s_dts[:, :], AF.Ln) if SC else None
            inc(bi, "sp")
            wt(VE, "sp", cnt["sp"])
            wt(VE, "vcst", 1)
            bi = VE.tensor_scalar_mul(s_dA[:, :], s_dts[:, :], c_aneg[:, :]) if VE else None
            inc(bi, "vp")
            cap_da = cnt["vp"]
            cap_sp_a7 = cnt["sp"]
            if stop_after == "a7b" and gi == 0:
                do_tap("bn", s_B[:, :], "vp", cap_da)
                do_tap("cn", s_C[:, :], "vp", cap_da)
                do_tap("da", s_dA[:, :], "vp", cap_da)
                return cnt

            if gi == 0:
                do_tap("bn", s_B[:, :], "vp", cap_bcn)
                do_tap("cn", s_C[:, :], "vp", cap_bcn)
                do_tap("da", s_dA[:, :], "vp", cap_da)

            for ps in range(2):
                wt(TE, "sp", cap_sp_a7)
                wt(TE, "vp", cap_bcn)
                wt(TE, "vssd", caps.get(("vssd_last",), 0))
                wt(TE, "vyf", cnt["vyf"])
                for k in range(NK):
                    si, soff = strip_take()
                    wt(TE, WLD[si % NSLOT], 16 * (si // NSLOT + 1))
                    last = None
                    for tt2 in range(2):
                        tglob = ps * 2 + tt2
                        for ns in range(4):
                            last = TE.matmul(
                                PS[tt2 * 4 + ns][:, :],
                                s_un[:, k * GT + tglob * 128:k * GT + tglob * 128 + 128],
                                r_w[:, soff + ns * 512:soff + (ns + 1) * 512],
                                start=(k == 0), stop=(k == NK - 1)) if TE else None
                    inc(last, "sfree")
                    rec_ev(f"sd:{si}", "sfree")
                cap_zx_mm = cnt["sfree"]
                wt(VE, "sfree", cap_zx_mm)
                for tt2 in range(2):
                    for ns in range(4):
                        bi = VE.tensor_copy(
                            s_zx[:, tt2 * 2048 + ns * 512:tt2 * 2048 + (ns + 1) * 512],
                            PS[tt2 * 4 + ns][:, :]) if VE else None
                        inc(bi, "vzx")
                cap_vzx = cnt["vzx"]
                if gi == 0 and ps == 0:
                    do_tap("zx", s_zx[:, :], "vzx", cap_vzx)

                for tt2 in range(2):
                    cloc = ps * 2 + tt2
                    zoff = tt2 * 2048
                    wt(VE, "vp", cap_da)
                    bi = VE.tensor_tensor_scan(
                        s_Acs[:, :], k_ones[0:HL, 0:1].broadcast_to([HL, CH]),
                        s_dA[:, cloc * CH:(cloc + 1) * CH], 0.0,
                        ALU.mult, ALU.add) if VE else None
                    inc(bi, "vp")
                    wt(VE, "vp", cnt["vp"])
                    bi = VE.tensor_tensor(s_mc[:, :], s_Acs[:, 0:1],
                                          s_Acs[:, CH - 1:CH], ALU.add) if VE else None
                    inc(bi, "vp")
                    wt(VE, "vp", cnt["vp"])
                    bi = VE.tensor_scalar_mul(s_mc[:, :], s_mc[:, :], 0.5) if VE else None
                    inc(bi, "vp")
                    cap_scan = cnt["vp"]

                    wt(TE, "vzx", cap_vzx)
                    wt(TE, "vp", cap_scan)
                    bi = TE.matmul(PS[0][:, 0:128], s_B[:, cloc * 128:(cloc + 1) * 128],
                                   s_C[:, cloc * 128:(cloc + 1) * 128],
                                   start=True, stop=True) if TE else None
                    inc(bi, "tp")
                    bi = TE.transpose(PS[1][:, 0:128],
                                      s_B[:, cloc * 128:(cloc + 1) * 128],
                                      k_id[:, :]) if TE else None
                    inc(bi, "tp")
                    bi = TE.transpose(PS[0][:, 128:128 + HL], s_Acs[:, :],
                                      k_id[0:HL, 0:HL]) if TE else None
                    inc(bi, "tp")
                    bi = TE.transpose(PS[0][0:1, 176:176 + HL], s_mc[:, :],
                                      k_id[0:HL, 0:HL]) if TE else None
                    inc(bi, "tp")
                    bi = TE.transpose(PS[0][0:1, 192:192 + HL],
                                      s_Acs[:, CH - 1:CH],
                                      k_id[0:HL, 0:HL]) if TE else None
                    inc(bi, "tp")
                    cap_t1 = cnt["tp"]
                    wt(VE, "tp", cap_t1)
                    bi = VE.tensor_tensor(s_scm[:, :], PS[0][:, 0:128],
                                          k_tril[:, :], ALU.mult) if VE else None
                    inc(bi, "vp")
                    bi = VE.tensor_copy(s_Btm[:, :], PS[1][:, 0:128]) if VE else None
                    inc(bi, "vp")
                    bi = VE.tensor_copy(s_AcsT[:, :], PS[0][:, 128:128 + HL]) if VE else None
                    inc(bi, "vp")
                    bi = VE.tensor_copy(s_mr[0:1, 0:HL], PS[0][0:1, 176:176 + HL]) if VE else None
                    inc(bi, "vp")
                    bi = VE.tensor_copy(s_mr[0:1, 16:16 + HL], PS[0][0:1, 192:192 + HL]) if VE else None
                    inc(bi, "vp")
                    cap_ev1 = cnt["vp"]
                    wt(TE, "vp", cap_ev1)
                    bi = TE.matmul(PS[0][:, 144:144 + HL], k_ones1[:, :],
                                   s_mr[0:1, 0:HL], start=True, stop=True) if TE else None
                    inc(bi, "tp")
                    bi = TE.matmul(PS[0][:, 160:160 + HL], k_ones1[:, :],
                                   s_mr[0:1, 16:16 + HL], start=True, stop=True) if TE else None
                    inc(bi, "tp")
                    cap_t2 = cnt["tp"]
                    wt(VE, "tp", cap_t2)
                    bi = VE.tensor_tensor(s_d1[:, :], PS[0][:, 144:144 + HL],
                                          s_AcsT[:, :], ALU.subtract) if VE else None
                    inc(bi, "vp")
                    bi = VE.tensor_tensor(s_d2[:, :], PS[0][:, 160:160 + HL],
                                          s_AcsT[:, :], ALU.subtract) if VE else None
                    inc(bi, "vp")
                    cap_d = cnt["vp"]
                    wt(SC, "vp", cap_d)
                    bi = SC.activation(s_f[:, :], s_d1[:, :], AF.Exp) if SC else None
                    inc(bi, "sp")
                    bi = SC.activation(s_e1[:, :], s_d1[:, :], AF.Exp, scale=-1.0) if SC else None
                    inc(bi, "sp")
                    bi = SC.activation(s_E2[:, :], s_AcsT[:, :], AF.Exp) if SC else None
                    inc(bi, "sp")
                    bi = SC.activation(s_ds[:, :], s_d2[:, :], AF.Exp) if SC else None
                    inc(bi, "sp")
                    bi = SC.activation(s_Dvb[:, :], PS[0][:, 160:160 + HL], AF.Exp) if SC else None
                    inc(bi, "sp")
                    cap_exp = cnt["sp"]
                    wt(VE, "sp", cap_exp)
                    bi = VE.tensor_tensor(
                        s_Mall[:, :].rearrange("p (h l) -> p h l", h=HL),
                        s_scm[:, :].unsqueeze(1).broadcast_to([128, HL, CH]),
                        s_f[:, :].unsqueeze(2).broadcast_to([128, HL, CH]),
                        ALU.mult) if VE else None
                    inc(bi, "vp")
                    bi = VE.tensor_tensor(
                        s_xd[:, :].rearrange("p (h q) -> p h q", h=HL),
                        s_zx[:, zoff + 1024:zoff + 2048].rearrange("p (h q) -> p h q", h=HL),
                        s_ds[:, :].unsqueeze(2).broadcast_to([128, HL, P]),
                        ALU.mult) if VE else None
                    inc(bi, "vp")
                    cap_mall = cnt["vp"]

                    wt(TE, "vp", cap_mall)
                    for h in range(HL):
                        bank = PS[2] if h < 8 else PS[3]
                        co = (h % 8) * 64
                        bi = TE.matmul(bank[:, co:co + 64],
                                       s_Mall[:, h * CH:(h + 1) * CH],
                                       s_zx[:, zoff + 1024 + h * P:zoff + 1024 + (h + 1) * P],
                                       start=True, stop=True) if TE else None
                        inc(bi, "tp")
                    for half in range(2):
                        bi = TE.matmul(PS[4 + half][:, :],
                                       s_C[:, cloc * 128:(cloc + 1) * 128],
                                       S_state[:, half * 512:(half + 1) * 512],
                                       start=True, stop=True) if TE else None
                        inc(bi, "tp")
                    bi = TE.matmul(PS[6][:, :], s_Btm[:, :],
                                   s_xd[:, 0:512], start=True, stop=True) if TE else None
                    inc(bi, "tp")
                    bi = TE.matmul(PS[7][:, :], s_Btm[:, :],
                                   s_xd[:, 512:1024], start=True, stop=True) if TE else None
                    cap_yd = inc(bi, "tyd")

                    wt(VE, "tyd", cap_yd)
                    bi = VE.tensor_tensor(
                        S_state[:, :].rearrange("p (h q) -> p h q", h=HL),
                        S_state[:, :].rearrange("p (h q) -> p h q", h=HL),
                        s_Dvb[:, :].unsqueeze(2).broadcast_to([128, HL, P]),
                        ALU.mult) if VE else None
                    inc(bi, "vp")
                    wt(VE, "vp", cnt["vp"])
                    for half in range(2):
                        bi = VE.tensor_tensor(S_state[:, half * 512:(half + 1) * 512],
                                              S_state[:, half * 512:(half + 1) * 512],
                                              PS[6 + half][:, :], ALU.add) if VE else None
                        inc(bi, "vp")
                    wt(VE, "vssd", caps.get(("vssd_last",), 0))
                    wt(VE, "vp", cnt["vp"])
                    for half in range(2):
                        bi = VE.tensor_tensor(
                            s_y[:, half * 512:(half + 1) * 512].rearrange(
                                "p (h q) -> p h q", h=8),
                            PS[2 + half][:, :].rearrange("p (h q) -> p h q", h=8),
                            s_e1[:, half * 8:(half + 1) * 8].unsqueeze(2).broadcast_to(
                                [128, 8, P]),
                            ALU.mult) if VE else None
                        inc(bi, "vp")
                        bi = VE.tensor_tensor(
                            s_g1[:, half * 512:(half + 1) * 512].rearrange(
                                "p (h q) -> p h q", h=8),
                            PS[4 + half][:, :].rearrange("p (h q) -> p h q", h=8),
                            s_E2[:, half * 8:(half + 1) * 8].unsqueeze(2).broadcast_to(
                                [128, 8, P]),
                            ALU.mult) if VE else None
                        inc(bi, "vp")
                    wt(VE, "vp", cnt["vp"])
                    bi = VE.tensor_tensor(s_y[:, :], s_y[:, :], s_g1[:, :], ALU.add) if VE else None
                    inc(bi, "vp")
                    wt(VE, "vp", cnt["vp"])
                    bi = VE.tensor_tensor(s_g1[:, :], s_zx[:, zoff + 1024:zoff + 2048],
                                          c_dfull[:, :], ALU.mult) if VE else None
                    inc(bi, "vp")
                    wt(VE, "vp", cnt["vp"])
                    bi = VE.tensor_tensor(s_y[:, :], s_y[:, :], s_g1[:, :], ALU.add) if VE else None
                    inc(bi, "vp")
                    cap_ypre = cnt["vp"]
                    wt(SC, "vzx", cap_vzx)
                    bi = SC.activation(s_sz[:, :], s_zx[:, zoff:zoff + 1024],
                                       AF.Silu) if SC else None
                    inc(bi, "sp")
                    cap_silu = cnt["sp"]
                    wt(VE, "sp", cap_silu)
                    wt(VE, "vp", cap_ypre)
                    bi = VE.tensor_tensor(s_y[:, :], s_y[:, :], s_sz[:, :], ALU.mult) if VE else None
                    vssd_c = inc(bi, "vssd")
                    caps[("vssd_last",)] = vssd_c
                    if gi == 0 and cloc == 0:
                        do_tap("y0", s_y[:, :], "vssd", vssd_c)
                        do_tap("sst", S_state[:, :], "vssd", vssd_c)

                    wt(TE, "vssd", vssd_c)
                    for e8 in range(8):
                        bank = PS[2] if e8 < 4 else PS[3]
                        co = (e8 % 4) * 128
                        bi = TE.transpose(bank[:, co:co + 128],
                                          s_y[:, e8 * 128:(e8 + 1) * 128],
                                          k_id[:, :]) if TE else None
                        inc(bi, "typ")
                    cap_typ = cnt["typ"]
                    wt(VE, "typ", cap_typ)
                    for e8 in range(8):
                        bank = PS[2] if e8 < 4 else PS[3]
                        co = (e8 % 4) * 128
                        bi = VE.tensor_copy(
                            s_yfm[:, e8 * GT + cloc * 128:e8 * GT + (cloc + 1) * 128],
                            bank[:, co:co + 128]) if VE else None
                        inc(bi, "vyf")

            wt(TE, "vyf", cnt["vyf"])
            wt(TE, "sp", cnt["sp"])
            q = gi // 2
            th = gi % 2
            for half in range(2):
                if gi > 0 or half > 0:
                    wt(TE, "vpo", caps.get(("vpo_half",), 0))
                for e in range(8):
                    si, soff = strip_take()
                    wt(TE, WLD[si % NSLOT], 16 * (si // NSLOT + 1))
                    last = None
                    for m in range(8):
                        mg = half * 8 + m
                        last = TE.matmul(PS[m][:, :],
                                         r_w[:, soff + mg * 128:soff + (mg + 1) * 128],
                                         s_yfm[:, e * GT:(e + 1) * GT],
                                         start=(e == 0), stop=(e == 7)) if TE else None
                    inc(last, "sfree")
                    rec_ev(f"sd:{si}", "sfree")
                cap_op = cnt["sfree"]
                wt(VE, "sfree", cap_op)
                wt(VE, "post", caps.get(("post_half",), 0))
                for m in range(8):
                    bi = VE.tensor_copy(s_po[:, m * GT:(m + 1) * GT],
                                        PS[m][:, :]) if VE else None
                    inc(bi, "vpo")
                caps[("vpo_half",)] = cnt["vpo"]
                wt(GP, "vpo", cnt["vpo"])
                for m in range(8):
                    mg = half * 8 + m
                    bi = GP.dma_start(
                        out=part_out[q, mg * 128:(mg + 1) * 128, th * GT:(th + 1) * GT],
                        in_=s_po[:, m * GT:(m + 1) * GT]) if GP else None
                    inc(bi, "post", 16)
                caps[("post_half",)] = cnt["post"]
                if gi == 0 and half == 0:
                    do_tap("po", s_po[:, :], "vpo", caps[("vpo_half",)])
            if gi == 0:
                do_tap("yf", s_yfm[:, :], "vyf", cnt["vyf"])

            caps[("vpo", gi)] = cnt["vpo"]
            caps[("sp", gi)] = cnt["sp"]
            caps[("sfree", gi)] = cnt["sfree"]

        wt(GP, "post", cnt["post"])
        bi = GP.collective_compute(
            "ReduceScatter", ALU.add,
            replica_groups=[[0, 1, 2, 3], [4, 5, 6, 7]],
            ins=[part_out.ap().opt()],
            outs=[rs_out.ap().opt()]) if GP else None
        inc(bi, "cc")

        for Eng in (TE, VE, SC, GP, SY):
            wt(Eng, "cc", 1)

        for tg in range(2):
            if tg > 0:
                wt(SC, "sfree", caps[("c_sfree", 0)])
                wt(SC, "vx1", caps[("c_vx1", 0)])
                wt(SC, "vrt", caps[("c_vrt", 0)])
            base_cld = cnt["cld"]
            for k in range(NK):
                bi = SC.dma_start(out=s_uq[:, k * GT:(k + 1) * GT],
                                  in_=uq[k * 128:(k + 1) * 128,
                                         tg * GT:(tg + 1) * GT]) if SC else None
                inc(bi, "cld", 16)
                bi = SC.dma_start(out=s_g[:, k * GT:(k + 1) * GT],
                                  in_=rs_out[k * 128:(k + 1) * 128,
                                             tg * GT:(tg + 1) * GT]) if SC else None
                inc(bi, "cld", 16)

            if tg > 0:
                wt(TE, "vfo", caps[("c_vfo", 0)])
                wt(TE, "vg", cnt["vg"])
            for k in range(NK):
                if k == 0:
                    wt(TE, "cld", base_cld + 32 * NK)
                bi = TE.matmul(PS[0][0:1, :], k_ones[:, :],
                               s_uq[:, k * GT:(k + 1) * GT],
                               start=(k == 0), stop=(k == NK - 1)) if TE else None
                if k == NK - 1:
                    inc(bi, "tstat")
                if k == 0:
                    wt(SC, "cld", base_cld + 32 * NK)
                wt(SC, "tsq2", cnt["tsq2"] - 1)
                sbi = SC.activation(s_sq2[:, (k % 2) * GT:(k % 2 + 1) * GT],
                                    s_uq[:, k * GT:(k + 1) * GT], AF.Square) if SC else None
                inc(sbi, "sq2")
                wt(TE, "sq2", cnt["sq2"])
                bi = TE.matmul(PS[1][0:1, :], k_ones[:, :],
                               s_sq2[:, (k % 2) * GT:(k % 2 + 1) * GT],
                               start=(k == 0), stop=(k == NK - 1)) if TE else None
                inc(bi, "tsq2")
            wt(VE, "tstat", cnt["tstat"])
            wt(VE, "tsq2", cnt["tsq2"])
            if tg > 0:
                wt(VE, "outd", caps[("c_outd", 0)])
                wt(VE, "sfree", caps[("c_sfree", 0)])
            bi = VE.tensor_scalar_mul(R[0:1, 0 * GT:1 * GT], PS[0][0:1, :], 1.0 / D) if VE else None
            inc(bi, "vp")
            bi = VE.tensor_scalar_mul(R[0:1, 1 * GT:2 * GT], PS[1][0:1, :], 1.0 / D) if VE else None
            inc(bi, "vp")
            wt(VE, "vp", cnt["vp"])
            bi = VE.tensor_tensor(R[0:1, 2 * GT:3 * GT], R[0:1, 0 * GT:1 * GT], R[0:1, 0 * GT:1 * GT], ALU.mult) if VE else None
            inc(bi, "vp")
            wt(VE, "vp", cnt["vp"])
            bi = VE.tensor_tensor(R[0:1, 1 * GT:2 * GT], R[0:1, 1 * GT:2 * GT], R[0:1, 2 * GT:3 * GT], ALU.subtract) if VE else None
            inc(bi, "vp")
            wt(VE, "vp", cnt["vp"])
            bi = VE.tensor_scalar_add(R[0:1, 2 * GT:3 * GT], R[0:1, 1 * GT:2 * GT], EPS) if VE else None
            inc(bi, "vp")
            wt(SC, "vp", cnt["vp"])
            bi = SC.activation(R[0:1, 3 * GT:4 * GT], R[0:1, 2 * GT:3 * GT], AF.Ln) if SC else None
            inc(bi, "sp")
            wt(SC, "sp", cnt["sp"])
            bi = SC.activation(R[0:1, 3 * GT:4 * GT], R[0:1, 3 * GT:4 * GT], AF.Exp, scale=-0.5) if SC else None
            inc(bi, "sp")
            wt(VE, "sp", cnt["sp"])
            bi = VE.tensor_scalar_add(R[0:1, 4 * GT:5 * GT], R[0:1, 3 * GT:4 * GT], 1.0) if VE else None
            inc(bi, "vp")
            bi = VE.scalar_tensor_tensor(R[0:1, 5 * GT:6 * GT], R[0:1, 0 * GT:1 * GT], -1.0, R[0:1, 3 * GT:4 * GT],
                                         ALU.mult, ALU.mult) if VE else None
            inc(bi, "vp")
            cap_r = cnt["vp"]
            wt(TE, "vp", cap_r)
            bi = TE.matmul(PS[2][:, :], k_ones1[:, :], R[0:1, 4 * GT:5 * GT],
                           start=True, stop=True) if TE else None
            inc(bi, "tp")
            bi = TE.matmul(PS[3][:, :], k_ones1[:, :], R[0:1, 5 * GT:6 * GT],
                           start=True, stop=True) if TE else None
            inc(bi, "tp")
            cap_bc2 = cnt["tp"]
            wt(VE, "tp", cap_bc2)
            for k in range(NK):
                bi = VE.tensor_tensor(s_fo[:, k * GT:(k + 1) * GT],
                                      s_uq[:, k * GT:(k + 1) * GT],
                                      PS[2][:, :], ALU.mult) if VE else None
                inc(bi, "vp")
                wt(VE, "vp", cnt["vp"])
                bi = VE.tensor_tensor(s_fo[:, k * GT:(k + 1) * GT],
                                      s_fo[:, k * GT:(k + 1) * GT],
                                      PS[3][:, :], ALU.add) if VE else None
                inc(bi, "vp")
                wt(VE, "vp", cnt["vp"])
                if k == 0:
                    wt(VE, "cld", base_cld + 32 * NK)
                bi = VE.tensor_tensor(s_fo[:, k * GT:(k + 1) * GT],
                                      s_fo[:, k * GT:(k + 1) * GT],
                                      s_g[:, k * GT:(k + 1) * GT], ALU.add) if VE else None
                inc(bi, "vx1")
            cap_x1 = cnt["vx1"]
            caps[("c_vx1", tg)] = cap_x1
            if tg == 0:
                do_tap("x1", s_fo[:, :], "vx1", cap_x1)

            wt(TE, "vx1", cap_x1)
            for k in range(NK):
                bi = TE.matmul(PS[0][0:1, :], k_ones[:, :],
                               s_fo[:, k * GT:(k + 1) * GT],
                               start=(k == 0), stop=(k == NK - 1)) if TE else None
                if k == NK - 1:
                    inc(bi, "tstat")
                wt(SC, "vx1", cap_x1)
                wt(SC, "tsq2", cnt["tsq2"] - 1)
                sbi = SC.activation(s_sq2[:, (k % 2) * GT:(k % 2 + 1) * GT],
                                    s_fo[:, k * GT:(k + 1) * GT], AF.Square) if SC else None
                inc(sbi, "sq2")
                wt(TE, "sq2", cnt["sq2"])
                bi = TE.matmul(PS[1][0:1, :], k_ones[:, :],
                               s_sq2[:, (k % 2) * GT:(k % 2 + 1) * GT],
                               start=(k == 0), stop=(k == NK - 1)) if TE else None
                inc(bi, "tsq2")
            wt(VE, "tstat", cnt["tstat"])
            wt(VE, "tsq2", cnt["tsq2"])
            bi = VE.tensor_scalar_mul(R[0:1, 0 * GT:1 * GT], PS[0][0:1, :], 1.0 / D) if VE else None
            inc(bi, "vp")
            bi = VE.tensor_scalar_mul(R[0:1, 1 * GT:2 * GT], PS[1][0:1, :], 1.0 / D) if VE else None
            inc(bi, "vp")
            wt(VE, "vp", cnt["vp"])
            bi = VE.tensor_tensor(R[0:1, 2 * GT:3 * GT], R[0:1, 0 * GT:1 * GT], R[0:1, 0 * GT:1 * GT], ALU.mult) if VE else None
            inc(bi, "vp")
            wt(VE, "vp", cnt["vp"])
            bi = VE.tensor_tensor(R[0:1, 1 * GT:2 * GT], R[0:1, 1 * GT:2 * GT], R[0:1, 2 * GT:3 * GT], ALU.subtract) if VE else None
            inc(bi, "vp")
            wt(VE, "vp", cnt["vp"])
            bi = VE.tensor_scalar_add(R[0:1, 2 * GT:3 * GT], R[0:1, 1 * GT:2 * GT], EPS) if VE else None
            inc(bi, "vp")
            wt(SC, "vp", cnt["vp"])
            bi = SC.activation(R[0:1, 3 * GT:4 * GT], R[0:1, 2 * GT:3 * GT], AF.Ln) if SC else None
            inc(bi, "sp")
            wt(SC, "sp", cnt["sp"])
            bi = SC.activation(R[0:1, 3 * GT:4 * GT], R[0:1, 3 * GT:4 * GT], AF.Exp, scale=-0.5) if SC else None
            inc(bi, "sp")
            wt(VE, "sp", cnt["sp"])
            bi = VE.scalar_tensor_tensor(R[0:1, 5 * GT:6 * GT], R[0:1, 0 * GT:1 * GT], -1.0, R[0:1, 3 * GT:4 * GT],
                                         ALU.mult, ALU.mult) if VE else None
            inc(bi, "vp")
            cap_r2 = cnt["vp"]
            wt(TE, "vp", cap_r2)
            bi = TE.matmul(PS[2][:, :], k_ones1[:, :], R[0:1, 3 * GT:4 * GT],
                           start=True, stop=True) if TE else None
            inc(bi, "tp")
            bi = TE.matmul(PS[3][:, :], k_ones1[:, :], R[0:1, 5 * GT:6 * GT],
                           start=True, stop=True) if TE else None
            inc(bi, "tp")
            cap_bc3 = cnt["tp"]
            wt(VE, "tp", cap_bc3)
            for k in range(NK):
                bi = VE.tensor_tensor(s_ln[:, k * GT:(k + 1) * GT],
                                      s_fo[:, k * GT:(k + 1) * GT],
                                      PS[2][:, :], ALU.mult) if VE else None
                inc(bi, "vp")
                wt(VE, "vp", cnt["vp"])
                bi = VE.tensor_tensor(s_ln[:, k * GT:(k + 1) * GT],
                                      s_ln[:, k * GT:(k + 1) * GT],
                                      PS[3][:, :], ALU.add) if VE else None
                inc(bi, "vrt")
            cap_lnd = cnt["vrt"]
            if tg == 0:
                do_tap("ln", s_ln[:, :], "vrt", cap_lnd)

            wt(TE, "vrt", cap_lnd)
            for k in range(NK):
                bi = TE.matmul(PS[4][0:8, :], c_wrt[:, k * 8:(k + 1) * 8],
                               s_ln[:, k * GT:(k + 1) * GT],
                               start=(k == 0), stop=(k == NK - 1)) if TE else None
                if k == NK - 1:
                    inc(bi, "trt")
            wt(VE, "trt", cnt["trt"])
            bi = VE.tensor_copy(s_lg[:, :], PS[4][0:8, :]) if VE else None
            inc(bi, "vrt")
            cap_lgev = cnt["vrt"]
            wt(TE, "vrt", cap_lgev)
            for tc in range(4):
                bi = TE.transpose(PS[5][:, tc * 8:(tc + 1) * 8],
                                  s_lg[:, tc * 128:(tc + 1) * 128],
                                  k_id[0:8, 0:8]) if TE else None
                inc(bi, "trt")
            wt(VE, "trt", cnt["trt"])
            bi = VE.tensor_copy(s_lgt[:, :], PS[5][:, 0:32]) if VE else None
            inc(bi, "vrt")
            for tc in range(4):
                blk = s_lgt[:, tc * 8:(tc + 1) * 8]
                wt(VE, "vrt", cnt["vrt"])
                bi = VE.reduce_max(s_m1[:, :], blk, axis=AX.X) if VE else None
                inc(bi, "vrt")
                wt(VE, "vrt", cnt["vrt"])
                bi = VE.tensor_scalar(s_lg2[:, tc * 8:(tc + 1) * 8], blk,
                                      s_m1[:, :], None, ALU.is_ge) if VE else None
                inc(bi, "vrt")
                wt(VE, "vrt", cnt["vrt"])
                bi = VE.scalar_tensor_tensor(s_wtk[:, tc * 8:(tc + 1) * 8],
                                             s_lg2[:, tc * 8:(tc + 1) * 8],
                                             -1e30, blk, ALU.mult, ALU.add) if VE else None
                inc(bi, "vrt")
                wt(VE, "vrt", cnt["vrt"])
                bi = VE.reduce_max(s_m2[:, :], s_wtk[:, tc * 8:(tc + 1) * 8],
                                   axis=AX.X) if VE else None
                inc(bi, "vrt")
                wt(VE, "vrt", cnt["vrt"])
                bi = VE.tensor_tensor(s_dd[:, :], s_m2[:, :], s_m1[:, :],
                                      ALU.subtract) if VE else None
                inc(bi, "vrt")
                wt(SC, "vrt", cnt["vrt"])
                bi = SC.activation(s_ee[:, :], s_dd[:, :], AF.Exp) if SC else None
                inc(bi, "sp")
                wt(VE, "sp", cnt["sp"])
                bi = VE.tensor_scalar_add(s_p1[:, :], s_ee[:, :], 1.0) if VE else None
                inc(bi, "vrt")
                wt(VE, "vrt", cnt["vrt"])
                bi = VE.reciprocal(s_p1[:, :], s_p1[:, :]) if VE else None
                inc(bi, "vrt")
                wt(VE, "vrt", cnt["vrt"])
                bi = VE.tensor_tensor(s_p2[:, :], s_p1[:, :], s_ee[:, :],
                                      ALU.mult) if VE else None
                inc(bi, "vrt")
                wt(VE, "vrt", cnt["vrt"])
                bi = VE.tensor_scalar(s_lg2[:, tc * 8:(tc + 1) * 8],
                                      s_wtk[:, tc * 8:(tc + 1) * 8],
                                      s_m2[:, :], s_p2[:, :], ALU.is_ge,
                                      ALU.mult) if VE else None
                inc(bi, "vrt")
                wt(VE, "vrt", cnt["vrt"])
                bi = VE.tensor_scalar(s_wtk[:, tc * 8:(tc + 1) * 8], blk,
                                      s_m1[:, :], s_p1[:, :], ALU.is_ge,
                                      ALU.mult) if VE else None
                inc(bi, "vrt")
                wt(VE, "vrt", cnt["vrt"])
                bi = VE.tensor_tensor(s_wtk[:, tc * 8:(tc + 1) * 8],
                                      s_wtk[:, tc * 8:(tc + 1) * 8],
                                      s_lg2[:, tc * 8:(tc + 1) * 8], ALU.add) if VE else None
                inc(bi, "vrt")
            cap_wtk = cnt["vrt"]
            if tg == 0:
                do_tap("wt", s_wtk[:, :], "vrt", cap_wtk)
                do_tap("lg", s_lgt[:, :], "vrt", cap_wtk)

            wt(TE, "vrt", cap_wtk)
            for tc in range(4):
                bi = TE.transpose(PS[5][0:8, tc * 128:(tc + 1) * 128],
                                  s_wtk[:, tc * 8:(tc + 1) * 8],
                                  k_id[:, :]) if TE else None
                inc(bi, "trt")
            wt(VE, "trt", cnt["trt"])
            for tc in range(4):
                bi = VE.tensor_copy(s_wfm8[:, tc * 128:(tc + 1) * 128],
                                    PS[5][0:8, tc * 128:(tc + 1) * 128]) if VE else None
                inc(bi, "vrt")
            cap_wfm8 = cnt["vrt"]
            caps[("c_vrt", tg)] = cnt["vrt"]
            for e in range(NE):
                wt(GP, "vrt", cap_wfm8)
                wt(GP, "trt", cnt["trt"])
                bi = GP.dma_start(out=s_wrow[0:1, :], in_=s_wfm8[e:e + 1, :]) if GP else None
                inc(bi, "grt", 16)
                wt(TE, "grt", cnt["grt"])
                if e == 0:
                    wt(TE, "vrt", cap_wfm8)
                bi = TE.matmul(PS[6][:, :], k_ones1[:, :], s_wrow[0:1, :],
                               start=True, stop=True) if TE else None
                inc(bi, "trt")
                wt(VE, "trt", cnt["trt"])
                bi = VE.tensor_copy(s_wbc[:, e * GT:(e + 1) * GT], PS[6][:, :]) if VE else None
                inc(bi, "vg")
            cap_wbc = cnt["vg"]

            for e in range(NE):
                for mh in range(2):
                    for k in range(NK):
                        si, soff = strip_take()
                        wt(TE, WLD[si % NSLOT], 16 * (si // NSLOT + 1))
                        if k == 0:
                            wt(TE, "vg", cnt["vg"])
                            wt(TE, "vfo", cnt["vfo"])
                        last = None
                        for m in range(8):
                            mg = mh * 8 + m
                            last = TE.matmul(
                                PS[m][:, :],
                                r_w[:, soff + mg * 128:soff + (mg + 1) * 128],
                                s_ln[:, k * GT:(k + 1) * GT],
                                start=(k == 0), stop=(k == NK - 1)) if TE else None
                        inc(last, "sfree")
                        rec_ev(f"sd:{si}", "sfree")
                    cap_fc = cnt["sfree"]
                    wt(VE, "sfree", cap_fc)
                    for m in range(8):
                        mg = mh * 8 + m
                        bi = VE.tensor_copy(s_sq2[:, 0:GT], PS[m][:, :]) if VE else None
                        inc(bi, "vp")
                        wt(VE, "vp", cnt["vp"])
                        bi = VE.scalar_tensor_tensor(
                            s_g[:, mg * GT:(mg + 1) * GT], PS[m][:, :], 0.0,
                            s_sq2[:, 0:GT], ALU.max, ALU.mult) if VE else None
                        inc(bi, "vp")
                        wt(VE, "vp", cnt["vp"])
                        bi = VE.tensor_tensor(s_g[:, mg * GT:(mg + 1) * GT],
                                              s_g[:, mg * GT:(mg + 1) * GT],
                                              s_wbc[:, e * GT:(e + 1) * GT],
                                              ALU.mult) if VE else None
                        inc(bi, "vg")
                cap_g = cnt["vg"]
                for mh in range(2):
                    for k2 in range(NK):
                        si, soff = strip_take()
                        wt(TE, WLD[si % NSLOT], 16 * (si // NSLOT + 1))
                        if k2 == 0:
                            wt(TE, "vg", cap_g)
                            wt(TE, "vfo", cnt["vfo"])
                        last = None
                        for m in range(8):
                            mg = mh * 8 + m
                            last = TE.matmul(
                                PS[m][:, :],
                                r_w[:, soff + mg * 128:soff + (mg + 1) * 128],
                                s_g[:, k2 * GT:(k2 + 1) * GT],
                                start=(k2 == 0), stop=(k2 == NK - 1)) if TE else None
                        inc(last, "sfree")
                        rec_ev(f"sd:{si}", "sfree")
                    cap_pj = cnt["sfree"]
                    wt(VE, "sfree", cap_pj)
                    for m in range(8):
                        mg = mh * 8 + m
                        bi = VE.tensor_tensor(s_fo[:, mg * GT:(mg + 1) * GT],
                                              s_fo[:, mg * GT:(mg + 1) * GT],
                                              PS[m][:, :], ALU.add) if VE else None
                        inc(bi, "vfo")
            caps[("c_vfo", tg)] = cnt["vfo"]
            caps[("c_sfree", tg)] = cnt["sfree"]
            wt(GP, "vfo", cnt["vfo"])
            for m in range(NK):
                bi = GP.dma_start(out=out[m * 128:(m + 1) * 128, tg * GT:(tg + 1) * GT],
                                  in_=s_fo[:, m * GT:(m + 1) * GT]) if GP else None
                inc(bi, "outd", 16)
            caps[("c_outd", tg)] = cnt["outd"]

        wt(GP, "outd", cnt["outd"])
        return cnt

    emit("pre", None)
    state["rec"] = False

    with nc.Block() as block:
        block.sync(lambda e: emit("sync", e))
        block.tensor(lambda e: emit("tensor", e))
        block.vector(lambda e: emit("vector", e))
        block.scalar(lambda e: emit("scalar", e))
        block.gpsimd(lambda e: emit("gpsimd", e))

    return nc


def prep_core_inputs(ins, c):
    g, j = c // 4, c % 4
    u = np.asarray(ins["L0_attn_u"][g])
    W = np.asarray(ins["L0_attn_in_proj_w"])
    Wout = np.asarray(ins["L0_attn_out_proj_w"])
    zr = W[j * 1024:(j + 1) * 1024]
    xr = W[4096 + j * 1024:4096 + (j + 1) * 1024]
    fm = np.concatenate([W[8192:8320], W[8320:8448],
                         W[8448 + 16 * j:8448 + 16 * (j + 1)]], 0)
    fmT = np.zeros((2048, 512), np.float32)
    fmT[:, 0:272] = fm.T
    cb = np.zeros((128, 512), np.float32)
    cb[:, 0:16] = np.asarray(ins["L0_attn_ln_w"]).reshape(16, 128).T
    cb[:, 16:32] = np.asarray(ins["L0_attn_ln_b"]).reshape(16, 128).T
    cb[:, 32] = np.asarray(ins["L0_attn_bnorm_w"])
    cb[:, 33] = np.asarray(ins["L0_attn_cnorm_w"])
    cb[0:16, 34] = np.asarray(ins["L0_attn_dt_bias"][16 * j:16 * (j + 1)])
    cb[0:16, 35] = np.asarray(ins["L0_attn_A_log"][16 * j:16 * (j + 1)])
    cb[:, 128:256] = (np.asarray(ins["L0_ffn_w_router"]).reshape(16, 128, 8)
                      .transpose(1, 0, 2).reshape(128, 128))
    dv = np.asarray(ins["L0_attn_D"][16 * j:16 * (j + 1)])
    return {
        "uT": np.ascontiguousarray(u.T),
        "uq": np.ascontiguousarray(u[j * 1024:(j + 1) * 1024].T),
        "w_tmT": np.ascontiguousarray(np.concatenate([zr, xr], 0).T),
        "w_fmT": fmT,
        "w_outT": np.ascontiguousarray(Wout.T[j * 1024:(j + 1) * 1024]),
        "w_fc": np.ascontiguousarray(ins["L0_ffn_w_fc"]),
        "w_pj": np.ascontiguousarray(ins["L0_ffn_w_proj"]),
        "cblk": cb,
        "dfull": np.ascontiguousarray(
            np.broadcast_to(np.repeat(dv, 64)[None, :], (128, 1024))),
    }


def assemble(results):
    out = np.empty((2, 4096, 2048), dtype=np.float32)
    for c in range(8):
        g, j = c // 4, c % 4
        out[g, j * 1024:(j + 1) * 1024, :] = results[c]["out"].T
    return out


def kernel(**inputs):
    nc = build()
    in_maps = [prep_core_inputs(inputs, c) for c in range(8)]
    res = run_bass_kernel_spmd(nc, in_maps, core_ids=list(range(8)))
    return assemble(res.results)


# version 05
# speedup vs baseline: 1.0217x; 1.0026x over previous
import numpy as np
import concourse.bass as bass
import concourse.mybir as mybir
from concourse.bass_utils import run_bass_kernel_spmd

F32 = mybir.dt.float32
AF = mybir.ActivationFunctionType
ALU = mybir.AluOpType
AX = mybir.AxisListType

D = 2048
NK = 16
SEQ = 4096
NG = 8
GT = 512
CH = 128
HL = 16
P = 64
HP = HL * P
DS = 128
EPS = 1e-5
NSLOT = 3
NE = 8

SEMS = ["uld", "uld1", "wld", "wld1", "wld2", "fmld", "fmld1", "cst", "gcst", "vcst", "scst", "csq", "tsq",
        "tstat", "vp", "sp", "tp", "gp", "vun", "ufree", "tfm", "sfree",
        "fmfree", "vzx", "tyd", "vssd", "typ", "vyf", "vpo", "post", "cc",
        "cld", "vx1", "sq2", "tsq2", "trt", "vrt", "grt", "vg", "vfo", "outd"]


def build(taps=(), stop_after=None):
    nc = bass.Bass("TRN2", target_bir_lowering=False, num_devices=8)

    uT = nc.dram_tensor("uT", [D, SEQ], F32, kind="ExternalInput").ap()
    uq = nc.dram_tensor("uq", [D, 1024], F32, kind="ExternalInput").ap()
    w_tmT = nc.dram_tensor("w_tmT", [D, 2048], F32, kind="ExternalInput").ap()
    w_fmT = nc.dram_tensor("w_fmT", [D, 512], F32, kind="ExternalInput").ap()
    w_outT = nc.dram_tensor("w_outT", [HP, D], F32, kind="ExternalInput").ap()
    w_fc = nc.dram_tensor("w_fc", [NE, D, D], F32, kind="ExternalInput").ap()
    w_pj = nc.dram_tensor("w_pj", [NE, D, D], F32, kind="ExternalInput").ap()
    cblk = nc.dram_tensor("cblk", [128, 512], F32, kind="ExternalInput").ap()
    dfull = nc.dram_tensor("dfull", [128, HP], F32, kind="ExternalInput").ap()

    part_out = nc.dram_tensor("part_out", [4, D, 1024], F32)
    rs_out = nc.dram_tensor("rs_out", [D, 1024], F32)
    out = nc.dram_tensor("out", [D, 1024], F32, kind="ExternalOutput").ap()

    tap_shapes = {"un": [128, NK * GT], "zx": [128, 2 * 2048], "bn": [128, GT],
                  "cn": [128, GT], "da": [HL, GT], "y0": [128, HP],
                  "yf": [128, 8 * GT], "po": [128, 8 * GT], "x1": [128, NK * GT],
                  "ln": [128, NK * GT], "lg": [128, 32], "wt": [128, 32],
                  "sst": [128, HP], "u0": [128, NK * GT], "rows": [1, 8 * GT], "pbc": [128, 2 * GT], "all": [128, 51200], "lnw": [128, 2 * NK], "braw": [128, GT], "brow": [1, GT]}
    tap_t = {t: nc.dram_tensor("tap_" + t, tap_shapes[t], F32,
                               kind="ExternalOutput").ap() for t in taps}

    def sb(name, shape, off):
        assert off + int(np.prod(shape[1:])) * 4 <= 224 * 1024, (name, off)
        return nc.alloc_sbuf_tensor_at(name, shape, F32, offset=off)

    k_id = sb("k_id", [128, 128], 16512)
    k_tril = sb("k_tril", [128, 128], 17024)
    k_ones = sb("k_ones", [128, 1], 17536)
    k_ones1 = sb("k_ones1", [1, 128], 17568)
    c_blkt = sb("c_blk", [128, 512], 18080)
    c_blk = c_blkt[:, :]
    c_lnw = c_blkt[:, 0:16]
    c_lnb = c_blkt[:, 16:32]
    c_bnw = c_blkt[:, 32:33]
    c_cnw = c_blkt[:, 33:34]
    c_dtb = c_blkt[0:HL, 34:35]
    c_alog = c_blkt[0:HL, 35:36]
    c_wrt = c_blkt[:, 128:256]
    c_aneg = sb("c_aneg", [HL, 1], 20160)
    c_dfull = sb("c_dfull", [128, HP], 20224)
    S_state = sb("S_state", [128, HP], 24320)
    r_w = sb("r_w", [128, NSLOT * 2048], 28416)
    s_rows = sb("s_rows", [1, 8 * GT], 52992)

    A0 = 68 * 1024
    s_u = sb("s_u", [128, NK * GT], A0)
    s_un = sb("s_un", [128, NK * GT], A0 + 32 * 1024)
    s_zx = sb("s_zx", [128, 2 * 2048], A0 + 64 * 1024)
    s_B = sb("s_B", [128, GT], A0 + 80 * 1024)
    s_C = sb("s_C", [128, GT], A0 + 82 * 1024)
    s_sq = sb("s_sq", [128, 2 * GT], A0 + 84 * 1024)
    s_dA = sb("s_dA", [HL, GT], A0 + 88 * 1024)
    s_dts = sb("s_dts", [HL, GT], A0 + 90 * 1024)
    s_Acs = sb("s_Acs", [HL, CH], A0 + 92 * 1024)
    s_AcsT = sb("s_AcsT", [128, HL], A0 + 92 * 1024 + 512)
    s_d1 = sb("s_d1", [128, HL], A0 + 92 * 1024 + 576)
    s_d2 = sb("s_d2", [128, HL], A0 + 92 * 1024 + 640)
    s_f = sb("s_f", [128, HL], A0 + 92 * 1024 + 704)
    s_e1 = sb("s_e1", [128, HL], A0 + 92 * 1024 + 768)
    s_E2 = sb("s_E2", [128, HL], A0 + 92 * 1024 + 832)
    s_ds = sb("s_ds", [128, HL], A0 + 92 * 1024 + 896)
    s_Dvb = sb("s_Dvb", [128, HL], A0 + 92 * 1024 + 960)
    s_mr = sb("s_mr", [1, 32], A0 + 93 * 1024)
    s_mc = sb("s_mc", [HL, 1], A0 + 93 * 1024 + 128)
    s_scm = sb("s_scm", [128, CH], A0 + 94 * 1024)
    s_Btm = sb("s_Btm", [128, DS], A0 + 94 * 1024 + 512)
    s_Mall = sb("s_Mall", [128, HL * CH], A0 + 95 * 1024)
    s_xd = sb("s_xd", [128, HP], A0 + 103 * 1024)
    s_y = sb("s_y", [128, HP], A0 + 107 * 1024)
    s_g1 = sb("s_g1", [128, HP], A0 + 111 * 1024)
    s_sz = sb("s_sz", [128, HP], A0 + 115 * 1024)
    s_yfm = sb("s_yfm", [128, 8 * GT], A0 + 119 * 1024)
    s_po = sb("s_po", [128, 8 * GT], A0 + 135 * 1024)
    s_wfm = sb("s_wfm", [128, 2 * 512], A0 + 151 * 1024)

    s_fo = sb("s_fo", [128, NK * GT], A0)
    s_ln = sb("s_ln", [128, NK * GT], A0 + 32 * 1024)
    s_uq = sb("s_uq", [128, NK * GT], A0 + 64 * 1024)
    s_g = sb("s_g", [128, NK * GT], A0 + 96 * 1024)
    s_sq2 = sb("s_sq2", [128, 2 * GT], A0 + 128 * 1024)
    s_lg = sb("s_lg", [8, GT], A0 + 132 * 1024)
    s_wrow = sb("s_wrow", [1, GT], A0 + 134 * 1024)
    s_lgt = sb("s_lgt", [128, 32], A0 + 136 * 1024)
    s_wtk = sb("s_wtk", [128, 32], A0 + 136 * 1024 + 128)
    s_lg2 = sb("s_lg2", [128, 32], A0 + 136 * 1024 + 256)
    s_m1 = sb("s_m1", [128, 1], A0 + 136 * 1024 + 384)
    s_m2 = sb("s_m2", [128, 1], A0 + 136 * 1024 + 416)
    s_p1 = sb("s_p1", [128, 1], A0 + 136 * 1024 + 448)
    s_p2 = sb("s_p2", [128, 1], A0 + 136 * 1024 + 480)
    s_dd = sb("s_dd", [128, 1], A0 + 136 * 1024 + 512)
    s_ee = sb("s_ee", [128, 1], A0 + 136 * 1024 + 544)
    s_wfm8 = sb("s_wfm8", [8, GT], A0 + 137 * 1024)
    s_wbc = sb("s_wbc", [128, NE * GT], A0 + 139 * 1024)

    PS = [nc.alloc_psum_tensor(f"ps{i}", [128, 512], F32) for i in range(8)]

    sems = {n: nc.alloc_semaphore("sem_" + n) for n in SEMS}
    tables = {}
    state = {"rec": True}

    strip_sched = []
    for gi in range(NG):
        for ps in range(2):
            for k in range(NK):
                strip_sched.append(("tm", k))
        for half in range(2):
            for e in range(8):
                strip_sched.append(("out", e))
    for tg in range(2):
        for e in range(NE):
            for mh in range(2):
                for k in range(NK):
                    strip_sched.append(("fc", e, k))
            for mh in range(2):
                for k in range(NK):
                    strip_sched.append(("pj", e, k))
    N_STRIPS = len(strip_sched)

    def strip_src(ent):
        if ent[0] == "tm":
            return w_tmT[ent[1] * 128:(ent[1] + 1) * 128, :]
        if ent[0] == "out":
            return w_outT[ent[1] * 128:(ent[1] + 1) * 128, :]
        if ent[0] == "fc":
            return w_fc[ent[1], ent[2] * 128:(ent[2] + 1) * 128, :]
        return w_pj[ent[1], ent[2] * 128:(ent[2] + 1) * 128, :]

    def emit(eng, E):
        cnt = {n: 0 for n in SEMS}
        rec = state["rec"]
        TE = E if eng == "tensor" else None
        VE = E if eng == "vector" else None
        SC = E if eng == "scalar" else None
        GP = E if eng == "gpsimd" else None
        SY = E if eng == "sync" else None
        R = s_rows

        def inc(bi, ch, n=1):
            cnt[ch] += n
            if bi is not None:
                bi.then_inc(sems[ch], n)
            return cnt[ch]

        def wt(Eng, ch, val):
            if Eng is not None and val is not None and val > 0:
                Eng.wait_ge(sems[ch], val)

        def rec_ev(key, ch):
            if rec:
                tables[key] = (ch, cnt[ch])

        def wt_ev(Eng, key):
            ch, val = tables.get(key, (None, None))
            if Eng is not None and ch is not None and val > 0:
                Eng.wait_ge(sems[ch], val)

        def do_tap(name, buf, ch=None, val=None):
            if name not in tap_t:
                return
            if ch is not None:
                wt(GP, ch, val)
            bi = GP.dma_start(out=tap_t[name], in_=buf) if GP else None
            inc(bi, "gp", 16)
            for Eng in (TE, VE, SC, SY):
                wt(Eng, "gp", cnt["gp"])

        ULD = ["uld", "uld1"]
        WLD = ["wld", "wld1", "wld2"]
        FMLD = ["fmld", "fmld1"]

        strips_issued = [0]
        sidx = [0]

        def strips_issue_upto(n):
            while strips_issued[0] < min(n, N_STRIPS):
                i = strips_issued[0]
                if i >= NSLOT:
                    wt_ev(SY, f"sd:{i - NSLOT}")
                src = strip_src(strip_sched[i])
                bi = SY.dma_start(
                    out=r_w[:, (i % NSLOT) * 2048:(i % NSLOT) * 2048 + src.shape[1]],
                    in_=src) if SY else None
                inc(bi, WLD[i % NSLOT], 16)
                strips_issued[0] += 1

        def strip_take():
            i = sidx[0]
            sidx[0] += 1
            strips_issue_upto(i + NSLOT + 1)
            return i, (i % NSLOT) * 2048

        bi = SC.dma_start(out=c_blk, in_=cblk) if SC else None
        inc(bi, "cst", 16)
        bi = SC.dma_start(out=c_dfull[:, :], in_=dfull) if SC else None
        ncst = inc(bi, "cst", 16)

        bi = GP.memset(k_id[:, :], 1.0) if GP else None
        inc(bi, "gp")
        wt(GP, "gp", cnt["gp"])
        bi = GP.affine_select(out=k_id[:, :], in_=k_id[:, :],
                              compare_op=ALU.is_equal, fill=0.0, base=0,
                              pattern=[[-1, 128]], channel_multiplier=1) if GP else None
        inc(bi, "gp")
        bi = GP.memset(k_tril[:, :], 1.0) if GP else None
        inc(bi, "gp")
        wt(GP, "gp", cnt["gp"])
        bi = GP.affine_select(out=k_tril[:, :], in_=k_tril[:, :],
                              compare_op=ALU.is_ge, fill=0.0, base=0,
                              pattern=[[1, 128]], channel_multiplier=-1) if GP else None
        inc(bi, "gp")
        bi = GP.memset(k_ones[:, :], 1.0) if GP else None
        inc(bi, "gp")
        bi = GP.memset(k_ones1[:, :], 1.0) if GP else None
        inc(bi, "gp")
        bi = GP.memset(S_state[:, :], 0.0) if GP else None
        inc(bi, "gcst")

        wt(SC, "cst", ncst)
        bi = SC.activation(c_aneg[:, :], c_alog, AF.Exp) if SC else None
        inc(bi, "scst")
        wt(VE, "scst", 1)
        bi = VE.tensor_scalar_mul(c_aneg[:, :], c_aneg[:, :], -1.0) if VE else None
        inc(bi, "vcst")

        wt(TE, "gcst", 1)
        wt(TE, "cst", ncst)
        wt(VE, "gcst", 1)
        wt(VE, "cst", ncst)
        wt(SC, "gcst", 1)

        for k in range(NK):
            bi = SC.dma_start(out=s_u[:, k * GT:(k + 1) * GT],
                              in_=uT[k * 128:(k + 1) * 128, 0:GT]) if SC else None
            inc(bi, "uld", 16)
        for kk in range(2):
            bi = SC.dma_start(out=s_wfm[:, (kk % 2) * 512:(kk % 2) * 512 + 512],
                              in_=w_fmT[kk * 128:(kk + 1) * 128, :]) if SC else None
            inc(bi, FMLD[kk % 2], 16)
        strips_issue_upto(NSLOT)

        caps = {}

        for gi in range(NG):
            if gi > 0:
                wt(TE, "vpo", caps[("vpo", gi - 1)])
                wt(TE, "sp", caps[("sp", gi - 1)])

            for k in range(NK):
                if k == 0:
                    wt(TE, ULD[gi % 2], 256 * (gi // 2 + 1))
                bi = TE.matmul(PS[0][0:1, :], k_ones[:, :],
                               s_u[:, k * GT:(k + 1) * GT],
                               start=(k == 0), stop=(k == NK - 1)) if TE else None
                if k == NK - 1:
                    inc(bi, "tstat")
                if k == 0:
                    wt(SC, ULD[gi % 2], 256 * (gi // 2 + 1))
                    wt(SC, "vun", 16 * gi)
                wt(SC, "tsq", 16 * gi + k - 1)
                sbi = SC.activation(s_sq[:, (k % 2) * GT:(k % 2 + 1) * GT],
                                    s_u[:, k * GT:(k + 1) * GT],
                                    AF.Square) if SC else None
                inc(sbi, "csq")
                wt(TE, "csq", 16 * gi + k + 1)
                bi = TE.matmul(PS[1][0:1, :], k_ones[:, :],
                               s_sq[:, (k % 2) * GT:(k % 2 + 1) * GT],
                               start=(k == 0), stop=(k == NK - 1)) if TE else None
                inc(bi, "tsq")

            if gi > 0:
                wt(VE, "sfree", caps[("sfree", gi - 1)])
            wt(VE, "tstat", cnt["tstat"])
            wt(VE, "tsq", cnt["tsq"])
            bi = VE.tensor_scalar_mul(R[0:1, 0 * GT:1 * GT], PS[0][0:1, :], 1.0 / D) if VE else None
            inc(bi, "vp")
            bi = VE.tensor_scalar_mul(R[0:1, 1 * GT:2 * GT], PS[1][0:1, :], 1.0 / D) if VE else None
            inc(bi, "vp")
            wt(VE, "vp", cnt["vp"])
            bi = VE.tensor_tensor(R[0:1, 2 * GT:3 * GT], R[0:1, 0 * GT:1 * GT], R[0:1, 0 * GT:1 * GT], ALU.mult) if VE else None
            inc(bi, "vp")
            wt(VE, "vp", cnt["vp"])
            bi = VE.tensor_tensor(R[0:1, 1 * GT:2 * GT], R[0:1, 1 * GT:2 * GT], R[0:1, 2 * GT:3 * GT], ALU.subtract) if VE else None
            inc(bi, "vp")
            wt(VE, "vp", cnt["vp"])
            bi = VE.tensor_scalar_add(R[0:1, 2 * GT:3 * GT], R[0:1, 1 * GT:2 * GT], EPS) if VE else None
            inc(bi, "vp")
            wt(SC, "vp", cnt["vp"])
            bi = SC.activation(R[0:1, 3 * GT:4 * GT], R[0:1, 2 * GT:3 * GT], AF.Ln) if SC else None
            inc(bi, "sp")
            wt(SC, "sp", cnt["sp"])
            bi = SC.activation(R[0:1, 3 * GT:4 * GT], R[0:1, 3 * GT:4 * GT], AF.Exp, scale=-0.5) if SC else None
            inc(bi, "sp")
            wt(VE, "sp", cnt["sp"])
            bi = VE.reciprocal(R[0:1, 4 * GT:5 * GT], R[0:1, 2 * GT:3 * GT]) if VE else None
            inc(bi, "vp")
            wt(VE, "vp", cnt["vp"])
            bi = VE.tensor_tensor(R[0:1, 4 * GT:5 * GT], R[0:1, 4 * GT:5 * GT], R[0:1, 1 * GT:2 * GT], ALU.mult) if VE else None
            inc(bi, "vp")
            wt(VE, "vp", cnt["vp"])
            bi = VE.tensor_scalar_add(R[0:1, 4 * GT:5 * GT], R[0:1, 4 * GT:5 * GT], EPS) if VE else None
            inc(bi, "vp")
            wt(SC, "vp", cnt["vp"])
            bi = SC.activation(R[0:1, 4 * GT:5 * GT], R[0:1, 4 * GT:5 * GT], AF.Ln) if SC else None
            inc(bi, "sp")
            wt(SC, "sp", cnt["sp"])
            bi = SC.activation(R[0:1, 4 * GT:5 * GT], R[0:1, 4 * GT:5 * GT], AF.Exp, scale=-0.5) if SC else None
            inc(bi, "sp")
            wt(VE, "sp", cnt["sp"])
            bi = VE.tensor_tensor(R[0:1, 5 * GT:6 * GT], R[0:1, 3 * GT:4 * GT], R[0:1, 4 * GT:5 * GT], ALU.mult) if VE else None
            inc(bi, "vp")
            wt(VE, "vp", cnt["vp"])
            bi = VE.scalar_tensor_tensor(R[0:1, 6 * GT:7 * GT], R[0:1, 0 * GT:1 * GT], -1.0, R[0:1, 5 * GT:6 * GT],
                                         ALU.mult, ALU.mult) if VE else None
            inc(bi, "vp")

            cap_rows = cnt["vp"]
            if gi == 0:
                do_tap("u0", s_u[:, :], "vp", cap_rows)
                do_tap("rows", s_rows[:, :], "vp", cap_rows)

            wt(TE, "vp", cap_rows)
            bi = TE.matmul(PS[2][:, :], k_ones1[:, :], R[0:1, 5 * GT:6 * GT],
                           start=True, stop=True) if TE else None
            inc(bi, "tp")
            bi = TE.matmul(PS[3][:, :], k_ones1[:, :], R[0:1, 6 * GT:7 * GT],
                           start=True, stop=True) if TE else None
            inc(bi, "tp")
            cap_bc = cnt["tp"]

            wt(VE, "tp", cap_bc)
            bi = VE.tensor_copy(s_sq[:, 0:GT], PS[2][:, :]) if VE else None
            inc(bi, "vp")
            bi = VE.tensor_copy(s_sq[:, GT:2 * GT], PS[3][:, :]) if VE else None
            inc(bi, "vp")
            wt(VE, "vp", cnt["vp"])
            for k in range(NK):
                bi = VE.tensor_tensor(s_un[:, k * GT:(k + 1) * GT],
                                      s_u[:, k * GT:(k + 1) * GT],
                                      s_sq[:, 0:GT], ALU.mult) if VE else None
                inc(bi, "vp")
                wt(VE, "vp", cnt["vp"])
                bi = VE.tensor_tensor(s_un[:, k * GT:(k + 1) * GT],
                                      s_un[:, k * GT:(k + 1) * GT],
                                      s_sq[:, GT:2 * GT], ALU.add) if VE else None
                inc(bi, "ufree")
                wt(VE, "ufree", cnt["ufree"])
                bi = VE.tensor_scalar(s_un[:, k * GT:(k + 1) * GT],
                                      s_un[:, k * GT:(k + 1) * GT],
                                      c_blkt[:, k:k + 1], c_blkt[:, 16 + k:17 + k],
                                      ALU.mult, ALU.add) if VE else None
                inc(bi, "vun")
            if gi == 0:
                if "pbc" in tap_t:
                    wt(VE, "vun", 16)
                    bi = VE.tensor_copy(s_sq[:, 0:GT], PS[2][:, :]) if VE else None
                    inc(bi, "vp")
                    bi = VE.tensor_copy(s_sq[:, GT:2 * GT], PS[3][:, :]) if VE else None
                    inc(bi, "vp")
                    do_tap("pbc", s_sq[:, :], "vp", cnt["vp"])
                do_tap("un", s_un[:, :], "vun", 16)
                if "lnw" in tap_t:
                    bi = VE.tensor_copy(s_sq[:, 0:NK], c_lnw) if VE else None
                    inc(bi, "vp")
                    bi = VE.tensor_copy(s_sq[:, NK:2 * NK], c_lnb) if VE else None
                    inc(bi, "vp")
                    do_tap("lnw", s_sq[:, 0:2 * NK], "vp", cnt["vp"])
                if "all" in tap_t:
                    s_all = nc.alloc_sbuf_tensor_at("s_all", [128, 51200], F32,
                                                    offset=17024)
                    do_tap("all", s_all[:, :], "vun", 16)
                if "zx" in tap_t:
                    wt(VE, "vun", 16)
                    bi = VE.tensor_copy(s_zx[:, 0:GT], s_un[:, 8 * GT:9 * GT]) if VE else None
                    inc(bi, "vp")
                    bi = VE.tensor_copy(s_zx[:, GT:2 * GT], s_un[:, 0:GT]) if VE else None
                    inc(bi, "vp")
                    do_tap("zx", s_zx[:, :], "vp", cnt["vp"])
            if stop_after == "a5" and gi == 0:
                return cnt
            if gi + 1 < NG:
                for k in range(NK):
                    wt(SC, "ufree", 16 * gi + k + 1)
                    bi = SC.dma_start(
                        out=s_u[:, k * GT:(k + 1) * GT],
                        in_=uT[k * 128:(k + 1) * 128,
                               (gi + 1) * GT:(gi + 2) * GT]) if SC else None
                    inc(bi, ULD[(gi + 1) % 2], 16)

            for k in range(NK):
                gidx = 16 * gi + k
                wt(TE, FMLD[gidx % 2], 16 * (gidx // 2 + 1))
                wt(TE, "vun", 16 * gi + k + 1)
                slot = gidx % 2
                wf = s_wfm[:, slot * 512:slot * 512 + 512]
                bi = TE.matmul(PS[6][:, :], wf[:, 0:128],
                               s_un[:, k * GT:(k + 1) * GT],
                               start=(k == 0), stop=(k == NK - 1)) if TE else None
                if k == NK - 1:
                    inc(bi, "tfm")
                bi2 = TE.matmul(PS[7][:, :], wf[:, 128:256],
                                s_un[:, k * GT:(k + 1) * GT],
                                start=(k == 0), stop=(k == NK - 1)) if TE else None
                if k == NK - 1:
                    inc(bi2, "tfm")
                bi3 = TE.matmul(PS[0][0:HL, :], wf[:, 256:272],
                                s_un[:, k * GT:(k + 1) * GT],
                                start=(k == 0), stop=(k == NK - 1)) if TE else None
                inc(bi3, "fmfree")
                nxt = gidx + 2
                if nxt < 16 * NG:
                    wt(SC, "fmfree", nxt - 1)
                    bi = SC.dma_start(
                        out=s_wfm[:, (nxt % 2) * 512:(nxt % 2) * 512 + 512],
                        in_=w_fmT[(nxt % 16) * 128:(nxt % 16) * 128 + 128, :]) if SC else None
                    inc(bi, FMLD[nxt % 2], 16)

            wt(VE, "tfm", cnt["tfm"])
            bi = VE.tensor_copy(s_B[:, :], PS[6][:, :]) if VE else None
            inc(bi, "vp")
            bi = VE.tensor_copy(s_C[:, :], PS[7][:, :]) if VE else None
            inc(bi, "vp")
            cap_bc_evac = cnt["vp"]
            if gi == 0:
                do_tap("braw", s_B[:, :], "vp", cap_bc_evac)
            for X, cx in ((s_B, c_bnw), (s_C, c_cnw)):
                wt(SC, "vp", cap_bc_evac)
                sqslot = 0 if X is s_B else GT
                bi = SC.activation(s_sq[:, sqslot:sqslot + GT], X[:, :],
                                   AF.Square) if SC else None
                inc(bi, "sp")
                if gi == 0 and X is s_B:
                    do_tap("brow", R[0:1, 2 * GT:3 * GT], "sp", cnt["sp"])
                wt(TE, "sp", cnt["sp"])
                if X is s_C:
                    wt(TE, "vp", cnt["vp"])
                bi = TE.matmul(PS[2][0:1, :], k_ones[:, :], s_sq[:, sqslot:sqslot + GT],
                               start=True, stop=True) if TE else None
                inc(bi, "tp")
                wt(VE, "tp", cnt["tp"])
                bi = VE.tensor_scalar(R[0:1, 2 * GT:3 * GT], PS[2][0:1, :], 1.0 / DS, EPS,
                                      ALU.mult, ALU.add) if VE else None
                inc(bi, "vp")
                wt(SC, "vp", cnt["vp"])
                bi = SC.activation(R[0:1, 2 * GT:3 * GT], R[0:1, 2 * GT:3 * GT], AF.Ln) if SC else None
                inc(bi, "sp")
                wt(SC, "sp", cnt["sp"])
                bi = SC.activation(R[0:1, 2 * GT:3 * GT], R[0:1, 2 * GT:3 * GT], AF.Exp, scale=-0.5) if SC else None
                inc(bi, "sp")
                wt(TE, "sp", cnt["sp"])
                bi = TE.matmul(PS[3][:, :], k_ones1[:, :], R[0:1, 2 * GT:3 * GT],
                               start=True, stop=True) if TE else None
                inc(bi, "tp")
                wt(VE, "tp", cnt["tp"])
                bi = VE.tensor_tensor(X[:, :], X[:, :], PS[3][:, :], ALU.mult) if VE else None
                inc(bi, "vp")
                wt(VE, "vp", cnt["vp"])
                bi = VE.tensor_scalar_mul(X[:, :], X[:, :], cx) if VE else None
                inc(bi, "vp")
            cap_bcn = cnt["vp"]
            if stop_after == "a7" and gi == 0:
                do_tap("bn", s_B[:, :], "vp", cap_bcn)
                do_tap("cn", s_C[:, :], "vp", cap_bcn)
                return cnt
            wt(SC, "tfm", cnt["tfm"])
            bi = SC.activation(s_dts[:, :], PS[0][0:HL, :], AF.Exp,
                               bias=c_dtb) if SC else None
            inc(bi, "sp")
            wt(VE, "sp", cnt["sp"])
            bi = VE.tensor_scalar_add(s_dts[:, :], s_dts[:, :], 1.0) if VE else None
            inc(bi, "vp")
            wt(SC, "vp", cnt["vp"])
            bi = SC.activation(s_dts[:, :], s_dts[:, :], AF.Ln) if SC else None
            inc(bi, "sp")
            wt(VE, "sp", cnt["sp"])
            wt(VE, "vcst", 1)
            bi = VE.tensor_scalar_mul(s_dA[:, :], s_dts[:, :], c_aneg[:, :]) if VE else None
            inc(bi, "vp")
            cap_da = cnt["vp"]
            cap_sp_a7 = cnt["sp"]
            if stop_after == "a7b" and gi == 0:
                do_tap("bn", s_B[:, :], "vp", cap_da)
                do_tap("cn", s_C[:, :], "vp", cap_da)
                do_tap("da", s_dA[:, :], "vp", cap_da)
                return cnt

            if gi == 0:
                do_tap("bn", s_B[:, :], "vp", cap_bcn)
                do_tap("cn", s_C[:, :], "vp", cap_bcn)
                do_tap("da", s_dA[:, :], "vp", cap_da)

            for ps in range(2):
                wt(TE, "sp", cap_sp_a7)
                wt(TE, "vp", cap_bcn)
                wt(TE, "vssd", caps.get(("vssd_last",), 0))
                wt(TE, "vyf", cnt["vyf"])
                for k in range(NK):
                    si, soff = strip_take()
                    wt(TE, WLD[si % NSLOT], 16 * (si // NSLOT + 1))
                    last = None
                    for tt2 in range(2):
                        tglob = ps * 2 + tt2
                        for ns in range(4):
                            last = TE.matmul(
                                PS[tt2 * 4 + ns][:, :],
                                s_un[:, k * GT + tglob * 128:k * GT + tglob * 128 + 128],
                                r_w[:, soff + ns * 512:soff + (ns + 1) * 512],
                                start=(k == 0), stop=(k == NK - 1)) if TE else None
                    inc(last, "sfree")
                    rec_ev(f"sd:{si}", "sfree")
                cap_zx_mm = cnt["sfree"]
                wt(VE, "sfree", cap_zx_mm)
                for tt2 in range(2):
                    for ns in range(4):
                        bi = VE.tensor_copy(
                            s_zx[:, tt2 * 2048 + ns * 512:tt2 * 2048 + (ns + 1) * 512],
                            PS[tt2 * 4 + ns][:, :]) if VE else None
                        inc(bi, "vzx")
                cap_vzx = cnt["vzx"]
                if gi == 0 and ps == 0:
                    do_tap("zx", s_zx[:, :], "vzx", cap_vzx)

                for tt2 in range(2):
                    cloc = ps * 2 + tt2
                    zoff = tt2 * 2048
                    wt(VE, "vp", cap_da)
                    bi = VE.tensor_tensor_scan(
                        s_Acs[:, :], k_ones[0:HL, 0:1].broadcast_to([HL, CH]),
                        s_dA[:, cloc * CH:(cloc + 1) * CH], 0.0,
                        ALU.mult, ALU.add) if VE else None
                    inc(bi, "vp")
                    wt(VE, "vp", cnt["vp"])
                    bi = VE.tensor_tensor(s_mc[:, :], s_Acs[:, 0:1],
                                          s_Acs[:, CH - 1:CH], ALU.add) if VE else None
                    inc(bi, "vp")
                    wt(VE, "vp", cnt["vp"])
                    bi = VE.tensor_scalar_mul(s_mc[:, :], s_mc[:, :], 0.5) if VE else None
                    inc(bi, "vp")
                    cap_scan = cnt["vp"]

                    wt(TE, "vzx", cap_vzx)
                    wt(TE, "vp", cap_scan)
                    bi = TE.matmul(PS[0][:, 0:128], s_B[:, cloc * 128:(cloc + 1) * 128],
                                   s_C[:, cloc * 128:(cloc + 1) * 128],
                                   start=True, stop=True) if TE else None
                    inc(bi, "tp")
                    bi = TE.transpose(PS[1][:, 0:128],
                                      s_B[:, cloc * 128:(cloc + 1) * 128],
                                      k_id[:, :]) if TE else None
                    inc(bi, "tp")
                    bi = TE.transpose(PS[0][:, 128:128 + HL], s_Acs[:, :],
                                      k_id[0:HL, 0:HL]) if TE else None
                    inc(bi, "tp")
                    bi = TE.transpose(PS[0][0:1, 176:176 + HL], s_mc[:, :],
                                      k_id[0:HL, 0:HL]) if TE else None
                    inc(bi, "tp")
                    bi = TE.transpose(PS[0][0:1, 192:192 + HL],
                                      s_Acs[:, CH - 1:CH],
                                      k_id[0:HL, 0:HL]) if TE else None
                    inc(bi, "tp")
                    cap_t1 = cnt["tp"]
                    wt(VE, "tp", cap_t1)
                    bi = VE.tensor_tensor(s_scm[:, :], PS[0][:, 0:128],
                                          k_tril[:, :], ALU.mult) if VE else None
                    inc(bi, "vp")
                    bi = VE.tensor_copy(s_Btm[:, :], PS[1][:, 0:128]) if VE else None
                    inc(bi, "vp")
                    bi = VE.tensor_copy(s_AcsT[:, :], PS[0][:, 128:128 + HL]) if VE else None
                    inc(bi, "vp")
                    bi = VE.tensor_copy(s_mr[0:1, 0:HL], PS[0][0:1, 176:176 + HL]) if VE else None
                    inc(bi, "vp")
                    bi = VE.tensor_copy(s_mr[0:1, 16:16 + HL], PS[0][0:1, 192:192 + HL]) if VE else None
                    inc(bi, "vp")
                    cap_ev1 = cnt["vp"]
                    wt(TE, "vp", cap_ev1)
                    bi = TE.matmul(PS[0][:, 144:144 + HL], k_ones1[:, :],
                                   s_mr[0:1, 0:HL], start=True, stop=True) if TE else None
                    inc(bi, "tp")
                    bi = TE.matmul(PS[0][:, 160:160 + HL], k_ones1[:, :],
                                   s_mr[0:1, 16:16 + HL], start=True, stop=True) if TE else None
                    inc(bi, "tp")
                    cap_t2 = cnt["tp"]
                    wt(VE, "tp", cap_t2)
                    bi = VE.tensor_tensor(s_d1[:, :], PS[0][:, 144:144 + HL],
                                          s_AcsT[:, :], ALU.subtract) if VE else None
                    inc(bi, "vp")
                    bi = VE.tensor_tensor(s_d2[:, :], PS[0][:, 160:160 + HL],
                                          s_AcsT[:, :], ALU.subtract) if VE else None
                    inc(bi, "vp")
                    cap_d = cnt["vp"]
                    wt(SC, "vp", cap_d)
                    bi = SC.activation(s_f[:, :], s_d1[:, :], AF.Exp) if SC else None
                    inc(bi, "sp")
                    bi = SC.activation(s_e1[:, :], s_d1[:, :], AF.Exp, scale=-1.0) if SC else None
                    inc(bi, "sp")
                    bi = SC.activation(s_E2[:, :], s_AcsT[:, :], AF.Exp) if SC else None
                    inc(bi, "sp")
                    bi = SC.activation(s_ds[:, :], s_d2[:, :], AF.Exp) if SC else None
                    inc(bi, "sp")
                    bi = SC.activation(s_Dvb[:, :], PS[0][:, 160:160 + HL], AF.Exp) if SC else None
                    inc(bi, "sp")
                    cap_exp = cnt["sp"]
                    wt(VE, "sp", cap_exp)
                    bi = VE.tensor_tensor(
                        s_Mall[:, :].rearrange("p (h l) -> p h l", h=HL),
                        s_scm[:, :].unsqueeze(1).broadcast_to([128, HL, CH]),
                        s_f[:, :].unsqueeze(2).broadcast_to([128, HL, CH]),
                        ALU.mult) if VE else None
                    inc(bi, "vp")
                    bi = VE.tensor_tensor(
                        s_xd[:, :].rearrange("p (h q) -> p h q", h=HL),
                        s_zx[:, zoff + 1024:zoff + 2048].rearrange("p (h q) -> p h q", h=HL),
                        s_ds[:, :].unsqueeze(2).broadcast_to([128, HL, P]),
                        ALU.mult) if VE else None
                    inc(bi, "vp")
                    cap_mall = cnt["vp"]

                    wt(TE, "vp", cap_mall)
                    for h in range(HL):
                        bank = PS[2] if h < 8 else PS[3]
                        co = (h % 8) * 64
                        bi = TE.matmul(bank[:, co:co + 64],
                                       s_Mall[:, h * CH:(h + 1) * CH],
                                       s_zx[:, zoff + 1024 + h * P:zoff + 1024 + (h + 1) * P],
                                       start=True, stop=True) if TE else None
                        inc(bi, "tp")
                    for half in range(2):
                        bi = TE.matmul(PS[4 + half][:, :],
                                       s_C[:, cloc * 128:(cloc + 1) * 128],
                                       S_state[:, half * 512:(half + 1) * 512],
                                       start=True, stop=True) if TE else None
                        inc(bi, "tp")
                    bi = TE.matmul(PS[6][:, :], s_Btm[:, :],
                                   s_xd[:, 0:512], start=True, stop=True) if TE else None
                    inc(bi, "tp")
                    bi = TE.matmul(PS[7][:, :], s_Btm[:, :],
                                   s_xd[:, 512:1024], start=True, stop=True) if TE else None
                    cap_yd = inc(bi, "tyd")

                    wt(VE, "tyd", cap_yd)
                    bi = VE.tensor_tensor(
                        S_state[:, :].rearrange("p (h q) -> p h q", h=HL),
                        S_state[:, :].rearrange("p (h q) -> p h q", h=HL),
                        s_Dvb[:, :].unsqueeze(2).broadcast_to([128, HL, P]),
                        ALU.mult) if VE else None
                    inc(bi, "vp")
                    wt(VE, "vp", cnt["vp"])
                    for half in range(2):
                        bi = VE.tensor_tensor(S_state[:, half * 512:(half + 1) * 512],
                                              S_state[:, half * 512:(half + 1) * 512],
                                              PS[6 + half][:, :], ALU.add) if VE else None
                        inc(bi, "vp")
                    wt(VE, "vssd", caps.get(("vssd_last",), 0))
                    wt(VE, "vp", cnt["vp"])
                    for half in range(2):
                        bi = VE.tensor_tensor(
                            s_y[:, half * 512:(half + 1) * 512].rearrange(
                                "p (h q) -> p h q", h=8),
                            PS[2 + half][:, :].rearrange("p (h q) -> p h q", h=8),
                            s_e1[:, half * 8:(half + 1) * 8].unsqueeze(2).broadcast_to(
                                [128, 8, P]),
                            ALU.mult) if VE else None
                        inc(bi, "vp")
                        bi = VE.tensor_tensor(
                            s_g1[:, half * 512:(half + 1) * 512].rearrange(
                                "p (h q) -> p h q", h=8),
                            PS[4 + half][:, :].rearrange("p (h q) -> p h q", h=8),
                            s_E2[:, half * 8:(half + 1) * 8].unsqueeze(2).broadcast_to(
                                [128, 8, P]),
                            ALU.mult) if VE else None
                        inc(bi, "vp")
                    wt(VE, "vp", cnt["vp"])
                    bi = VE.tensor_tensor(s_y[:, :], s_y[:, :], s_g1[:, :], ALU.add) if VE else None
                    inc(bi, "vp")
                    wt(VE, "vp", cnt["vp"])
                    bi = VE.tensor_tensor(s_g1[:, :], s_zx[:, zoff + 1024:zoff + 2048],
                                          c_dfull[:, :], ALU.mult) if VE else None
                    inc(bi, "vp")
                    wt(VE, "vp", cnt["vp"])
                    bi = VE.tensor_tensor(s_y[:, :], s_y[:, :], s_g1[:, :], ALU.add) if VE else None
                    inc(bi, "vp")
                    cap_ypre = cnt["vp"]
                    wt(SC, "vzx", cap_vzx)
                    bi = SC.activation(s_sz[:, :], s_zx[:, zoff:zoff + 1024],
                                       AF.Silu) if SC else None
                    inc(bi, "sp")
                    cap_silu = cnt["sp"]
                    wt(VE, "sp", cap_silu)
                    wt(VE, "vp", cap_ypre)
                    bi = VE.tensor_tensor(s_y[:, :], s_y[:, :], s_sz[:, :], ALU.mult) if VE else None
                    vssd_c = inc(bi, "vssd")
                    caps[("vssd_last",)] = vssd_c
                    if gi == 0 and cloc == 0:
                        do_tap("y0", s_y[:, :], "vssd", vssd_c)
                        do_tap("sst", S_state[:, :], "vssd", vssd_c)

                    wt(TE, "vssd", vssd_c)
                    for e8 in range(8):
                        bank = PS[2] if e8 < 4 else PS[3]
                        co = (e8 % 4) * 128
                        bi = TE.transpose(bank[:, co:co + 128],
                                          s_y[:, e8 * 128:(e8 + 1) * 128],
                                          k_id[:, :]) if TE else None
                        inc(bi, "typ")
                    cap_typ = cnt["typ"]
                    wt(VE, "typ", cap_typ)
                    for e8 in range(8):
                        bank = PS[2] if e8 < 4 else PS[3]
                        co = (e8 % 4) * 128
                        bi = VE.tensor_copy(
                            s_yfm[:, e8 * GT + cloc * 128:e8 * GT + (cloc + 1) * 128],
                            bank[:, co:co + 128]) if VE else None
                        inc(bi, "vyf")

            wt(TE, "vyf", cnt["vyf"])
            wt(TE, "sp", cnt["sp"])
            q = gi // 2
            th = gi % 2
            for half in range(2):
                if gi > 0 or half > 0:
                    wt(TE, "vpo", caps.get(("vpo_half",), 0))
                for e in range(8):
                    si, soff = strip_take()
                    wt(TE, WLD[si % NSLOT], 16 * (si // NSLOT + 1))
                    last = None
                    for m in range(8):
                        mg = half * 8 + m
                        last = TE.matmul(PS[m][:, :],
                                         r_w[:, soff + mg * 128:soff + (mg + 1) * 128],
                                         s_yfm[:, e * GT:(e + 1) * GT],
                                         start=(e == 0), stop=(e == 7)) if TE else None
                    inc(last, "sfree")
                    rec_ev(f"sd:{si}", "sfree")
                cap_op = cnt["sfree"]
                wt(VE, "sfree", cap_op)
                wt(VE, "post", caps.get(("post_half",), 0))
                for m in range(8):
                    bi = VE.tensor_copy(s_po[:, m * GT:(m + 1) * GT],
                                        PS[m][:, :]) if VE else None
                    inc(bi, "vpo")
                caps[("vpo_half",)] = cnt["vpo"]
                wt(GP, "vpo", cnt["vpo"])
                for m in range(8):
                    mg = half * 8 + m
                    bi = GP.dma_start(
                        out=part_out[q, mg * 128:(mg + 1) * 128, th * GT:(th + 1) * GT],
                        in_=s_po[:, m * GT:(m + 1) * GT]) if GP else None
                    inc(bi, "post", 16)
                caps[("post_half",)] = cnt["post"]
                if gi == 0 and half == 0:
                    do_tap("po", s_po[:, :], "vpo", caps[("vpo_half",)])
            if gi == 0:
                do_tap("yf", s_yfm[:, :], "vyf", cnt["vyf"])

            caps[("vpo", gi)] = cnt["vpo"]
            caps[("sp", gi)] = cnt["sp"]
            caps[("sfree", gi)] = cnt["sfree"]

        wt(GP, "post", cnt["post"])
        bi = GP.collective_compute(
            "ReduceScatter", ALU.add,
            replica_groups=[[0, 1, 2, 3], [4, 5, 6, 7]],
            ins=[part_out.ap().opt()],
            outs=[rs_out.ap().opt()]) if GP else None
        inc(bi, "cc")

        for Eng in (TE, VE, SC, GP, SY):
            wt(Eng, "cc", 1)

        for tg in range(2):
            if tg > 0:
                wt(SC, "sfree", caps[("c_sfree", 0)])
                wt(SC, "vx1", caps[("c_vx1", 0)])
                wt(SC, "vrt", caps[("c_vrt", 0)])
            base_cld = cnt["cld"]
            for k in range(NK):
                bi = SC.dma_start(out=s_uq[:, k * GT:(k + 1) * GT],
                                  in_=uq[k * 128:(k + 1) * 128,
                                         tg * GT:(tg + 1) * GT]) if SC else None
                inc(bi, "cld", 16)
                bi = SC.dma_start(out=s_g[:, k * GT:(k + 1) * GT],
                                  in_=rs_out[k * 128:(k + 1) * 128,
                                             tg * GT:(tg + 1) * GT]) if SC else None
                inc(bi, "cld", 16)

            if tg > 0:
                wt(TE, "vfo", caps[("c_vfo", 0)])
                wt(TE, "vg", cnt["vg"])
            for k in range(NK):
                if k == 0:
                    wt(TE, "cld", base_cld + 32 * NK)
                bi = TE.matmul(PS[0][0:1, :], k_ones[:, :],
                               s_uq[:, k * GT:(k + 1) * GT],
                               start=(k == 0), stop=(k == NK - 1)) if TE else None
                if k == NK - 1:
                    inc(bi, "tstat")
                if k == 0:
                    wt(SC, "cld", base_cld + 32 * NK)
                wt(SC, "tsq2", cnt["tsq2"] - 1)
                sbi = SC.activation(s_sq2[:, (k % 2) * GT:(k % 2 + 1) * GT],
                                    s_uq[:, k * GT:(k + 1) * GT], AF.Square) if SC else None
                inc(sbi, "sq2")
                wt(TE, "sq2", cnt["sq2"])
                bi = TE.matmul(PS[1][0:1, :], k_ones[:, :],
                               s_sq2[:, (k % 2) * GT:(k % 2 + 1) * GT],
                               start=(k == 0), stop=(k == NK - 1)) if TE else None
                inc(bi, "tsq2")
            wt(VE, "tstat", cnt["tstat"])
            wt(VE, "tsq2", cnt["tsq2"])
            if tg > 0:
                wt(VE, "outd", caps[("c_outd", 0)])
                wt(VE, "sfree", caps[("c_sfree", 0)])
            bi = VE.tensor_scalar_mul(R[0:1, 0 * GT:1 * GT], PS[0][0:1, :], 1.0 / D) if VE else None
            inc(bi, "vp")
            bi = VE.tensor_scalar_mul(R[0:1, 1 * GT:2 * GT], PS[1][0:1, :], 1.0 / D) if VE else None
            inc(bi, "vp")
            wt(VE, "vp", cnt["vp"])
            bi = VE.tensor_tensor(R[0:1, 2 * GT:3 * GT], R[0:1, 0 * GT:1 * GT], R[0:1, 0 * GT:1 * GT], ALU.mult) if VE else None
            inc(bi, "vp")
            wt(VE, "vp", cnt["vp"])
            bi = VE.tensor_tensor(R[0:1, 1 * GT:2 * GT], R[0:1, 1 * GT:2 * GT], R[0:1, 2 * GT:3 * GT], ALU.subtract) if VE else None
            inc(bi, "vp")
            wt(VE, "vp", cnt["vp"])
            bi = VE.tensor_scalar_add(R[0:1, 2 * GT:3 * GT], R[0:1, 1 * GT:2 * GT], EPS) if VE else None
            inc(bi, "vp")
            wt(SC, "vp", cnt["vp"])
            bi = SC.activation(R[0:1, 3 * GT:4 * GT], R[0:1, 2 * GT:3 * GT], AF.Ln) if SC else None
            inc(bi, "sp")
            wt(SC, "sp", cnt["sp"])
            bi = SC.activation(R[0:1, 3 * GT:4 * GT], R[0:1, 3 * GT:4 * GT], AF.Exp, scale=-0.5) if SC else None
            inc(bi, "sp")
            wt(VE, "sp", cnt["sp"])
            bi = VE.tensor_scalar_add(R[0:1, 4 * GT:5 * GT], R[0:1, 3 * GT:4 * GT], 1.0) if VE else None
            inc(bi, "vp")
            bi = VE.scalar_tensor_tensor(R[0:1, 5 * GT:6 * GT], R[0:1, 0 * GT:1 * GT], -1.0, R[0:1, 3 * GT:4 * GT],
                                         ALU.mult, ALU.mult) if VE else None
            inc(bi, "vp")
            cap_r = cnt["vp"]
            wt(TE, "vp", cap_r)
            bi = TE.matmul(PS[2][:, :], k_ones1[:, :], R[0:1, 4 * GT:5 * GT],
                           start=True, stop=True) if TE else None
            inc(bi, "tp")
            bi = TE.matmul(PS[3][:, :], k_ones1[:, :], R[0:1, 5 * GT:6 * GT],
                           start=True, stop=True) if TE else None
            inc(bi, "tp")
            cap_bc2 = cnt["tp"]
            wt(VE, "tp", cap_bc2)
            for k in range(NK):
                bi = VE.tensor_tensor(s_fo[:, k * GT:(k + 1) * GT],
                                      s_uq[:, k * GT:(k + 1) * GT],
                                      PS[2][:, :], ALU.mult) if VE else None
                inc(bi, "vp")
                wt(VE, "vp", cnt["vp"])
                bi = VE.tensor_tensor(s_fo[:, k * GT:(k + 1) * GT],
                                      s_fo[:, k * GT:(k + 1) * GT],
                                      PS[3][:, :], ALU.add) if VE else None
                inc(bi, "vp")
                wt(VE, "vp", cnt["vp"])
                if k == 0:
                    wt(VE, "cld", base_cld + 32 * NK)
                bi = VE.tensor_tensor(s_fo[:, k * GT:(k + 1) * GT],
                                      s_fo[:, k * GT:(k + 1) * GT],
                                      s_g[:, k * GT:(k + 1) * GT], ALU.add) if VE else None
                inc(bi, "vx1")
            cap_x1 = cnt["vx1"]
            caps[("c_vx1", tg)] = cap_x1
            if tg == 0:
                do_tap("x1", s_fo[:, :], "vx1", cap_x1)

            wt(TE, "vx1", cap_x1)
            for k in range(NK):
                bi = TE.matmul(PS[0][0:1, :], k_ones[:, :],
                               s_fo[:, k * GT:(k + 1) * GT],
                               start=(k == 0), stop=(k == NK - 1)) if TE else None
                if k == NK - 1:
                    inc(bi, "tstat")
                wt(SC, "vx1", cap_x1)
                wt(SC, "tsq2", cnt["tsq2"] - 1)
                sbi = SC.activation(s_sq2[:, (k % 2) * GT:(k % 2 + 1) * GT],
                                    s_fo[:, k * GT:(k + 1) * GT], AF.Square) if SC else None
                inc(sbi, "sq2")
                wt(TE, "sq2", cnt["sq2"])
                bi = TE.matmul(PS[1][0:1, :], k_ones[:, :],
                               s_sq2[:, (k % 2) * GT:(k % 2 + 1) * GT],
                               start=(k == 0), stop=(k == NK - 1)) if TE else None
                inc(bi, "tsq2")
            wt(VE, "tstat", cnt["tstat"])
            wt(VE, "tsq2", cnt["tsq2"])
            bi = VE.tensor_scalar_mul(R[0:1, 0 * GT:1 * GT], PS[0][0:1, :], 1.0 / D) if VE else None
            inc(bi, "vp")
            bi = VE.tensor_scalar_mul(R[0:1, 1 * GT:2 * GT], PS[1][0:1, :], 1.0 / D) if VE else None
            inc(bi, "vp")
            wt(VE, "vp", cnt["vp"])
            bi = VE.tensor_tensor(R[0:1, 2 * GT:3 * GT], R[0:1, 0 * GT:1 * GT], R[0:1, 0 * GT:1 * GT], ALU.mult) if VE else None
            inc(bi, "vp")
            wt(VE, "vp", cnt["vp"])
            bi = VE.tensor_tensor(R[0:1, 1 * GT:2 * GT], R[0:1, 1 * GT:2 * GT], R[0:1, 2 * GT:3 * GT], ALU.subtract) if VE else None
            inc(bi, "vp")
            wt(VE, "vp", cnt["vp"])
            bi = VE.tensor_scalar_add(R[0:1, 2 * GT:3 * GT], R[0:1, 1 * GT:2 * GT], EPS) if VE else None
            inc(bi, "vp")
            wt(SC, "vp", cnt["vp"])
            bi = SC.activation(R[0:1, 3 * GT:4 * GT], R[0:1, 2 * GT:3 * GT], AF.Ln) if SC else None
            inc(bi, "sp")
            wt(SC, "sp", cnt["sp"])
            bi = SC.activation(R[0:1, 3 * GT:4 * GT], R[0:1, 3 * GT:4 * GT], AF.Exp, scale=-0.5) if SC else None
            inc(bi, "sp")
            wt(VE, "sp", cnt["sp"])
            bi = VE.scalar_tensor_tensor(R[0:1, 5 * GT:6 * GT], R[0:1, 0 * GT:1 * GT], -1.0, R[0:1, 3 * GT:4 * GT],
                                         ALU.mult, ALU.mult) if VE else None
            inc(bi, "vp")
            cap_r2 = cnt["vp"]
            wt(TE, "vp", cap_r2)
            bi = TE.matmul(PS[2][:, :], k_ones1[:, :], R[0:1, 3 * GT:4 * GT],
                           start=True, stop=True) if TE else None
            inc(bi, "tp")
            bi = TE.matmul(PS[3][:, :], k_ones1[:, :], R[0:1, 5 * GT:6 * GT],
                           start=True, stop=True) if TE else None
            inc(bi, "tp")
            cap_bc3 = cnt["tp"]
            wt(VE, "tp", cap_bc3)
            for k in range(NK):
                bi = VE.tensor_tensor(s_ln[:, k * GT:(k + 1) * GT],
                                      s_fo[:, k * GT:(k + 1) * GT],
                                      PS[2][:, :], ALU.mult) if VE else None
                inc(bi, "vp")
                wt(VE, "vp", cnt["vp"])
                bi = VE.tensor_tensor(s_ln[:, k * GT:(k + 1) * GT],
                                      s_ln[:, k * GT:(k + 1) * GT],
                                      PS[3][:, :], ALU.add) if VE else None
                inc(bi, "vrt")
            cap_lnd = cnt["vrt"]
            if tg == 0:
                do_tap("ln", s_ln[:, :], "vrt", cap_lnd)

            wt(TE, "vrt", cap_lnd)
            for k in range(NK):
                bi = TE.matmul(PS[4][0:8, :], c_wrt[:, k * 8:(k + 1) * 8],
                               s_ln[:, k * GT:(k + 1) * GT],
                               start=(k == 0), stop=(k == NK - 1)) if TE else None
                if k == NK - 1:
                    inc(bi, "trt")
            wt(VE, "trt", cnt["trt"])
            bi = VE.tensor_copy(s_lg[:, :], PS[4][0:8, :]) if VE else None
            inc(bi, "vrt")
            cap_lgev = cnt["vrt"]
            wt(TE, "vrt", cap_lgev)
            for tc in range(4):
                bi = TE.transpose(PS[5][:, tc * 8:(tc + 1) * 8],
                                  s_lg[:, tc * 128:(tc + 1) * 128],
                                  k_id[0:8, 0:8]) if TE else None
                inc(bi, "trt")
            wt(VE, "trt", cnt["trt"])
            bi = VE.tensor_copy(s_lgt[:, :], PS[5][:, 0:32]) if VE else None
            inc(bi, "vrt")
            for tc in range(4):
                blk = s_lgt[:, tc * 8:(tc + 1) * 8]
                wt(VE, "vrt", cnt["vrt"])
                bi = VE.reduce_max(s_m1[:, :], blk, axis=AX.X) if VE else None
                inc(bi, "vrt")
                wt(VE, "vrt", cnt["vrt"])
                bi = VE.tensor_scalar(s_lg2[:, tc * 8:(tc + 1) * 8], blk,
                                      s_m1[:, :], None, ALU.is_ge) if VE else None
                inc(bi, "vrt")
                wt(VE, "vrt", cnt["vrt"])
                bi = VE.scalar_tensor_tensor(s_wtk[:, tc * 8:(tc + 1) * 8],
                                             s_lg2[:, tc * 8:(tc + 1) * 8],
                                             -1e30, blk, ALU.mult, ALU.add) if VE else None
                inc(bi, "vrt")
                wt(VE, "vrt", cnt["vrt"])
                bi = VE.reduce_max(s_m2[:, :], s_wtk[:, tc * 8:(tc + 1) * 8],
                                   axis=AX.X) if VE else None
                inc(bi, "vrt")
                wt(VE, "vrt", cnt["vrt"])
                bi = VE.tensor_tensor(s_dd[:, :], s_m2[:, :], s_m1[:, :],
                                      ALU.subtract) if VE else None
                inc(bi, "vrt")
                wt(SC, "vrt", cnt["vrt"])
                bi = SC.activation(s_ee[:, :], s_dd[:, :], AF.Exp) if SC else None
                inc(bi, "sp")
                wt(VE, "sp", cnt["sp"])
                bi = VE.tensor_scalar_add(s_p1[:, :], s_ee[:, :], 1.0) if VE else None
                inc(bi, "vrt")
                wt(VE, "vrt", cnt["vrt"])
                bi = VE.reciprocal(s_p1[:, :], s_p1[:, :]) if VE else None
                inc(bi, "vrt")
                wt(VE, "vrt", cnt["vrt"])
                bi = VE.tensor_tensor(s_p2[:, :], s_p1[:, :], s_ee[:, :],
                                      ALU.mult) if VE else None
                inc(bi, "vrt")
                wt(VE, "vrt", cnt["vrt"])
                bi = VE.tensor_scalar(s_lg2[:, tc * 8:(tc + 1) * 8],
                                      s_wtk[:, tc * 8:(tc + 1) * 8],
                                      s_m2[:, :], s_p2[:, :], ALU.is_ge,
                                      ALU.mult) if VE else None
                inc(bi, "vrt")
                wt(VE, "vrt", cnt["vrt"])
                bi = VE.tensor_scalar(s_wtk[:, tc * 8:(tc + 1) * 8], blk,
                                      s_m1[:, :], s_p1[:, :], ALU.is_ge,
                                      ALU.mult) if VE else None
                inc(bi, "vrt")
                wt(VE, "vrt", cnt["vrt"])
                bi = VE.tensor_tensor(s_wtk[:, tc * 8:(tc + 1) * 8],
                                      s_wtk[:, tc * 8:(tc + 1) * 8],
                                      s_lg2[:, tc * 8:(tc + 1) * 8], ALU.add) if VE else None
                inc(bi, "vrt")
            cap_wtk = cnt["vrt"]
            if tg == 0:
                do_tap("wt", s_wtk[:, :], "vrt", cap_wtk)
                do_tap("lg", s_lgt[:, :], "vrt", cap_wtk)

            wt(TE, "vrt", cap_wtk)
            for tc in range(4):
                bi = TE.transpose(PS[5][0:8, tc * 128:(tc + 1) * 128],
                                  s_wtk[:, tc * 8:(tc + 1) * 8],
                                  k_id[:, :]) if TE else None
                inc(bi, "trt")
            wt(VE, "trt", cnt["trt"])
            for tc in range(4):
                bi = VE.tensor_copy(s_wfm8[:, tc * 128:(tc + 1) * 128],
                                    PS[5][0:8, tc * 128:(tc + 1) * 128]) if VE else None
                inc(bi, "vrt")
            cap_wfm8 = cnt["vrt"]
            caps[("c_vrt", tg)] = cnt["vrt"]
            for e in range(NE):
                wt(GP, "vrt", cap_wfm8)
                wt(GP, "trt", cnt["trt"])
                bi = GP.dma_start(out=s_wrow[0:1, :], in_=s_wfm8[e:e + 1, :]) if GP else None
                inc(bi, "grt", 16)
                wt(TE, "grt", cnt["grt"])
                wt(TE, "vg", cnt["vg"])
                if e == 0:
                    wt(TE, "vrt", cap_wfm8)
                bi = TE.matmul(PS[6][:, :], k_ones1[:, :], s_wrow[0:1, :],
                               start=True, stop=True) if TE else None
                inc(bi, "trt")
                wt(VE, "trt", cnt["trt"])
                bi = VE.tensor_copy(s_wbc[:, e * GT:(e + 1) * GT], PS[6][:, :]) if VE else None
                inc(bi, "vg")
            cap_wbc = cnt["vg"]

            for e in range(NE):
                for mh in range(2):
                    for k in range(NK):
                        si, soff = strip_take()
                        wt(TE, WLD[si % NSLOT], 16 * (si // NSLOT + 1))
                        if k == 0:
                            wt(TE, "vg", cnt["vg"])
                            wt(TE, "vfo", cnt["vfo"])
                        last = None
                        for m in range(8):
                            mg = mh * 8 + m
                            last = TE.matmul(
                                PS[m][:, :],
                                r_w[:, soff + mg * 128:soff + (mg + 1) * 128],
                                s_ln[:, k * GT:(k + 1) * GT],
                                start=(k == 0), stop=(k == NK - 1)) if TE else None
                        inc(last, "sfree")
                        rec_ev(f"sd:{si}", "sfree")
                    cap_fc = cnt["sfree"]
                    wt(VE, "sfree", cap_fc)
                    for m in range(8):
                        mg = mh * 8 + m
                        bi = VE.tensor_copy(s_sq2[:, 0:GT], PS[m][:, :]) if VE else None
                        inc(bi, "vp")
                        wt(VE, "vp", cnt["vp"])
                        bi = VE.scalar_tensor_tensor(
                            s_g[:, mg * GT:(mg + 1) * GT], PS[m][:, :], 0.0,
                            s_sq2[:, 0:GT], ALU.max, ALU.mult) if VE else None
                        inc(bi, "vp")
                        wt(VE, "vp", cnt["vp"])
                        bi = VE.tensor_tensor(s_g[:, mg * GT:(mg + 1) * GT],
                                              s_g[:, mg * GT:(mg + 1) * GT],
                                              s_wbc[:, e * GT:(e + 1) * GT],
                                              ALU.mult) if VE else None
                        inc(bi, "vg")
                cap_g = cnt["vg"]
                for mh in range(2):
                    for k2 in range(NK):
                        si, soff = strip_take()
                        wt(TE, WLD[si % NSLOT], 16 * (si // NSLOT + 1))
                        if k2 == 0:
                            wt(TE, "vg", cap_g)
                            wt(TE, "vfo", cnt["vfo"])
                        last = None
                        for m in range(8):
                            mg = mh * 8 + m
                            last = TE.matmul(
                                PS[m][:, :],
                                r_w[:, soff + mg * 128:soff + (mg + 1) * 128],
                                s_g[:, k2 * GT:(k2 + 1) * GT],
                                start=(k2 == 0), stop=(k2 == NK - 1)) if TE else None
                        inc(last, "sfree")
                        rec_ev(f"sd:{si}", "sfree")
                    cap_pj = cnt["sfree"]
                    wt(VE, "sfree", cap_pj)
                    for m in range(8):
                        mg = mh * 8 + m
                        bi = VE.tensor_tensor(s_fo[:, mg * GT:(mg + 1) * GT],
                                              s_fo[:, mg * GT:(mg + 1) * GT],
                                              PS[m][:, :], ALU.add) if VE else None
                        inc(bi, "vfo")
            caps[("c_vfo", tg)] = cnt["vfo"]
            caps[("c_sfree", tg)] = cnt["sfree"]
            wt(GP, "vfo", cnt["vfo"])
            for m in range(NK):
                bi = GP.dma_start(out=out[m * 128:(m + 1) * 128, tg * GT:(tg + 1) * GT],
                                  in_=s_fo[:, m * GT:(m + 1) * GT]) if GP else None
                inc(bi, "outd", 16)
            caps[("c_outd", tg)] = cnt["outd"]

        wt(GP, "outd", cnt["outd"])
        return cnt

    emit("pre", None)
    state["rec"] = False

    with nc.Block() as block:
        block.sync(lambda e: emit("sync", e))
        block.tensor(lambda e: emit("tensor", e))
        block.vector(lambda e: emit("vector", e))
        block.scalar(lambda e: emit("scalar", e))
        block.gpsimd(lambda e: emit("gpsimd", e))

    return nc


def prep_core_inputs(ins, c):
    g, j = c // 4, c % 4
    u = np.asarray(ins["L0_attn_u"][g])
    W = np.asarray(ins["L0_attn_in_proj_w"])
    Wout = np.asarray(ins["L0_attn_out_proj_w"])
    zr = W[j * 1024:(j + 1) * 1024]
    xr = W[4096 + j * 1024:4096 + (j + 1) * 1024]
    fm = np.concatenate([W[8192:8320], W[8320:8448],
                         W[8448 + 16 * j:8448 + 16 * (j + 1)]], 0)
    fmT = np.zeros((2048, 512), np.float32)
    fmT[:, 0:272] = fm.T
    cb = np.zeros((128, 512), np.float32)
    cb[:, 0:16] = np.asarray(ins["L0_attn_ln_w"]).reshape(16, 128).T
    cb[:, 16:32] = np.asarray(ins["L0_attn_ln_b"]).reshape(16, 128).T
    cb[:, 32] = np.asarray(ins["L0_attn_bnorm_w"])
    cb[:, 33] = np.asarray(ins["L0_attn_cnorm_w"])
    cb[0:16, 34] = np.asarray(ins["L0_attn_dt_bias"][16 * j:16 * (j + 1)])
    cb[0:16, 35] = np.asarray(ins["L0_attn_A_log"][16 * j:16 * (j + 1)])
    cb[:, 128:256] = (np.asarray(ins["L0_ffn_w_router"]).reshape(16, 128, 8)
                      .transpose(1, 0, 2).reshape(128, 128))
    dv = np.asarray(ins["L0_attn_D"][16 * j:16 * (j + 1)])
    return {
        "uT": np.ascontiguousarray(u.T),
        "uq": np.ascontiguousarray(u[j * 1024:(j + 1) * 1024].T),
        "w_tmT": np.ascontiguousarray(np.concatenate([zr, xr], 0).T),
        "w_fmT": fmT,
        "w_outT": np.ascontiguousarray(Wout.T[j * 1024:(j + 1) * 1024]),
        "w_fc": np.ascontiguousarray(ins["L0_ffn_w_fc"]),
        "w_pj": np.ascontiguousarray(ins["L0_ffn_w_proj"]),
        "cblk": cb,
        "dfull": np.ascontiguousarray(
            np.broadcast_to(np.repeat(dv, 64)[None, :], (128, 1024))),
    }


def assemble(results):
    out = np.empty((2, 4096, 2048), dtype=np.float32)
    for c in range(8):
        g, j = c // 4, c % 4
        out[g, j * 1024:(j + 1) * 1024, :] = results[c]["out"].T
    return out


def kernel(**inputs):
    nc = build()
    in_maps = [prep_core_inputs(inputs, c) for c in range(8)]
    res = run_bass_kernel_spmd(nc, in_maps, core_ids=list(range(8)))
    return assemble(res.results)


# version 08
# speedup vs baseline: 1.2964x; 1.2688x over previous
import numpy as np
import concourse.bass as bass
import concourse.mybir as mybir
from concourse.bass_utils import run_bass_kernel_spmd

F32 = mybir.dt.float32
AF = mybir.ActivationFunctionType
ALU = mybir.AluOpType
AX = mybir.AxisListType

D = 2048
NK = 16
SEQ = 4096
NG = 8
GT = 512
CH = 128
HL = 16
P = 64
HP = HL * P
DS = 128
EPS = 1e-5
NSLOT = 3
NE = 8

SEMS = ["uld", "uld1", "wld", "wld1", "wld2", "fmld", "fmld1", "cst", "gcst", "vcst", "scst", "csq", "tsq",
        "tstat", "vp", "sp", "tp", "gp", "vun", "ufree", "tfm", "sfree",
        "fmfree", "vzx", "tyd", "vssd", "typ", "vyf", "vpo", "post", "cc",
        "cld", "vx1", "sq2", "tsq2", "trt", "vrt", "grt", "vg", "vfo", "outd", "tapd"]


def build(taps=(), stop_after=None):
    nc = bass.Bass("TRN2", target_bir_lowering=False, num_devices=8)

    uT = nc.dram_tensor("uT", [D, SEQ], F32, kind="ExternalInput").ap()
    uq = nc.dram_tensor("uq", [D, 1024], F32, kind="ExternalInput").ap()
    w_tmT = nc.dram_tensor("w_tmT", [D, 2048], F32, kind="ExternalInput").ap()
    w_fmT = nc.dram_tensor("w_fmT", [D, 512], F32, kind="ExternalInput").ap()
    w_outT = nc.dram_tensor("w_outT", [HP, D], F32, kind="ExternalInput").ap()
    w_fc = nc.dram_tensor("w_fc", [NE, D, D], F32, kind="ExternalInput").ap()
    w_pj = nc.dram_tensor("w_pj", [NE, D, D], F32, kind="ExternalInput").ap()
    cblk = nc.dram_tensor("cblk", [128, 512], F32, kind="ExternalInput").ap()
    dfull = nc.dram_tensor("dfull", [128, HP], F32, kind="ExternalInput").ap()

    part_out = nc.dram_tensor("part_out", [4, D, 1024], F32)
    rs_out = nc.dram_tensor("rs_out", [D, 1024], F32)
    out = nc.dram_tensor("out", [D, 1024], F32, kind="ExternalOutput").ap()

    tap_shapes = {"un": [128, NK * GT], "zx": [128, 2 * 2048], "bn": [128, GT],
                  "cn": [128, GT], "da": [HL, GT], "y0": [128, HP],
                  "yf": [128, 8 * GT], "po": [128, 8 * GT], "x1": [128, NK * GT],
                  "ln": [128, NK * GT], "lg": [128, 32], "wt": [128, 32],
                  "sst": [128, HP], "u0": [128, NK * GT], "rows": [1, 8 * GT], "pbc": [128, 2 * GT], "all": [128, 51200], "lnw": [128, 2 * NK], "braw": [128, GT], "brow": [1, GT], "rm8": [8, 512], "rmt": [128, 32], "msk": [8, 512], "rmraw": [8, 512], "oh": [128, 768], "wb": [128, 192], "lnc": [128, 3072], "gc": [128, 3072], "ytm": [96, 4096], "fo": [128, 16 * 512]}
    tap_t = {t: nc.dram_tensor("tap_" + t, tap_shapes[t], F32,
                               kind="ExternalOutput").ap() for t in taps}

    def sb(name, shape, off):
        assert off + int(np.prod(shape[1:])) * 4 <= 224 * 1024, (name, off)
        return nc.alloc_sbuf_tensor_at(name, shape, F32, offset=off)

    k_id = sb("k_id", [128, 128], 16512)
    k_tril = sb("k_tril", [128, 128], 17024)
    k_ones = sb("k_ones", [128, 1], 17536)
    k_ones1 = sb("k_ones1", [1, 128], 17568)
    c_blkt = sb("c_blk", [128, 512], 18080)
    c_blk = c_blkt[:, :]
    c_lnw = c_blkt[:, 0:16]
    c_lnb = c_blkt[:, 16:32]
    c_bnw = c_blkt[:, 32:33]
    c_cnw = c_blkt[:, 33:34]
    c_dtb = c_blkt[0:HL, 34:35]
    c_alog = c_blkt[0:HL, 35:36]
    c_wrt = c_blkt[:, 128:256]
    c_aneg = sb("c_aneg", [HL, 1], 20160)
    c_dfull = sb("c_dfull", [128, HP], 20224)
    S_state = sb("S_state", [128, HP], 24320)
    r_w = sb("r_w", [128, NSLOT * 2048], 28416)
    s_rows = sb("s_rows", [1, 8 * GT], 52992)

    A0 = 68 * 1024
    s_u = sb("s_u", [128, NK * GT], A0)
    s_un = sb("s_un", [128, NK * GT], A0 + 32 * 1024)
    s_zx = sb("s_zx", [128, 2 * 2048], A0 + 64 * 1024)
    s_B = sb("s_B", [128, GT], A0 + 80 * 1024)
    s_C = sb("s_C", [128, GT], A0 + 82 * 1024)
    s_sq = sb("s_sq", [128, 2 * GT], A0 + 84 * 1024)
    s_dA = sb("s_dA", [HL, GT], A0 + 88 * 1024)
    s_dts = sb("s_dts", [HL, GT], A0 + 90 * 1024)
    s_Acs = sb("s_Acs", [HL, CH], A0 + 92 * 1024)
    s_AcsT = sb("s_AcsT", [128, HL], A0 + 92 * 1024 + 512)
    s_d1 = sb("s_d1", [128, HL], A0 + 92 * 1024 + 576)
    s_d2 = sb("s_d2", [128, HL], A0 + 92 * 1024 + 640)
    s_f = sb("s_f", [128, HL], A0 + 92 * 1024 + 704)
    s_e1 = sb("s_e1", [128, HL], A0 + 92 * 1024 + 768)
    s_E2 = sb("s_E2", [128, HL], A0 + 92 * 1024 + 832)
    s_ds = sb("s_ds", [128, HL], A0 + 92 * 1024 + 896)
    s_Dvb = sb("s_Dvb", [128, HL], A0 + 92 * 1024 + 960)
    s_mr = sb("s_mr", [1, 32], A0 + 93 * 1024)
    s_mc = sb("s_mc", [HL, 1], A0 + 93 * 1024 + 128)
    s_scm = sb("s_scm", [128, CH], A0 + 94 * 1024)
    s_Btm = sb("s_Btm", [128, DS], A0 + 94 * 1024 + 512)
    s_Mall = sb("s_Mall", [128, HL * CH], A0 + 95 * 1024)
    s_xd = sb("s_xd", [128, HP], A0 + 103 * 1024)
    s_y = sb("s_y", [128, HP], A0 + 107 * 1024)
    s_g1 = sb("s_g1", [128, HP], A0 + 111 * 1024)
    s_sz = sb("s_sz", [128, HP], A0 + 115 * 1024)
    s_yfm = sb("s_yfm", [128, 8 * GT], A0 + 119 * 1024)
    s_po = sb("s_po", [128, 8 * GT], A0 + 135 * 1024)
    s_wfm = sb("s_wfm", [128, 2 * 512], A0 + 151 * 1024)

    s_fo = sb("s_fo", [128, NK * GT], A0)
    s_ln = sb("s_ln", [128, NK * GT], A0 + 32 * 1024)
    s_uq = sb("s_uq", [128, NK * GT], A0 + 64 * 1024)
    s_g = sb("s_g", [128, NK * GT], A0 + 96 * 1024)
    s_sq2 = sb("s_sq2", [128, 2 * GT], A0 + 128 * 1024)
    s_lg = sb("s_lg", [8, GT], A0 + 132 * 1024)
    s_rm8 = sb("s_rm8", [8, GT], A0 + 134 * 1024)
    s_lgt = sb("s_lgt", [128, 32], A0 + 136 * 1024)
    s_wtk = sb("s_wtk", [128, 32], A0 + 136 * 1024 + 128)
    s_lg2 = sb("s_lg2", [128, 32], A0 + 136 * 1024 + 256)
    s_m1 = sb("s_m1", [128, 1], A0 + 136 * 1024 + 384)
    s_m2 = sb("s_m2", [128, 1], A0 + 136 * 1024 + 416)
    s_p1 = sb("s_p1", [128, 1], A0 + 136 * 1024 + 448)
    s_p2 = sb("s_p2", [128, 1], A0 + 136 * 1024 + 480)
    s_dd = sb("s_dd", [128, 1], A0 + 136 * 1024 + 512)
    s_ee = sb("s_ee", [128, 1], A0 + 136 * 1024 + 544)
    s_wfm8 = sb("s_wfm8", [8, GT], A0 + 137 * 1024)
    s_statc = sb("s_statc", [128, 8], A0 + 136 * 1024 + 576)
    s_rmt = sb("s_rmt", [128, 32], A0 + 136 * 1024 + 608)
    s_mrow = sb("s_mrow", [8, GT], A0 + 139 * 1024)
    k_iota = sb("k_iota", [128, 192], A0 + 141 * 1024)
    s_oh = sb("s_oh", [128, 4 * 192], A0 + 142 * 1024)
    s_ohT = sb("s_ohT", [128, 2 * 512], A0 + 146 * 1024)
    s_wb = sb("s_wb", [128, 192], A0 + 150 * 1024)
    s_lnc = sb("s_lnc", [128, 16 * 192], A0 + 64 * 1024)
    s_gc = sb("s_gc", [128, 16 * 192], A0 + 76 * 1024)
    s_ytm = sb("s_ytm", [128, 2 * 2048], A0 + 96 * 1024)
    c_cswb = sb("c_cswb", [128, 8], 69408)

    PS = [nc.alloc_psum_tensor(f"ps{i}", [128, 512], F32) for i in range(8)]

    sems = {n: nc.alloc_semaphore("sem_" + n) for n in SEMS}
    tables = {}
    state = {"rec": True}

    strip_sched = []
    for gi in range(NG):
        for ps in range(2):
            for k in range(NK):
                strip_sched.append(("tm", k))
        for half in range(2):
            for e in range(8):
                strip_sched.append(("out", e))
    for tg in range(2):
        for e in range(NE):
            for mh in range(2):
                for k in range(NK):
                    strip_sched.append(("fc", e, k, mh))
            for k in range(NK):
                strip_sched.append(("pj", e, k))
    N_STRIPS = len(strip_sched)

    def strip_src(ent):
        if ent[0] == "tm":
            return w_tmT[ent[1] * 128:(ent[1] + 1) * 128, :]
        if ent[0] == "out":
            return w_outT[ent[1] * 128:(ent[1] + 1) * 128, :]
        if ent[0] == "fc":
            return w_fc[ent[1], ent[2] * 128:(ent[2] + 1) * 128,
                        ent[3] * 1024:(ent[3] + 1) * 1024]
        return w_pj[ent[1], ent[2] * 128:(ent[2] + 1) * 128, :]

    def emit(eng, E):
        cnt = {n: 0 for n in SEMS}
        rec = state["rec"]
        TE = E if eng == "tensor" else None
        VE = E if eng == "vector" else None
        SC = E if eng == "scalar" else None
        GP = E if eng == "gpsimd" else None
        SY = E if eng == "sync" else None
        R = s_rows

        def inc(bi, ch, n=1):
            cnt[ch] += n
            if bi is not None:
                bi.then_inc(sems[ch], n)
            return cnt[ch]

        def wt(Eng, ch, val):
            if Eng is not None and val is not None and val > 0:
                Eng.wait_ge(sems[ch], val)

        def rec_ev(key, ch):
            if rec:
                tables[key] = (ch, cnt[ch])

        def wt_ev(Eng, key):
            ch, val = tables.get(key, (None, None))
            if Eng is not None and ch is not None and val > 0:
                Eng.wait_ge(sems[ch], val)

        def do_tap(name, buf, ch=None, val=None):
            if name not in tap_t:
                return
            if ch is not None:
                wt(GP, ch, val)
            bi = GP.dma_start(out=tap_t[name], in_=buf) if GP else None
            inc(bi, "tapd", 16)
            for Eng in (TE, VE, SC, SY, GP):
                wt(Eng, "tapd", cnt["tapd"])

        ULD = ["uld", "uld1"]
        WLD = ["wld", "wld1", "wld2"]
        FMLD = ["fmld", "fmld1"]

        strips_issued = [0]
        sidx = [0]

        def strips_issue_upto(n):
            while strips_issued[0] < min(n, N_STRIPS):
                i = strips_issued[0]
                if i >= NSLOT:
                    wt_ev(SY, f"sd:{i - NSLOT}")
                src = strip_src(strip_sched[i])
                bi = SY.dma_start(
                    out=r_w[:, (i % NSLOT) * 2048:(i % NSLOT) * 2048 + src.shape[1]],
                    in_=src) if SY else None
                inc(bi, WLD[i % NSLOT], 16)
                strips_issued[0] += 1

        def strip_take():
            i = sidx[0]
            sidx[0] += 1
            strips_issue_upto(i + NSLOT + 1)
            return i, (i % NSLOT) * 2048

        bi = SC.dma_start(out=c_blk, in_=cblk) if SC else None
        inc(bi, "cst", 16)
        bi = SC.dma_start(out=c_dfull[:, :], in_=dfull) if SC else None
        ncst = inc(bi, "cst", 16)

        bi = GP.memset(k_id[:, :], 1.0) if GP else None
        inc(bi, "gp")
        wt(GP, "gp", cnt["gp"])
        bi = GP.affine_select(out=k_id[:, :], in_=k_id[:, :],
                              compare_op=ALU.is_equal, fill=0.0, base=0,
                              pattern=[[-1, 128]], channel_multiplier=1) if GP else None
        inc(bi, "gp")
        bi = GP.memset(k_tril[:, :], 1.0) if GP else None
        inc(bi, "gp")
        wt(GP, "gp", cnt["gp"])
        bi = GP.affine_select(out=k_tril[:, :], in_=k_tril[:, :],
                              compare_op=ALU.is_ge, fill=0.0, base=0,
                              pattern=[[1, 128]], channel_multiplier=-1) if GP else None
        inc(bi, "gp")
        bi = GP.memset(k_ones[:, :], 1.0) if GP else None
        inc(bi, "gp")
        bi = GP.memset(k_ones1[:, :], 1.0) if GP else None
        inc(bi, "gp")
        bi = GP.memset(S_state[:, :], 0.0) if GP else None
        inc(bi, "gcst")

        wt(SC, "cst", ncst)
        bi = SC.activation(c_aneg[:, :], c_alog, AF.Exp) if SC else None
        inc(bi, "scst")
        wt(VE, "scst", 1)
        bi = VE.tensor_scalar_mul(c_aneg[:, :], c_aneg[:, :], -1.0) if VE else None
        inc(bi, "vcst")

        wt(TE, "gcst", 1)
        wt(TE, "cst", ncst)
        wt(VE, "gcst", 1)
        wt(VE, "cst", ncst)
        wt(SC, "gcst", 1)

        for k in range(NK):
            bi = TE.matmul(PS[4][0:1, 0:8], k_ones[:, :], c_wrt[:, k * 8:(k + 1) * 8],
                           start=(k == 0), stop=(k == NK - 1)) if TE else None
            if k == NK - 1:
                inc(bi, "tp")
        wt(VE, "tp", cnt["tp"])
        bi = VE.tensor_copy(R[0:1, 7 * GT:7 * GT + 8], PS[4][0:1, 0:8]) if VE else None
        inc(bi, "vp")
        wt(TE, "vp", cnt["vp"])
        bi = TE.matmul(PS[4][:, 0:8], k_ones1[:, :], R[0:1, 7 * GT:7 * GT + 8],
                       start=True, stop=True) if TE else None
        inc(bi, "tp")
        wt(VE, "tp", cnt["tp"])
        bi = VE.tensor_copy(c_cswb[:, :], PS[4][:, 0:8]) if VE else None
        inc(bi, "vp")

        for k in range(NK):
            bi = SC.dma_start(out=s_u[:, k * GT:(k + 1) * GT],
                              in_=uT[k * 128:(k + 1) * 128, 0:GT]) if SC else None
            inc(bi, "uld", 16)
        for kk in range(2):
            bi = SC.dma_start(out=s_wfm[:, (kk % 2) * 512:(kk % 2) * 512 + 512],
                              in_=w_fmT[kk * 128:(kk + 1) * 128, :]) if SC else None
            inc(bi, FMLD[kk % 2], 16)
        strips_issue_upto(NSLOT)

        caps = {}

        for gi in range(NG):
            if gi > 0:
                wt(TE, "vpo", caps[("vpo", gi - 1)])
                wt(TE, "sp", caps[("sp", gi - 1)])

            for k in range(NK):
                if k == 0:
                    wt(TE, ULD[gi % 2], 256 * (gi // 2 + 1))
                bi = TE.matmul(PS[0][0:1, :], k_ones[:, :],
                               s_u[:, k * GT:(k + 1) * GT],
                               start=(k == 0), stop=(k == NK - 1)) if TE else None
                if k == NK - 1:
                    inc(bi, "tstat")
                if k == 0:
                    wt(SC, ULD[gi % 2], 256 * (gi // 2 + 1))
                    wt(SC, "vun", 16 * gi)
                wt(SC, "tsq", 16 * gi + k - 1)
                sbi = SC.activation(s_sq[:, (k % 2) * GT:(k % 2 + 1) * GT],
                                    s_u[:, k * GT:(k + 1) * GT],
                                    AF.Square) if SC else None
                inc(sbi, "csq")
                wt(TE, "csq", 16 * gi + k + 1)
                bi = TE.matmul(PS[1][0:1, :], k_ones[:, :],
                               s_sq[:, (k % 2) * GT:(k % 2 + 1) * GT],
                               start=(k == 0), stop=(k == NK - 1)) if TE else None
                inc(bi, "tsq")

            if gi > 0:
                wt(VE, "sfree", caps[("sfree", gi - 1)])
            wt(VE, "tstat", cnt["tstat"])
            wt(VE, "tsq", cnt["tsq"])
            bi = VE.tensor_scalar_mul(R[0:1, 0 * GT:1 * GT], PS[0][0:1, :], 1.0 / D) if VE else None
            inc(bi, "vp")
            bi = VE.tensor_scalar_mul(R[0:1, 1 * GT:2 * GT], PS[1][0:1, :], 1.0 / D) if VE else None
            inc(bi, "vp")
            wt(VE, "vp", cnt["vp"])
            bi = VE.tensor_tensor(R[0:1, 2 * GT:3 * GT], R[0:1, 0 * GT:1 * GT], R[0:1, 0 * GT:1 * GT], ALU.mult) if VE else None
            inc(bi, "vp")
            wt(VE, "vp", cnt["vp"])
            bi = VE.tensor_tensor(R[0:1, 1 * GT:2 * GT], R[0:1, 1 * GT:2 * GT], R[0:1, 2 * GT:3 * GT], ALU.subtract) if VE else None
            inc(bi, "vp")
            wt(VE, "vp", cnt["vp"])
            bi = VE.tensor_scalar_add(R[0:1, 2 * GT:3 * GT], R[0:1, 1 * GT:2 * GT], EPS) if VE else None
            inc(bi, "vp")
            wt(SC, "vp", cnt["vp"])
            bi = SC.activation(R[0:1, 3 * GT:4 * GT], R[0:1, 2 * GT:3 * GT], AF.Ln) if SC else None
            inc(bi, "sp")
            wt(SC, "sp", cnt["sp"])
            bi = SC.activation(R[0:1, 3 * GT:4 * GT], R[0:1, 3 * GT:4 * GT], AF.Exp, scale=-0.5) if SC else None
            inc(bi, "sp")
            wt(VE, "sp", cnt["sp"])
            bi = VE.reciprocal(R[0:1, 4 * GT:5 * GT], R[0:1, 2 * GT:3 * GT]) if VE else None
            inc(bi, "vp")
            wt(VE, "vp", cnt["vp"])
            bi = VE.tensor_tensor(R[0:1, 4 * GT:5 * GT], R[0:1, 4 * GT:5 * GT], R[0:1, 1 * GT:2 * GT], ALU.mult) if VE else None
            inc(bi, "vp")
            wt(VE, "vp", cnt["vp"])
            bi = VE.tensor_scalar_add(R[0:1, 4 * GT:5 * GT], R[0:1, 4 * GT:5 * GT], EPS) if VE else None
            inc(bi, "vp")
            wt(SC, "vp", cnt["vp"])
            bi = SC.activation(R[0:1, 4 * GT:5 * GT], R[0:1, 4 * GT:5 * GT], AF.Ln) if SC else None
            inc(bi, "sp")
            wt(SC, "sp", cnt["sp"])
            bi = SC.activation(R[0:1, 4 * GT:5 * GT], R[0:1, 4 * GT:5 * GT], AF.Exp, scale=-0.5) if SC else None
            inc(bi, "sp")
            wt(VE, "sp", cnt["sp"])
            bi = VE.tensor_tensor(R[0:1, 5 * GT:6 * GT], R[0:1, 3 * GT:4 * GT], R[0:1, 4 * GT:5 * GT], ALU.mult) if VE else None
            inc(bi, "vp")
            wt(VE, "vp", cnt["vp"])
            bi = VE.scalar_tensor_tensor(R[0:1, 6 * GT:7 * GT], R[0:1, 0 * GT:1 * GT], -1.0, R[0:1, 5 * GT:6 * GT],
                                         ALU.mult, ALU.mult) if VE else None
            inc(bi, "vp")

            cap_rows = cnt["vp"]
            if gi == 0:
                do_tap("u0", s_u[:, :], "vp", cap_rows)
                do_tap("rows", s_rows[:, :], "vp", cap_rows)

            wt(TE, "vp", cap_rows)
            bi = TE.matmul(PS[2][:, :], k_ones1[:, :], R[0:1, 5 * GT:6 * GT],
                           start=True, stop=True) if TE else None
            inc(bi, "tp")
            bi = TE.matmul(PS[3][:, :], k_ones1[:, :], R[0:1, 6 * GT:7 * GT],
                           start=True, stop=True) if TE else None
            inc(bi, "tp")
            cap_bc = cnt["tp"]

            wt(VE, "tp", cap_bc)
            bi = VE.tensor_copy(s_sq[:, 0:GT], PS[2][:, :]) if VE else None
            inc(bi, "vp")
            bi = VE.tensor_copy(s_sq[:, GT:2 * GT], PS[3][:, :]) if VE else None
            inc(bi, "vp")
            wt(VE, "vp", cnt["vp"])
            for k in range(NK):
                bi = VE.tensor_tensor(s_un[:, k * GT:(k + 1) * GT],
                                      s_u[:, k * GT:(k + 1) * GT],
                                      s_sq[:, 0:GT], ALU.mult) if VE else None
                inc(bi, "vp")
                wt(VE, "vp", cnt["vp"])
                bi = VE.tensor_tensor(s_un[:, k * GT:(k + 1) * GT],
                                      s_un[:, k * GT:(k + 1) * GT],
                                      s_sq[:, GT:2 * GT], ALU.add) if VE else None
                inc(bi, "ufree")
                wt(VE, "ufree", cnt["ufree"])
                bi = VE.tensor_scalar(s_un[:, k * GT:(k + 1) * GT],
                                      s_un[:, k * GT:(k + 1) * GT],
                                      c_blkt[:, k:k + 1], c_blkt[:, 16 + k:17 + k],
                                      ALU.mult, ALU.add) if VE else None
                inc(bi, "vun")
            if gi == 0:
                if "pbc" in tap_t:
                    wt(VE, "vun", 16)
                    bi = VE.tensor_copy(s_sq[:, 0:GT], PS[2][:, :]) if VE else None
                    inc(bi, "vp")
                    bi = VE.tensor_copy(s_sq[:, GT:2 * GT], PS[3][:, :]) if VE else None
                    inc(bi, "vp")
                    do_tap("pbc", s_sq[:, :], "vp", cnt["vp"])
                do_tap("un", s_un[:, :], "vun", 16)
                if "lnw" in tap_t:
                    bi = VE.tensor_copy(s_sq[:, 0:NK], c_lnw) if VE else None
                    inc(bi, "vp")
                    bi = VE.tensor_copy(s_sq[:, NK:2 * NK], c_lnb) if VE else None
                    inc(bi, "vp")
                    do_tap("lnw", s_sq[:, 0:2 * NK], "vp", cnt["vp"])
                if "all" in tap_t:
                    s_all = nc.alloc_sbuf_tensor_at("s_all", [128, 51200], F32,
                                                    offset=17024)
                    do_tap("all", s_all[:, :], "vun", 16)
                if "zx" in tap_t:
                    wt(VE, "vun", 16)
                    bi = VE.tensor_copy(s_zx[:, 0:GT], s_un[:, 8 * GT:9 * GT]) if VE else None
                    inc(bi, "vp")
                    bi = VE.tensor_copy(s_zx[:, GT:2 * GT], s_un[:, 0:GT]) if VE else None
                    inc(bi, "vp")
                    do_tap("zx", s_zx[:, :], "vp", cnt["vp"])
            if stop_after == "a5" and gi == 0:
                return cnt
            if gi + 1 < NG:
                for k in range(NK):
                    wt(SC, "ufree", 16 * gi + k + 1)
                    bi = SC.dma_start(
                        out=s_u[:, k * GT:(k + 1) * GT],
                        in_=uT[k * 128:(k + 1) * 128,
                               (gi + 1) * GT:(gi + 2) * GT]) if SC else None
                    inc(bi, ULD[(gi + 1) % 2], 16)

            for k in range(NK):
                gidx = 16 * gi + k
                wt(TE, FMLD[gidx % 2], 16 * (gidx // 2 + 1))
                wt(TE, "vun", 16 * gi + k + 1)
                slot = gidx % 2
                wf = s_wfm[:, slot * 512:slot * 512 + 512]
                bi = TE.matmul(PS[6][:, :], wf[:, 0:128],
                               s_un[:, k * GT:(k + 1) * GT],
                               start=(k == 0), stop=(k == NK - 1)) if TE else None
                if k == NK - 1:
                    inc(bi, "tfm")
                bi2 = TE.matmul(PS[7][:, :], wf[:, 128:256],
                                s_un[:, k * GT:(k + 1) * GT],
                                start=(k == 0), stop=(k == NK - 1)) if TE else None
                if k == NK - 1:
                    inc(bi2, "tfm")
                bi3 = TE.matmul(PS[0][0:HL, :], wf[:, 256:272],
                                s_un[:, k * GT:(k + 1) * GT],
                                start=(k == 0), stop=(k == NK - 1)) if TE else None
                inc(bi3, "fmfree")
                nxt = gidx + 2
                if nxt < 16 * NG:
                    wt(SC, "fmfree", nxt - 1)
                    bi = SC.dma_start(
                        out=s_wfm[:, (nxt % 2) * 512:(nxt % 2) * 512 + 512],
                        in_=w_fmT[(nxt % 16) * 128:(nxt % 16) * 128 + 128, :]) if SC else None
                    inc(bi, FMLD[nxt % 2], 16)

            wt(VE, "tfm", cnt["tfm"])
            bi = VE.tensor_copy(s_B[:, :], PS[6][:, :]) if VE else None
            inc(bi, "vp")
            bi = VE.tensor_copy(s_C[:, :], PS[7][:, :]) if VE else None
            inc(bi, "vp")
            cap_bc_evac = cnt["vp"]
            if gi == 0:
                do_tap("braw", s_B[:, :], "vp", cap_bc_evac)
            for X, cx in ((s_B, c_bnw), (s_C, c_cnw)):
                wt(SC, "vp", cap_bc_evac)
                sqslot = 0 if X is s_B else GT
                bi = SC.activation(s_sq[:, sqslot:sqslot + GT], X[:, :],
                                   AF.Square) if SC else None
                inc(bi, "sp")
                if gi == 0 and X is s_B:
                    do_tap("brow", R[0:1, 2 * GT:3 * GT], "sp", cnt["sp"])
                wt(TE, "sp", cnt["sp"])
                if X is s_C:
                    wt(TE, "vp", cnt["vp"])
                bi = TE.matmul(PS[2][0:1, :], k_ones[:, :], s_sq[:, sqslot:sqslot + GT],
                               start=True, stop=True) if TE else None
                inc(bi, "tp")
                wt(VE, "tp", cnt["tp"])
                bi = VE.tensor_scalar(R[0:1, 2 * GT:3 * GT], PS[2][0:1, :], 1.0 / DS, EPS,
                                      ALU.mult, ALU.add) if VE else None
                inc(bi, "vp")
                wt(SC, "vp", cnt["vp"])
                bi = SC.activation(R[0:1, 2 * GT:3 * GT], R[0:1, 2 * GT:3 * GT], AF.Ln) if SC else None
                inc(bi, "sp")
                wt(SC, "sp", cnt["sp"])
                bi = SC.activation(R[0:1, 2 * GT:3 * GT], R[0:1, 2 * GT:3 * GT], AF.Exp, scale=-0.5) if SC else None
                inc(bi, "sp")
                wt(TE, "sp", cnt["sp"])
                bi = TE.matmul(PS[3][:, :], k_ones1[:, :], R[0:1, 2 * GT:3 * GT],
                               start=True, stop=True) if TE else None
                inc(bi, "tp")
                wt(VE, "tp", cnt["tp"])
                bi = VE.tensor_tensor(X[:, :], X[:, :], PS[3][:, :], ALU.mult) if VE else None
                inc(bi, "vp")
                wt(VE, "vp", cnt["vp"])
                bi = VE.tensor_scalar_mul(X[:, :], X[:, :], cx) if VE else None
                inc(bi, "vp")
            cap_bcn = cnt["vp"]
            if stop_after == "a7" and gi == 0:
                do_tap("bn", s_B[:, :], "vp", cap_bcn)
                do_tap("cn", s_C[:, :], "vp", cap_bcn)
                return cnt
            wt(SC, "tfm", cnt["tfm"])
            bi = SC.activation(s_dts[:, :], PS[0][0:HL, :], AF.Exp,
                               bias=c_dtb) if SC else None
            inc(bi, "sp")
            wt(VE, "sp", cnt["sp"])
            bi = VE.tensor_scalar_add(s_dts[:, :], s_dts[:, :], 1.0) if VE else None
            inc(bi, "vp")
            wt(SC, "vp", cnt["vp"])
            bi = SC.activation(s_dts[:, :], s_dts[:, :], AF.Ln) if SC else None
            inc(bi, "sp")
            wt(VE, "sp", cnt["sp"])
            wt(VE, "vcst", 1)
            bi = VE.tensor_scalar_mul(s_dA[:, :], s_dts[:, :], c_aneg[:, :]) if VE else None
            inc(bi, "vp")
            cap_da = cnt["vp"]
            cap_sp_a7 = cnt["sp"]
            if stop_after == "a7b" and gi == 0:
                do_tap("bn", s_B[:, :], "vp", cap_da)
                do_tap("cn", s_C[:, :], "vp", cap_da)
                do_tap("da", s_dA[:, :], "vp", cap_da)
                return cnt

            if gi == 0:
                do_tap("bn", s_B[:, :], "vp", cap_bcn)
                do_tap("cn", s_C[:, :], "vp", cap_bcn)
                do_tap("da", s_dA[:, :], "vp", cap_da)

            for ps in range(2):
                wt(TE, "sp", cap_sp_a7)
                wt(TE, "vp", cap_bcn)
                wt(TE, "vssd", caps.get(("vssd_last",), 0))
                wt(TE, "vyf", cnt["vyf"])
                for k in range(NK):
                    si, soff = strip_take()
                    wt(TE, WLD[si % NSLOT], 16 * (si // NSLOT + 1))
                    last = None
                    for tt2 in range(2):
                        tglob = ps * 2 + tt2
                        for ns in range(4):
                            last = TE.matmul(
                                PS[tt2 * 4 + ns][:, :],
                                s_un[:, k * GT + tglob * 128:k * GT + tglob * 128 + 128],
                                r_w[:, soff + ns * 512:soff + (ns + 1) * 512],
                                start=(k == 0), stop=(k == NK - 1)) if TE else None
                    inc(last, "sfree")
                    rec_ev(f"sd:{si}", "sfree")
                cap_zx_mm = cnt["sfree"]
                wt(VE, "sfree", cap_zx_mm)
                for tt2 in range(2):
                    for ns in range(4):
                        bi = VE.tensor_copy(
                            s_zx[:, tt2 * 2048 + ns * 512:tt2 * 2048 + (ns + 1) * 512],
                            PS[tt2 * 4 + ns][:, :]) if VE else None
                        inc(bi, "vzx")
                cap_vzx = cnt["vzx"]
                if gi == 0 and ps == 0:
                    do_tap("zx", s_zx[:, :], "vzx", cap_vzx)

                for tt2 in range(2):
                    cloc = ps * 2 + tt2
                    zoff = tt2 * 2048
                    wt(VE, "vp", cap_da)
                    bi = VE.tensor_tensor_scan(
                        s_Acs[:, :], k_ones[0:HL, 0:1].broadcast_to([HL, CH]),
                        s_dA[:, cloc * CH:(cloc + 1) * CH], 0.0,
                        ALU.mult, ALU.add) if VE else None
                    inc(bi, "vp")
                    wt(VE, "vp", cnt["vp"])
                    bi = VE.tensor_tensor(s_mc[:, :], s_Acs[:, 0:1],
                                          s_Acs[:, CH - 1:CH], ALU.add) if VE else None
                    inc(bi, "vp")
                    wt(VE, "vp", cnt["vp"])
                    bi = VE.tensor_scalar_mul(s_mc[:, :], s_mc[:, :], 0.5) if VE else None
                    inc(bi, "vp")
                    cap_scan = cnt["vp"]

                    wt(TE, "vzx", cap_vzx)
                    wt(TE, "vp", cap_scan)
                    bi = TE.matmul(PS[0][:, 0:128], s_B[:, cloc * 128:(cloc + 1) * 128],
                                   s_C[:, cloc * 128:(cloc + 1) * 128],
                                   start=True, stop=True) if TE else None
                    inc(bi, "tp")
                    bi = TE.transpose(PS[1][:, 0:128],
                                      s_B[:, cloc * 128:(cloc + 1) * 128],
                                      k_id[:, :]) if TE else None
                    inc(bi, "tp")
                    bi = TE.transpose(PS[0][:, 128:128 + HL], s_Acs[:, :],
                                      k_id[0:HL, 0:HL]) if TE else None
                    inc(bi, "tp")
                    bi = TE.transpose(PS[0][0:1, 176:176 + HL], s_mc[:, :],
                                      k_id[0:HL, 0:HL]) if TE else None
                    inc(bi, "tp")
                    bi = TE.transpose(PS[0][0:1, 192:192 + HL],
                                      s_Acs[:, CH - 1:CH],
                                      k_id[0:HL, 0:HL]) if TE else None
                    inc(bi, "tp")
                    cap_t1 = cnt["tp"]
                    wt(VE, "tp", cap_t1)
                    bi = VE.tensor_tensor(s_scm[:, :], PS[0][:, 0:128],
                                          k_tril[:, :], ALU.mult) if VE else None
                    inc(bi, "vp")
                    bi = VE.tensor_copy(s_Btm[:, :], PS[1][:, 0:128]) if VE else None
                    inc(bi, "vp")
                    bi = VE.tensor_copy(s_AcsT[:, :], PS[0][:, 128:128 + HL]) if VE else None
                    inc(bi, "vp")
                    bi = VE.tensor_copy(s_mr[0:1, 0:HL], PS[0][0:1, 176:176 + HL]) if VE else None
                    inc(bi, "vp")
                    bi = VE.tensor_copy(s_mr[0:1, 16:16 + HL], PS[0][0:1, 192:192 + HL]) if VE else None
                    inc(bi, "vp")
                    cap_ev1 = cnt["vp"]
                    wt(TE, "vp", cap_ev1)
                    bi = TE.matmul(PS[0][:, 144:144 + HL], k_ones1[:, :],
                                   s_mr[0:1, 0:HL], start=True, stop=True) if TE else None
                    inc(bi, "tp")
                    bi = TE.matmul(PS[0][:, 160:160 + HL], k_ones1[:, :],
                                   s_mr[0:1, 16:16 + HL], start=True, stop=True) if TE else None
                    inc(bi, "tp")
                    cap_t2 = cnt["tp"]
                    wt(VE, "tp", cap_t2)
                    bi = VE.tensor_tensor(s_d1[:, :], PS[0][:, 144:144 + HL],
                                          s_AcsT[:, :], ALU.subtract) if VE else None
                    inc(bi, "vp")
                    bi = VE.tensor_tensor(s_d2[:, :], PS[0][:, 160:160 + HL],
                                          s_AcsT[:, :], ALU.subtract) if VE else None
                    inc(bi, "vp")
                    cap_d = cnt["vp"]
                    wt(SC, "vp", cap_d)
                    bi = SC.activation(s_f[:, :], s_d1[:, :], AF.Exp) if SC else None
                    inc(bi, "sp")
                    bi = SC.activation(s_e1[:, :], s_d1[:, :], AF.Exp, scale=-1.0) if SC else None
                    inc(bi, "sp")
                    bi = SC.activation(s_E2[:, :], s_AcsT[:, :], AF.Exp) if SC else None
                    inc(bi, "sp")
                    bi = SC.activation(s_ds[:, :], s_d2[:, :], AF.Exp) if SC else None
                    inc(bi, "sp")
                    bi = SC.activation(s_Dvb[:, :], PS[0][:, 160:160 + HL], AF.Exp) if SC else None
                    inc(bi, "sp")
                    cap_exp = cnt["sp"]
                    wt(VE, "sp", cap_exp)
                    bi = VE.tensor_tensor(
                        s_Mall[:, :].rearrange("p (h l) -> p h l", h=HL),
                        s_scm[:, :].unsqueeze(1).broadcast_to([128, HL, CH]),
                        s_f[:, :].unsqueeze(2).broadcast_to([128, HL, CH]),
                        ALU.mult) if VE else None
                    inc(bi, "vp")
                    bi = VE.tensor_tensor(
                        s_xd[:, :].rearrange("p (h q) -> p h q", h=HL),
                        s_zx[:, zoff + 1024:zoff + 2048].rearrange("p (h q) -> p h q", h=HL),
                        s_ds[:, :].unsqueeze(2).broadcast_to([128, HL, P]),
                        ALU.mult) if VE else None
                    inc(bi, "vp")
                    cap_mall = cnt["vp"]

                    wt(TE, "vp", cap_mall)
                    for h in range(HL):
                        bank = PS[2] if h < 8 else PS[3]
                        co = (h % 8) * 64
                        bi = TE.matmul(bank[:, co:co + 64],
                                       s_Mall[:, h * CH:(h + 1) * CH],
                                       s_zx[:, zoff + 1024 + h * P:zoff + 1024 + (h + 1) * P],
                                       start=True, stop=True) if TE else None
                        inc(bi, "tp")
                    for half in range(2):
                        bi = TE.matmul(PS[4 + half][:, :],
                                       s_C[:, cloc * 128:(cloc + 1) * 128],
                                       S_state[:, half * 512:(half + 1) * 512],
                                       start=True, stop=True) if TE else None
                        inc(bi, "tp")
                    bi = TE.matmul(PS[6][:, :], s_Btm[:, :],
                                   s_xd[:, 0:512], start=True, stop=True) if TE else None
                    inc(bi, "tp")
                    bi = TE.matmul(PS[7][:, :], s_Btm[:, :],
                                   s_xd[:, 512:1024], start=True, stop=True) if TE else None
                    cap_yd = inc(bi, "tyd")

                    wt(VE, "tyd", cap_yd)
                    bi = VE.tensor_tensor(
                        S_state[:, :].rearrange("p (h q) -> p h q", h=HL),
                        S_state[:, :].rearrange("p (h q) -> p h q", h=HL),
                        s_Dvb[:, :].unsqueeze(2).broadcast_to([128, HL, P]),
                        ALU.mult) if VE else None
                    inc(bi, "vp")
                    wt(VE, "vp", cnt["vp"])
                    for half in range(2):
                        bi = VE.tensor_tensor(S_state[:, half * 512:(half + 1) * 512],
                                              S_state[:, half * 512:(half + 1) * 512],
                                              PS[6 + half][:, :], ALU.add) if VE else None
                        inc(bi, "vp")
                    wt(VE, "vssd", caps.get(("vssd_last",), 0))
                    wt(VE, "vp", cnt["vp"])
                    for half in range(2):
                        bi = VE.tensor_tensor(
                            s_y[:, half * 512:(half + 1) * 512].rearrange(
                                "p (h q) -> p h q", h=8),
                            PS[2 + half][:, :].rearrange("p (h q) -> p h q", h=8),
                            s_e1[:, half * 8:(half + 1) * 8].unsqueeze(2).broadcast_to(
                                [128, 8, P]),
                            ALU.mult) if VE else None
                        inc(bi, "vp")
                        bi = VE.tensor_tensor(
                            s_g1[:, half * 512:(half + 1) * 512].rearrange(
                                "p (h q) -> p h q", h=8),
                            PS[4 + half][:, :].rearrange("p (h q) -> p h q", h=8),
                            s_E2[:, half * 8:(half + 1) * 8].unsqueeze(2).broadcast_to(
                                [128, 8, P]),
                            ALU.mult) if VE else None
                        inc(bi, "vp")
                    wt(VE, "vp", cnt["vp"])
                    bi = VE.tensor_tensor(s_y[:, :], s_y[:, :], s_g1[:, :], ALU.add) if VE else None
                    inc(bi, "vp")
                    wt(VE, "vp", cnt["vp"])
                    bi = VE.tensor_tensor(s_g1[:, :], s_zx[:, zoff + 1024:zoff + 2048],
                                          c_dfull[:, :], ALU.mult) if VE else None
                    inc(bi, "vp")
                    wt(VE, "vp", cnt["vp"])
                    bi = VE.tensor_tensor(s_y[:, :], s_y[:, :], s_g1[:, :], ALU.add) if VE else None
                    inc(bi, "vp")
                    cap_ypre = cnt["vp"]
                    wt(SC, "vzx", cap_vzx)
                    bi = SC.activation(s_sz[:, :], s_zx[:, zoff:zoff + 1024],
                                       AF.Silu) if SC else None
                    inc(bi, "sp")
                    cap_silu = cnt["sp"]
                    wt(VE, "sp", cap_silu)
                    wt(VE, "vp", cap_ypre)
                    bi = VE.tensor_tensor(s_y[:, :], s_y[:, :], s_sz[:, :], ALU.mult) if VE else None
                    vssd_c = inc(bi, "vssd")
                    caps[("vssd_last",)] = vssd_c
                    if gi == 0 and cloc == 0:
                        do_tap("y0", s_y[:, :], "vssd", vssd_c)
                        do_tap("sst", S_state[:, :], "vssd", vssd_c)

                    wt(TE, "vssd", vssd_c)
                    for e8 in range(8):
                        bank = PS[2] if e8 < 4 else PS[3]
                        co = (e8 % 4) * 128
                        bi = TE.transpose(bank[:, co:co + 128],
                                          s_y[:, e8 * 128:(e8 + 1) * 128],
                                          k_id[:, :]) if TE else None
                        inc(bi, "typ")
                    cap_typ = cnt["typ"]
                    wt(VE, "typ", cap_typ)
                    for e8 in range(8):
                        bank = PS[2] if e8 < 4 else PS[3]
                        co = (e8 % 4) * 128
                        bi = VE.tensor_copy(
                            s_yfm[:, e8 * GT + cloc * 128:e8 * GT + (cloc + 1) * 128],
                            bank[:, co:co + 128]) if VE else None
                        inc(bi, "vyf")

            wt(TE, "vyf", cnt["vyf"])
            wt(TE, "sp", cnt["sp"])
            q = gi // 2
            th = gi % 2
            for half in range(2):
                if gi > 0 or half > 0:
                    wt(TE, "vpo", caps.get(("vpo_half",), 0))
                for e in range(8):
                    si, soff = strip_take()
                    wt(TE, WLD[si % NSLOT], 16 * (si // NSLOT + 1))
                    last = None
                    for m in range(8):
                        mg = half * 8 + m
                        last = TE.matmul(PS[m][:, :],
                                         r_w[:, soff + mg * 128:soff + (mg + 1) * 128],
                                         s_yfm[:, e * GT:(e + 1) * GT],
                                         start=(e == 0), stop=(e == 7)) if TE else None
                    inc(last, "sfree")
                    rec_ev(f"sd:{si}", "sfree")
                cap_op = cnt["sfree"]
                wt(VE, "sfree", cap_op)
                wt(VE, "post", caps.get(("post_half",), 0))
                for m in range(8):
                    bi = VE.tensor_copy(s_po[:, m * GT:(m + 1) * GT],
                                        PS[m][:, :]) if VE else None
                    inc(bi, "vpo")
                caps[("vpo_half",)] = cnt["vpo"]
                wt(GP, "vpo", cnt["vpo"])
                for m in range(8):
                    mg = half * 8 + m
                    bi = GP.dma_start(
                        out=part_out[q, mg * 128:(mg + 1) * 128, th * GT:(th + 1) * GT],
                        in_=s_po[:, m * GT:(m + 1) * GT]) if GP else None
                    inc(bi, "post", 16)
                caps[("post_half",)] = cnt["post"]
                if gi == 0 and half == 0:
                    do_tap("po", s_po[:, :], "vpo", caps[("vpo_half",)])
            if gi == 0:
                do_tap("yf", s_yfm[:, :], "vyf", cnt["vyf"])

            caps[("vpo", gi)] = cnt["vpo"]
            caps[("sp", gi)] = cnt["sp"]
            caps[("sfree", gi)] = cnt["sfree"]

        wt(GP, "post", cnt["post"])
        bi = GP.collective_compute(
            "ReduceScatter", ALU.add,
            replica_groups=[[0, 1, 2, 3], [4, 5, 6, 7]],
            ins=[part_out.ap().opt()],
            outs=[rs_out.ap().opt()]) if GP else None
        inc(bi, "cc")

        for Eng in (TE, VE, SC, GP, SY):
            wt(Eng, "cc", 1)
        bi = GP.iota(k_iota[:, :], pattern=[[1, 192]], base=0,
                     channel_multiplier=0,
                     allow_small_or_imprecise_dtypes=True) if GP else None
        inc(bi, "gp")
        for Eng in (TE, VE, SC):
            wt(Eng, "gp", cnt["gp"])

        for tg in range(2):
            if tg > 0:
                wt(SC, "sfree", caps[("c_sfree", 0)])
                wt(SC, "vx1", caps[("c_vx1", 0)])
                wt(SC, "trt", caps[("c_trt", 0)])
                wt(SC, "vfo", caps[("c_vfo", 0)])
            base_cld = cnt["cld"]
            for k in range(NK):
                bi = SC.dma_start(out=s_uq[:, k * GT:(k + 1) * GT],
                                  in_=uq[k * 128:(k + 1) * 128,
                                         tg * GT:(tg + 1) * GT]) if SC else None
                inc(bi, "cld", 16)
                bi = SC.dma_start(out=s_g[:, k * GT:(k + 1) * GT],
                                  in_=rs_out[k * 128:(k + 1) * 128,
                                             tg * GT:(tg + 1) * GT]) if SC else None
                inc(bi, "cld", 16)

            if tg > 0:
                wt(TE, "vfo", caps[("c_vfo", 0)])
                wt(TE, "vg", cnt["vg"])
            for k in range(NK):
                if k == 0:
                    wt(TE, "cld", base_cld + 32 * NK)
                bi = TE.matmul(PS[0][0:1, :], k_ones[:, :],
                               s_uq[:, k * GT:(k + 1) * GT],
                               start=(k == 0), stop=(k == NK - 1)) if TE else None
                if k == NK - 1:
                    inc(bi, "tstat")
                if k == 0:
                    wt(SC, "cld", base_cld + 32 * NK)
                wt(SC, "tsq2", cnt["tsq2"] - 1)
                sbi = SC.activation(s_sq2[:, (k % 2) * GT:(k % 2 + 1) * GT],
                                    s_uq[:, k * GT:(k + 1) * GT], AF.Square) if SC else None
                inc(sbi, "sq2")
                wt(TE, "sq2", cnt["sq2"])
                bi = TE.matmul(PS[1][0:1, :], k_ones[:, :],
                               s_sq2[:, (k % 2) * GT:(k % 2 + 1) * GT],
                               start=(k == 0), stop=(k == NK - 1)) if TE else None
                inc(bi, "tsq2")
            wt(VE, "tstat", cnt["tstat"])
            wt(VE, "tsq2", cnt["tsq2"])
            if tg > 0:
                wt(VE, "outd", caps[("c_outd", 0)])
                wt(VE, "sfree", caps[("c_sfree", 0)])
            bi = VE.tensor_scalar_mul(R[0:1, 0 * GT:1 * GT], PS[0][0:1, :], 1.0 / D) if VE else None
            inc(bi, "vp")
            bi = VE.tensor_scalar_mul(R[0:1, 1 * GT:2 * GT], PS[1][0:1, :], 1.0 / D) if VE else None
            inc(bi, "vp")
            wt(VE, "vp", cnt["vp"])
            bi = VE.tensor_tensor(R[0:1, 2 * GT:3 * GT], R[0:1, 0 * GT:1 * GT], R[0:1, 0 * GT:1 * GT], ALU.mult) if VE else None
            inc(bi, "vp")
            wt(VE, "vp", cnt["vp"])
            bi = VE.tensor_tensor(R[0:1, 1 * GT:2 * GT], R[0:1, 1 * GT:2 * GT], R[0:1, 2 * GT:3 * GT], ALU.subtract) if VE else None
            inc(bi, "vp")
            wt(VE, "vp", cnt["vp"])
            bi = VE.tensor_scalar_add(R[0:1, 2 * GT:3 * GT], R[0:1, 1 * GT:2 * GT], EPS) if VE else None
            inc(bi, "vp")
            wt(SC, "vp", cnt["vp"])
            bi = SC.activation(R[0:1, 3 * GT:4 * GT], R[0:1, 2 * GT:3 * GT], AF.Ln) if SC else None
            inc(bi, "sp")
            wt(SC, "sp", cnt["sp"])
            bi = SC.activation(R[0:1, 3 * GT:4 * GT], R[0:1, 3 * GT:4 * GT], AF.Exp, scale=-0.5) if SC else None
            inc(bi, "sp")
            wt(VE, "sp", cnt["sp"])
            bi = VE.tensor_scalar_add(R[0:1, 4 * GT:5 * GT], R[0:1, 3 * GT:4 * GT], 1.0) if VE else None
            inc(bi, "vp")
            bi = VE.scalar_tensor_tensor(R[0:1, 5 * GT:6 * GT], R[0:1, 0 * GT:1 * GT], -1.0, R[0:1, 3 * GT:4 * GT],
                                         ALU.mult, ALU.mult) if VE else None
            inc(bi, "vp")
            cap_r = cnt["vp"]
            wt(TE, "vp", cap_r)
            bi = TE.matmul(PS[2][:, :], k_ones1[:, :], R[0:1, 4 * GT:5 * GT],
                           start=True, stop=True) if TE else None
            inc(bi, "tp")
            bi = TE.matmul(PS[3][:, :], k_ones1[:, :], R[0:1, 5 * GT:6 * GT],
                           start=True, stop=True) if TE else None
            inc(bi, "tp")
            cap_bc2 = cnt["tp"]
            wt(VE, "tp", cap_bc2)
            for k in range(NK):
                bi = VE.tensor_tensor(s_fo[:, k * GT:(k + 1) * GT],
                                      s_uq[:, k * GT:(k + 1) * GT],
                                      PS[2][:, :], ALU.mult) if VE else None
                inc(bi, "vp")
                wt(VE, "vp", cnt["vp"])
                bi = VE.tensor_tensor(s_fo[:, k * GT:(k + 1) * GT],
                                      s_fo[:, k * GT:(k + 1) * GT],
                                      PS[3][:, :], ALU.add) if VE else None
                inc(bi, "vp")
                wt(VE, "vp", cnt["vp"])
                if k == 0:
                    wt(VE, "cld", base_cld + 32 * NK)
                bi = VE.tensor_tensor(s_fo[:, k * GT:(k + 1) * GT],
                                      s_fo[:, k * GT:(k + 1) * GT],
                                      s_g[:, k * GT:(k + 1) * GT], ALU.add) if VE else None
                inc(bi, "vx1")
            cap_x1 = cnt["vx1"]
            caps[("c_vx1", tg)] = cap_x1
            if tg == 0:
                do_tap("x1", s_fo[:, :], "vx1", cap_x1)

            wt(TE, "vx1", cap_x1)
            for k in range(NK):
                bi = TE.matmul(PS[0][0:1, :], k_ones[:, :],
                               s_fo[:, k * GT:(k + 1) * GT],
                               start=(k == 0), stop=(k == NK - 1)) if TE else None
                if k == NK - 1:
                    inc(bi, "tstat")
                wt(SC, "vx1", cap_x1)
                wt(SC, "tsq2", cnt["tsq2"] - 1)
                sbi = SC.activation(s_sq2[:, (k % 2) * GT:(k % 2 + 1) * GT],
                                    s_fo[:, k * GT:(k + 1) * GT], AF.Square) if SC else None
                inc(sbi, "sq2")
                wt(TE, "sq2", cnt["sq2"])
                bi = TE.matmul(PS[1][0:1, :], k_ones[:, :],
                               s_sq2[:, (k % 2) * GT:(k % 2 + 1) * GT],
                               start=(k == 0), stop=(k == NK - 1)) if TE else None
                inc(bi, "tsq2")
            wt(VE, "tstat", cnt["tstat"])
            wt(VE, "tsq2", cnt["tsq2"])
            bi = VE.tensor_scalar_mul(R[0:1, 0 * GT:1 * GT], PS[0][0:1, :], 1.0 / D) if VE else None
            inc(bi, "vp")
            bi = VE.tensor_scalar_mul(R[0:1, 1 * GT:2 * GT], PS[1][0:1, :], 1.0 / D) if VE else None
            inc(bi, "vp")
            wt(VE, "vp", cnt["vp"])
            bi = VE.tensor_tensor(R[0:1, 2 * GT:3 * GT], R[0:1, 0 * GT:1 * GT], R[0:1, 0 * GT:1 * GT], ALU.mult) if VE else None
            inc(bi, "vp")
            wt(VE, "vp", cnt["vp"])
            bi = VE.tensor_tensor(R[0:1, 1 * GT:2 * GT], R[0:1, 1 * GT:2 * GT], R[0:1, 2 * GT:3 * GT], ALU.subtract) if VE else None
            inc(bi, "vp")
            wt(VE, "vp", cnt["vp"])
            bi = VE.tensor_scalar_add(R[0:1, 2 * GT:3 * GT], R[0:1, 1 * GT:2 * GT], EPS) if VE else None
            inc(bi, "vp")
            wt(SC, "vp", cnt["vp"])
            bi = SC.activation(R[0:1, 3 * GT:4 * GT], R[0:1, 2 * GT:3 * GT], AF.Ln) if SC else None
            inc(bi, "sp")
            wt(SC, "sp", cnt["sp"])
            bi = SC.activation(R[0:1, 3 * GT:4 * GT], R[0:1, 3 * GT:4 * GT], AF.Exp, scale=-0.5) if SC else None
            inc(bi, "sp")
            wt(VE, "sp", cnt["sp"])
            bi = VE.scalar_tensor_tensor(R[0:1, 5 * GT:6 * GT], R[0:1, 0 * GT:1 * GT], -1.0, R[0:1, 3 * GT:4 * GT],
                                         ALU.mult, ALU.mult) if VE else None
            inc(bi, "vp")
            cap_r2 = cnt["vp"]

            wt(TE, "vp", cap_r2)
            for tc in range(4):
                bi = TE.transpose(PS[6][:, tc:tc + 1],
                                  R[0:1, 3 * GT + tc * 128:3 * GT + (tc + 1) * 128],
                                  k_id[0:1, 0:1]) if TE else None
                inc(bi, "tp")
                bi = TE.transpose(PS[6][:, 4 + tc:5 + tc],
                                  R[0:1, 5 * GT + tc * 128:5 * GT + (tc + 1) * 128],
                                  k_id[0:1, 0:1]) if TE else None
                inc(bi, "tp")
            cap_stc_t = cnt["tp"]
            wt(VE, "tp", cap_stc_t)
            bi = VE.tensor_copy(s_statc[:, :], PS[6][:, 0:8]) if VE else None
            inc(bi, "vrt")
            cap_statc = cnt["vrt"]

            for tc in range(4):
                for kb in range(2):
                    for j in range(8):
                        k = kb * 8 + j
                        bank = PS[4] if j < 4 else PS[5]
                        co = (j % 4) * 128
                        if j == 0:
                            wt(TE, "vrt", cnt["vrt"])
                        bi = TE.transpose(
                            bank[:, co:co + 128],
                            s_fo[:, k * GT + tc * 128:k * GT + (tc + 1) * 128],
                            k_id[:, :]) if TE else None
                        inc(bi, "tp")
                    wt(VE, "tp", cnt["tp"])
                    for j in range(8):
                        k = kb * 8 + j
                        bank = PS[4] if j < 4 else PS[5]
                        co = (j % 4) * 128
                        bi = VE.tensor_copy(
                            s_ln[:, tc * 2048 + k * 128:tc * 2048 + (k + 1) * 128],
                            bank[:, co:co + 128]) if VE else None
                        inc(bi, "vrt")
            cap_x1tm = cnt["vrt"]

            wt(VE, "vrt", cap_x1tm)
            for tc in range(4):
                bi = VE.tensor_scalar(s_ln[:, tc * 2048:(tc + 1) * 2048],
                                      s_ln[:, tc * 2048:(tc + 1) * 2048],
                                      s_statc[:, tc:tc + 1],
                                      s_statc[:, 4 + tc:5 + tc],
                                      ALU.mult, ALU.add) if VE else None
                inc(bi, "vrt")
            cap_lnd = cnt["vrt"]
            if tg == 0:
                do_tap("ln", s_ln[:, :], "vrt", cap_lnd)

            wt(TE, "vrt", cap_lnd)
            for k in range(NK):
                bi = TE.matmul(PS[4][0:8, :], c_wrt[:, k * 8:(k + 1) * 8],
                               s_fo[:, k * GT:(k + 1) * GT],
                               start=(k == 0), stop=(k == NK - 1)) if TE else None
                if k == NK - 1:
                    inc(bi, "trt")
            wt(VE, "trt", cnt["trt"])
            bi = VE.tensor_copy(s_lg[:, :], PS[4][0:8, :]) if VE else None
            inc(bi, "vrt")
            cap_lgev = cnt["vrt"]
            wt(TE, "vrt", cap_lgev)
            for tc in range(4):
                bi = TE.transpose(PS[5][:, tc * 8:(tc + 1) * 8],
                                  s_lg[:, tc * 128:(tc + 1) * 128],
                                  k_id[0:8, 0:8]) if TE else None
                inc(bi, "trt")
            wt(VE, "trt", cnt["trt"])
            bi = VE.tensor_copy(s_lgt[:, :], PS[5][:, 0:32]) if VE else None
            inc(bi, "vrt")
            for tc in range(4):
                wt(VE, "vrt", cnt["vrt"])
                bi = VE.tensor_scalar_mul(s_lgt[:, tc * 8:(tc + 1) * 8],
                                          s_lgt[:, tc * 8:(tc + 1) * 8],
                                          s_statc[:, tc:tc + 1]) if VE else None
                inc(bi, "vrt")
                bi = VE.tensor_scalar_mul(s_lg2[:, tc * 8:(tc + 1) * 8],
                                          c_cswb[:, :],
                                          s_statc[:, 4 + tc:5 + tc]) if VE else None
                inc(bi, "vrt")
                wt(VE, "vrt", cnt["vrt"])
                bi = VE.tensor_tensor(s_lgt[:, tc * 8:(tc + 1) * 8],
                                      s_lgt[:, tc * 8:(tc + 1) * 8],
                                      s_lg2[:, tc * 8:(tc + 1) * 8], ALU.add) if VE else None
                inc(bi, "vrt")
            for tc in range(4):
                blk = s_lgt[:, tc * 8:(tc + 1) * 8]
                wt(VE, "vrt", cnt["vrt"])
                bi = VE.reduce_max(s_m1[:, :], blk, axis=AX.X) if VE else None
                inc(bi, "vrt")
                wt(VE, "vrt", cnt["vrt"])
                bi = VE.tensor_scalar(s_lg2[:, tc * 8:(tc + 1) * 8], blk,
                                      s_m1[:, :], None, ALU.is_ge) if VE else None
                inc(bi, "vrt")
                wt(VE, "vrt", cnt["vrt"])
                bi = VE.scalar_tensor_tensor(s_wtk[:, tc * 8:(tc + 1) * 8],
                                             s_lg2[:, tc * 8:(tc + 1) * 8],
                                             -1e30, blk, ALU.mult, ALU.add) if VE else None
                inc(bi, "vrt")
                wt(VE, "vrt", cnt["vrt"])
                bi = VE.reduce_max(s_m2[:, :], s_wtk[:, tc * 8:(tc + 1) * 8],
                                   axis=AX.X) if VE else None
                inc(bi, "vrt")
                wt(VE, "vrt", cnt["vrt"])
                bi = VE.tensor_tensor(s_dd[:, :], s_m2[:, :], s_m1[:, :],
                                      ALU.subtract) if VE else None
                inc(bi, "vrt")
                wt(SC, "vrt", cnt["vrt"])
                bi = SC.activation(s_ee[:, :], s_dd[:, :], AF.Exp) if SC else None
                inc(bi, "sp")
                wt(VE, "sp", cnt["sp"])
                bi = VE.tensor_scalar_add(s_p1[:, :], s_ee[:, :], 1.0) if VE else None
                inc(bi, "vrt")
                wt(VE, "vrt", cnt["vrt"])
                bi = VE.reciprocal(s_p1[:, :], s_p1[:, :]) if VE else None
                inc(bi, "vrt")
                wt(VE, "vrt", cnt["vrt"])
                bi = VE.tensor_tensor(s_p2[:, :], s_p1[:, :], s_ee[:, :],
                                      ALU.mult) if VE else None
                inc(bi, "vrt")
                wt(VE, "vrt", cnt["vrt"])
                bi = VE.tensor_scalar(s_lg2[:, tc * 8:(tc + 1) * 8],
                                      s_wtk[:, tc * 8:(tc + 1) * 8],
                                      s_m2[:, :], s_p2[:, :], ALU.is_ge,
                                      ALU.mult) if VE else None
                inc(bi, "vrt")
                wt(VE, "vrt", cnt["vrt"])
                bi = VE.tensor_scalar(s_wtk[:, tc * 8:(tc + 1) * 8], blk,
                                      s_m1[:, :], s_p1[:, :], ALU.is_ge,
                                      ALU.mult) if VE else None
                inc(bi, "vrt")
                wt(VE, "vrt", cnt["vrt"])
                bi = VE.tensor_tensor(s_wtk[:, tc * 8:(tc + 1) * 8],
                                      s_wtk[:, tc * 8:(tc + 1) * 8],
                                      s_lg2[:, tc * 8:(tc + 1) * 8], ALU.add) if VE else None
                inc(bi, "vrt")
            cap_wtk = cnt["vrt"]
            if tg == 0:
                do_tap("wt", s_wtk[:, :], "vrt", cap_wtk)
                do_tap("lg", s_lgt[:, :], "vrt", cap_wtk)


            wt(TE, "vrt", cap_wtk)
            for tc in range(4):
                bi = TE.transpose(PS[5][0:8, tc * 128:(tc + 1) * 128],
                                  s_wtk[:, tc * 8:(tc + 1) * 8],
                                  k_id[:, :]) if TE else None
                inc(bi, "trt")
            wt(VE, "trt", cnt["trt"])
            for tc in range(4):
                bi = VE.tensor_copy(s_wfm8[:, tc * 128:(tc + 1) * 128],
                                    PS[5][0:8, tc * 128:(tc + 1) * 128]) if VE else None
                inc(bi, "vrt")
            cap_wfm8 = cnt["vrt"]

            wt(VE, "vrt", cap_wfm8)
            bi = VE.tensor_scalar(s_mrow[:, :], s_wfm8[:, :], 0.0,
                                  None, ALU.is_gt) if VE else None
            inc(bi, "vrt")
            if tg == 0:
                do_tap("msk", s_mrow[:, :], "vrt", cnt["vrt"])
            wt(VE, "vrt", cnt["vrt"])
            bi = VE.tensor_tensor_scan(
                s_rm8[:, :],
                k_ones[0:8, 0:1].broadcast_to([8, GT]),
                s_mrow[:, :], 0.0, ALU.mult, ALU.add) if VE else None
            inc(bi, "vrt")
            if tg == 0:
                do_tap("rmraw", s_rm8[:, :], "vrt", cnt["vrt"])
            wt(VE, "vrt", cnt["vrt"])
            bi = VE.tensor_tensor(s_rm8[:, :], s_rm8[:, :],
                                  s_mrow[:, :], ALU.mult) if VE else None
            inc(bi, "vrt")
            wt(VE, "vrt", cnt["vrt"])
            bi = VE.tensor_scalar_add(s_rm8[:, :], s_rm8[:, :], -1.0) if VE else None
            inc(bi, "vrt")
            cap_rm = cnt["vrt"]
            if tg == 0:
                do_tap("rm8", s_rm8[:, :], "vrt", cap_rm)
            wt(TE, "vrt", cap_rm)
            for tc in range(4):
                bi = TE.transpose(PS[6][:, 64 + tc * 8:64 + (tc + 1) * 8],
                                  s_rm8[:, tc * 128:(tc + 1) * 128],
                                  k_id[0:8, 0:8]) if TE else None
                inc(bi, "trt")
            wt(VE, "trt", cnt["trt"])
            bi = VE.tensor_copy(s_rmt[:, :], PS[6][:, 64:96]) if VE else None
            inc(bi, "vrt")
            cap_rmt = cnt["vrt"]
            if tg == 0:
                do_tap("rmt", s_rmt[:, :], "vrt", cap_rmt)

            for e in range(NE):
                wt(VE, "vrt", cap_rmt)
                wt(VE, "trt", cnt["trt"])
                for tc in range(4):
                    bi = VE.tensor_scalar(s_oh[:, tc * 192:(tc + 1) * 192],
                                          k_iota[:, :],
                                          s_rmt[:, tc * 8 + e:tc * 8 + e + 1],
                                          None, ALU.is_equal) if VE else None
                    inc(bi, "vg")
                cap_oh = cnt["vg"]
                if tg == 0 and e == 0:
                    do_tap("oh", s_oh[:, :], "vg", cap_oh)
                wt(TE, "vg", cap_oh)
                wt(TE, "vfo", cnt["vfo"])
                for sh in range(2):
                    if sh == 1:
                        wt(TE, "vg", cnt["vg"])
                    for tc in range(4):
                        bi = TE.transpose(
                            PS[6][0:96, tc * 128:(tc + 1) * 128],
                            s_oh[:, tc * 192 + sh * 96:tc * 192 + sh * 96 + 96],
                            k_id[:, :]) if TE else None
                        inc(bi, "tp")
                    wt(VE, "tp", cnt["tp"])
                    bi = VE.tensor_copy(s_ohT[0:96, sh * 512:(sh + 1) * 512],
                                        PS[6][0:96, 0:512]) if VE else None
                    inc(bi, "vg")
                for tc in range(4):
                    bi = TE.matmul(PS[7][0:1, 0:192],
                                   s_wtk[:, tc * 8 + e:tc * 8 + e + 1],
                                   s_oh[:, tc * 192:(tc + 1) * 192],
                                   start=(tc == 0), stop=(tc == 3)) if TE else None
                    if tc == 3:
                        inc(bi, "trt")
                wt(VE, "trt", cnt["trt"])
                bi = VE.tensor_copy(s_mrow[0:1, 0:192], PS[7][0:1, 0:192]) if VE else None
                inc(bi, "vg")
                wt(TE, "vg", cnt["vg"])
                bi = TE.matmul(PS[7][:, 256:448], k_ones1[:, :], s_mrow[0:1, 0:192],
                               start=True, stop=True) if TE else None
                inc(bi, "trt")
                wt(VE, "trt", cnt["trt"])
                bi = VE.tensor_copy(s_wb[:, :], PS[7][:, 256:448]) if VE else None
                inc(bi, "vg")
                cap_wb = cnt["vg"]
                if tg == 0 and e == 0:
                    do_tap("wb", s_wb[:, :], "vg", cap_wb)
                wt(TE, "vg", cap_wb)
                for m in range(NK):
                    bank = PS[m // 2]
                    co = (m % 2) * 256
                    for tc in range(4):
                        bi = TE.matmul(bank[:, co:co + 192],
                                       s_ln[:, tc * 2048 + m * 128:tc * 2048 + (m + 1) * 128],
                                       s_oh[:, tc * 192:(tc + 1) * 192],
                                       start=(tc == 0), stop=(tc == 3)) if TE else None
                    inc(bi, "trt")
                cap_gmm = cnt["trt"]
                wt(VE, "trt", cap_gmm)
                for m in range(NK):
                    bank = PS[m // 2]
                    co = (m % 2) * 256
                    bi = VE.tensor_copy(s_lnc[:, m * 192:(m + 1) * 192],
                                        bank[:, co:co + 192]) if VE else None
                    inc(bi, "vg")
                cap_lnc = cnt["vg"]
                if tg == 0 and e == 0:
                    do_tap("lnc", s_lnc[:, :], "vg", cap_lnc)
                for mh in range(2):
                    for k in range(NK):
                        si, soff = strip_take()
                        wt(TE, WLD[si % NSLOT], 16 * (si // NSLOT + 1))
                        if k == 0:
                            wt(TE, "vg", cap_lnc if mh == 0 else cnt["vg"])
                        last = None
                        for m8 in range(8):
                            last = TE.matmul(PS[m8][:, 0:192],
                                             r_w[:, soff + m8 * 128:soff + (m8 + 1) * 128],
                                             s_lnc[:, k * 192:(k + 1) * 192],
                                             start=(k == 0), stop=(k == NK - 1)) if TE else None
                        inc(last, "sfree")
                        rec_ev(f"sd:{si}", "sfree")
                    cap_fc = cnt["sfree"]
                    wt(VE, "sfree", cap_fc)
                    for m8 in range(8):
                        m = mh * 8 + m8
                        bank = PS[m8]
                        bi = VE.tensor_copy(s_sq2[:, 0:192], bank[:, 0:192]) if VE else None
                        inc(bi, "vp")
                        wt(VE, "vp", cnt["vp"])
                        bi = VE.scalar_tensor_tensor(s_gc[:, m * 192:(m + 1) * 192],
                                                     bank[:, 0:192], 0.0,
                                                     s_sq2[:, 0:192],
                                                     ALU.max, ALU.mult) if VE else None
                        inc(bi, "vp")
                        wt(VE, "vp", cnt["vp"])
                        bi = VE.tensor_tensor(s_gc[:, m * 192:(m + 1) * 192],
                                              s_gc[:, m * 192:(m + 1) * 192],
                                              s_wb[:, :], ALU.mult) if VE else None
                        inc(bi, "vg")
                cap_g = cnt["vg"]
                if tg == 0 and e == 0:
                    do_tap("gc", s_gc[:, :], "vg", cap_g)
                for k2 in range(NK):
                    si, soff = strip_take()
                    wt(TE, WLD[si % NSLOT], 16 * (si // NSLOT + 1))
                    if k2 == 0:
                        wt(TE, "vg", cap_g)
                    last = None
                    for sh in range(2):
                        for ns in range(4):
                            last = TE.matmul(
                                PS[sh * 4 + ns][0:96, :],
                                s_gc[:, k2 * 192 + sh * 96:k2 * 192 + sh * 96 + 96],
                                r_w[:, soff + ns * 512:soff + (ns + 1) * 512],
                                start=(k2 == 0), stop=(k2 == NK - 1)) if TE else None
                    inc(last, "sfree")
                    rec_ev(f"sd:{si}", "sfree")
                cap_pj = cnt["sfree"]
                wt(VE, "sfree", cap_pj)
                for sh in range(2):
                    for ns in range(4):
                        bi = VE.tensor_copy(
                            s_ytm[0:96, sh * 2048 + ns * 512:sh * 2048 + (ns + 1) * 512],
                            PS[sh * 4 + ns][0:96, :]) if VE else None
                        inc(bi, "vg")
                cap_y = cnt["vg"]
                if tg == 0 and e == 0:
                    do_tap("ytm", s_ytm[0:96, :], "vg", cap_y)
                for mh in range(2):
                    wt(TE, "vg", cap_y)
                    if mh == 1:
                        wt(TE, "vfo", cnt["vfo"])
                    for m in range(8):
                        mg = mh * 8 + m
                        bi = TE.matmul(PS[m][:, :],
                                       s_ytm[0:96, mg * 128:(mg + 1) * 128],
                                       s_ohT[0:96, 0:512],
                                       start=True, stop=False) if TE else None
                        bi = TE.matmul(PS[m][:, :],
                                       s_ytm[0:96, 2048 + mg * 128:2048 + (mg + 1) * 128],
                                       s_ohT[0:96, 512:1024],
                                       start=False, stop=True) if TE else None
                        inc(bi, "trt")
                    wt(VE, "trt", cnt["trt"])
                    for m in range(8):
                        mg = mh * 8 + m
                        bi = VE.tensor_tensor(s_fo[:, mg * GT:(mg + 1) * GT],
                                              s_fo[:, mg * GT:(mg + 1) * GT],
                                              PS[m][:, :], ALU.add) if VE else None
                        inc(bi, "vfo")
            caps[("c_vfo", tg)] = cnt["vfo"]
            if tg == 0:
                do_tap("fo", s_fo[:, :], "vfo", cnt["vfo"])
            caps[("c_sfree", tg)] = cnt["sfree"]
            caps[("c_trt", tg)] = cnt["trt"]
            wt(GP, "vfo", cnt["vfo"])
            for m in range(NK):
                bi = GP.dma_start(out=out[m * 128:(m + 1) * 128, tg * GT:(tg + 1) * GT],
                                  in_=s_fo[:, m * GT:(m + 1) * GT]) if GP else None
                inc(bi, "outd", 16)
            caps[("c_outd", tg)] = cnt["outd"]

        wt(GP, "outd", cnt["outd"])
        return cnt

    emit("pre", None)
    state["rec"] = False

    with nc.Block() as block:
        block.sync(lambda e: emit("sync", e))
        block.tensor(lambda e: emit("tensor", e))
        block.vector(lambda e: emit("vector", e))
        block.scalar(lambda e: emit("scalar", e))
        block.gpsimd(lambda e: emit("gpsimd", e))

    return nc


def prep_core_inputs(ins, c):
    g, j = c // 4, c % 4
    u = np.asarray(ins["L0_attn_u"][g])
    W = np.asarray(ins["L0_attn_in_proj_w"])
    Wout = np.asarray(ins["L0_attn_out_proj_w"])
    zr = W[j * 1024:(j + 1) * 1024]
    xr = W[4096 + j * 1024:4096 + (j + 1) * 1024]
    fm = np.concatenate([W[8192:8320], W[8320:8448],
                         W[8448 + 16 * j:8448 + 16 * (j + 1)]], 0)
    fmT = np.zeros((2048, 512), np.float32)
    fmT[:, 0:272] = fm.T
    cb = np.zeros((128, 512), np.float32)
    cb[:, 0:16] = np.asarray(ins["L0_attn_ln_w"]).reshape(16, 128).T
    cb[:, 16:32] = np.asarray(ins["L0_attn_ln_b"]).reshape(16, 128).T
    cb[:, 32] = np.asarray(ins["L0_attn_bnorm_w"])
    cb[:, 33] = np.asarray(ins["L0_attn_cnorm_w"])
    cb[0:16, 34] = np.asarray(ins["L0_attn_dt_bias"][16 * j:16 * (j + 1)])
    cb[0:16, 35] = np.asarray(ins["L0_attn_A_log"][16 * j:16 * (j + 1)])
    cb[:, 128:256] = (np.asarray(ins["L0_ffn_w_router"]).reshape(16, 128, 8)
                      .transpose(1, 0, 2).reshape(128, 128))
    dv = np.asarray(ins["L0_attn_D"][16 * j:16 * (j + 1)])
    return {
        "uT": np.ascontiguousarray(u.T),
        "uq": np.ascontiguousarray(u[j * 1024:(j + 1) * 1024].T),
        "w_tmT": np.ascontiguousarray(np.concatenate([zr, xr], 0).T),
        "w_fmT": fmT,
        "w_outT": np.ascontiguousarray(Wout.T[j * 1024:(j + 1) * 1024]),
        "w_fc": np.ascontiguousarray(ins["L0_ffn_w_fc"]),
        "w_pj": np.ascontiguousarray(ins["L0_ffn_w_proj"]),
        "cblk": cb,
        "dfull": np.ascontiguousarray(
            np.broadcast_to(np.repeat(dv, 64)[None, :], (128, 1024))),
    }


def assemble(results):
    out = np.empty((2, 4096, 2048), dtype=np.float32)
    for c in range(8):
        g, j = c // 4, c % 4
        out[g, j * 1024:(j + 1) * 1024, :] = results[c]["out"].T
    return out


def kernel(**inputs):
    nc = build()
    in_maps = [prep_core_inputs(inputs, c) for c in range(8)]
    res = run_bass_kernel_spmd(nc, in_maps, core_ids=list(range(8)))
    return assemble(res.results)


# version 10 (final)
# speedup vs baseline: 1.3057x; 1.0072x over previous
import numpy as np
import concourse.bass as bass
import concourse.mybir as mybir
from concourse.bass_utils import run_bass_kernel_spmd

F32 = mybir.dt.float32
AF = mybir.ActivationFunctionType
ALU = mybir.AluOpType
AX = mybir.AxisListType

D = 2048
NK = 16
SEQ = 4096
NG = 8
GT = 512
CH = 128
HL = 16
P = 64
HP = HL * P
DS = 128
EPS = 1e-5
NSLOT = 3
NE = 8

SEMS = ["uld", "uld1", "wld", "wld1", "wld2", "fmld", "fmld1", "cst", "gcst", "vcst", "scst", "csq", "tsq",
        "tstat", "vp", "sp", "tp", "gp", "vun", "ufree", "tfm", "sfree",
        "fmfree", "vzx", "tyd", "vssd", "typ", "vyf", "vpo", "post", "cc",
        "cld", "vx1", "sq2", "tsq2", "trt", "vrt", "grt", "vg", "vfo", "outd", "tapd"]


def build(taps=(), stop_after=None):
    nc = bass.Bass("TRN2", target_bir_lowering=False, num_devices=8)

    uT = nc.dram_tensor("uT", [D, SEQ], F32, kind="ExternalInput").ap()
    uq = nc.dram_tensor("uq", [D, 1024], F32, kind="ExternalInput").ap()
    w_tmT = nc.dram_tensor("w_tmT", [D, 2048], F32, kind="ExternalInput").ap()
    w_fmT = nc.dram_tensor("w_fmT", [D, 512], F32, kind="ExternalInput").ap()
    w_outT = nc.dram_tensor("w_outT", [HP, D], F32, kind="ExternalInput").ap()
    w_fc = nc.dram_tensor("w_fc", [NE, D, D], F32, kind="ExternalInput").ap()
    w_pj = nc.dram_tensor("w_pj", [NE, D, D], F32, kind="ExternalInput").ap()
    cblk = nc.dram_tensor("cblk", [128, 512], F32, kind="ExternalInput").ap()
    dfull = nc.dram_tensor("dfull", [128, HP], F32, kind="ExternalInput").ap()

    part_out = [nc.dram_tensor(f"part_out{h}", [4, D, 512], F32) for h in range(2)]
    rs_out = [nc.dram_tensor(f"rs_out{h}", [D, 512], F32) for h in range(2)]
    out = nc.dram_tensor("out", [D, 1024], F32, kind="ExternalOutput").ap()

    tap_shapes = {"un": [128, NK * GT], "zx": [128, 2 * 2048], "bn": [128, GT],
                  "cn": [128, GT], "da": [HL, GT], "y0": [128, HP],
                  "yf": [128, 8 * GT], "po": [128, 8 * GT], "x1": [128, NK * GT],
                  "ln": [128, NK * GT], "lg": [128, 32], "wt": [128, 32],
                  "sst": [128, HP], "u0": [128, NK * GT], "rows": [1, 8 * GT], "pbc": [128, 2 * GT], "all": [128, 51200], "lnw": [128, 2 * NK], "braw": [128, GT], "brow": [1, GT], "rm8": [8, 512], "rmt": [128, 32], "msk": [8, 512], "rmraw": [8, 512], "oh": [128, 768], "wb": [128, 192], "lnc": [128, 3072], "gc": [128, 3072], "ytm": [96, 4096], "fo": [128, 16 * 512]}
    tap_t = {t: nc.dram_tensor("tap_" + t, tap_shapes[t], F32,
                               kind="ExternalOutput").ap() for t in taps}

    def sb(name, shape, off):
        assert off + int(np.prod(shape[1:])) * 4 <= 224 * 1024, (name, off)
        return nc.alloc_sbuf_tensor_at(name, shape, F32, offset=off)

    k_id = sb("k_id", [128, 128], 16512)
    k_tril = sb("k_tril", [128, 128], 17024)
    k_ones = sb("k_ones", [128, 1], 17536)
    k_ones1 = sb("k_ones1", [1, 128], 17568)
    c_blkt = sb("c_blk", [128, 512], 18080)
    c_blk = c_blkt[:, :]
    c_lnw = c_blkt[:, 0:16]
    c_lnb = c_blkt[:, 16:32]
    c_bnw = c_blkt[:, 32:33]
    c_cnw = c_blkt[:, 33:34]
    c_dtb = c_blkt[0:HL, 34:35]
    c_alog = c_blkt[0:HL, 35:36]
    c_wrt = c_blkt[:, 128:256]
    c_aneg = sb("c_aneg", [HL, 1], 20160)
    c_dfull = sb("c_dfull", [128, HP], 20224)
    S_state = sb("S_state", [128, HP], 24320)
    r_w = sb("r_w", [128, NSLOT * 2048], 28416)
    s_rows = sb("s_rows", [1, 8 * GT], 52992)

    A0 = 68 * 1024
    s_u = sb("s_u", [128, NK * GT], A0)
    s_un = sb("s_un", [128, NK * GT], A0 + 32 * 1024)
    s_zx = sb("s_zx", [128, 2 * 2048], A0 + 64 * 1024)
    s_B = sb("s_B", [128, GT], A0 + 80 * 1024)
    s_C = sb("s_C", [128, GT], A0 + 82 * 1024)
    s_sq = sb("s_sq", [128, 2 * GT], A0 + 84 * 1024)
    s_dA = sb("s_dA", [HL, GT], A0 + 88 * 1024)
    s_dts = sb("s_dts", [HL, GT], A0 + 90 * 1024)
    s_Acs = sb("s_Acs", [HL, CH], A0 + 92 * 1024)
    s_AcsT = sb("s_AcsT", [128, HL], A0 + 92 * 1024 + 512)
    s_d1 = sb("s_d1", [128, HL], A0 + 92 * 1024 + 576)
    s_d2 = sb("s_d2", [128, HL], A0 + 92 * 1024 + 640)
    s_f = sb("s_f", [128, HL], A0 + 92 * 1024 + 704)
    s_e1 = sb("s_e1", [128, HL], A0 + 92 * 1024 + 768)
    s_E2 = sb("s_E2", [128, HL], A0 + 92 * 1024 + 832)
    s_ds = sb("s_ds", [128, HL], A0 + 92 * 1024 + 896)
    s_Dvb = sb("s_Dvb", [128, HL], A0 + 92 * 1024 + 960)
    s_mr = sb("s_mr", [1, 32], A0 + 93 * 1024)
    s_mc = sb("s_mc", [HL, 1], A0 + 93 * 1024 + 128)
    s_scm = sb("s_scm", [128, CH], A0 + 94 * 1024)
    s_Btm = sb("s_Btm", [128, DS], A0 + 94 * 1024 + 512)
    s_Mall = sb("s_Mall", [128, HL * CH], A0 + 95 * 1024)
    s_xd = sb("s_xd", [128, HP], A0 + 103 * 1024)
    s_y = sb("s_y", [128, HP], A0 + 107 * 1024)
    s_g1 = sb("s_g1", [128, HP], A0 + 111 * 1024)
    s_sz = sb("s_sz", [128, HP], A0 + 115 * 1024)
    s_yfm = sb("s_yfm", [128, 8 * GT], A0 + 119 * 1024)
    s_po = sb("s_po", [128, 8 * GT], A0 + 135 * 1024)
    s_wfm = sb("s_wfm", [128, 2 * 512], A0 + 151 * 1024)

    s_fo = sb("s_fo", [128, NK * GT], A0)
    s_ln = sb("s_ln", [128, NK * GT], A0 + 32 * 1024)
    s_uq = sb("s_uq", [128, NK * GT], A0 + 64 * 1024)
    s_g = sb("s_g", [128, NK * GT], A0 + 96 * 1024)
    s_sq2 = sb("s_sq2", [128, 2 * GT], A0 + 128 * 1024)
    s_lg = sb("s_lg", [8, GT], A0 + 132 * 1024)
    s_rm8 = sb("s_rm8", [8, GT], A0 + 134 * 1024)
    s_lgt = sb("s_lgt", [128, 32], A0 + 136 * 1024)
    s_wtk = sb("s_wtk", [128, 32], A0 + 136 * 1024 + 128)
    s_lg2 = sb("s_lg2", [128, 32], A0 + 136 * 1024 + 256)
    s_m1 = sb("s_m1", [128, 1], A0 + 136 * 1024 + 384)
    s_m2 = sb("s_m2", [128, 1], A0 + 136 * 1024 + 416)
    s_p1 = sb("s_p1", [128, 1], A0 + 136 * 1024 + 448)
    s_p2 = sb("s_p2", [128, 1], A0 + 136 * 1024 + 480)
    s_dd = sb("s_dd", [128, 1], A0 + 136 * 1024 + 512)
    s_ee = sb("s_ee", [128, 1], A0 + 136 * 1024 + 544)
    s_wfm8 = sb("s_wfm8", [8, GT], A0 + 137 * 1024)
    s_statc = sb("s_statc", [128, 8], A0 + 136 * 1024 + 576)
    s_rmt = sb("s_rmt", [128, 32], A0 + 136 * 1024 + 608)
    s_mrow = sb("s_mrow", [8, GT], A0 + 139 * 1024)
    k_iota = sb("k_iota", [128, 192], A0 + 141 * 1024)
    s_oh = sb("s_oh", [128, 4 * 192], A0 + 142 * 1024)
    s_ohT = sb("s_ohT", [128, 2 * 512], A0 + 146 * 1024)
    s_wb = sb("s_wb", [128, 192], A0 + 150 * 1024)
    s_lnc = sb("s_lnc", [128, 16 * 192], A0 + 64 * 1024)
    s_gc = sb("s_gc", [128, 16 * 192], A0 + 76 * 1024)
    s_ytm = sb("s_ytm", [128, 2 * 2048], A0 + 96 * 1024)
    c_cswb = sb("c_cswb", [128, 8], 69408)

    PS = [nc.alloc_psum_tensor(f"ps{i}", [128, 512], F32) for i in range(8)]

    sems = {n: nc.alloc_semaphore("sem_" + n) for n in SEMS}
    tables = {}
    state = {"rec": True}

    strip_sched = []
    for gi in range(NG):
        for ps in range(2):
            for k in range(NK):
                strip_sched.append(("tm", k))
        for half in range(2):
            for e in range(8):
                strip_sched.append(("out", e))
    for tg in range(2):
        for e in range(NE):
            for mh in range(2):
                for k in range(NK):
                    strip_sched.append(("fc", e, k, mh))
            for k in range(NK):
                strip_sched.append(("pj", e, k))
    N_STRIPS = len(strip_sched)

    def strip_src(ent):
        if ent[0] == "tm":
            return w_tmT[ent[1] * 128:(ent[1] + 1) * 128, :]
        if ent[0] == "out":
            return w_outT[ent[1] * 128:(ent[1] + 1) * 128, :]
        if ent[0] == "fc":
            return w_fc[ent[1], ent[2] * 128:(ent[2] + 1) * 128,
                        ent[3] * 1024:(ent[3] + 1) * 1024]
        return w_pj[ent[1], ent[2] * 128:(ent[2] + 1) * 128, :]

    def emit(eng, E):
        cnt = {n: 0 for n in SEMS}
        rec = state["rec"]
        TE = E if eng == "tensor" else None
        VE = E if eng == "vector" else None
        SC = E if eng == "scalar" else None
        GP = E if eng == "gpsimd" else None
        SY = E if eng == "sync" else None
        R = s_rows

        def inc(bi, ch, n=1):
            cnt[ch] += n
            if bi is not None:
                bi.then_inc(sems[ch], n)
            return cnt[ch]

        def wt(Eng, ch, val):
            if Eng is not None and val is not None and val > 0:
                Eng.wait_ge(sems[ch], val)

        def rec_ev(key, ch):
            if rec:
                tables[key] = (ch, cnt[ch])

        def wt_ev(Eng, key):
            ch, val = tables.get(key, (None, None))
            if Eng is not None and ch is not None and val > 0:
                Eng.wait_ge(sems[ch], val)

        def do_tap(name, buf, ch=None, val=None):
            if name not in tap_t:
                return
            if ch is not None:
                wt(GP, ch, val)
            bi = GP.dma_start(out=tap_t[name], in_=buf) if GP else None
            inc(bi, "tapd", 16)
            for Eng in (TE, VE, SC, SY, GP):
                wt(Eng, "tapd", cnt["tapd"])

        ULD = ["uld", "uld1"]
        WLD = ["wld", "wld1", "wld2"]
        FMLD = ["fmld", "fmld1"]

        strips_issued = [0]
        sidx = [0]

        def strips_issue_upto(n):
            while strips_issued[0] < min(n, N_STRIPS):
                i = strips_issued[0]
                if i >= NSLOT:
                    wt_ev(SY, f"sd:{i - NSLOT}")
                src = strip_src(strip_sched[i])
                bi = SY.dma_start(
                    out=r_w[:, (i % NSLOT) * 2048:(i % NSLOT) * 2048 + src.shape[1]],
                    in_=src) if SY else None
                inc(bi, WLD[i % NSLOT], 16)
                strips_issued[0] += 1

        def strip_take():
            i = sidx[0]
            sidx[0] += 1
            strips_issue_upto(i + NSLOT + 1)
            return i, (i % NSLOT) * 2048

        bi = SC.dma_start(out=c_blk, in_=cblk) if SC else None
        inc(bi, "cst", 16)
        bi = SC.dma_start(out=c_dfull[:, :], in_=dfull) if SC else None
        ncst = inc(bi, "cst", 16)

        bi = GP.memset(k_id[:, :], 1.0) if GP else None
        inc(bi, "gp")
        wt(GP, "gp", cnt["gp"])
        bi = GP.affine_select(out=k_id[:, :], in_=k_id[:, :],
                              compare_op=ALU.is_equal, fill=0.0, base=0,
                              pattern=[[-1, 128]], channel_multiplier=1) if GP else None
        inc(bi, "gp")
        bi = GP.memset(k_tril[:, :], 1.0) if GP else None
        inc(bi, "gp")
        wt(GP, "gp", cnt["gp"])
        bi = GP.affine_select(out=k_tril[:, :], in_=k_tril[:, :],
                              compare_op=ALU.is_ge, fill=0.0, base=0,
                              pattern=[[1, 128]], channel_multiplier=-1) if GP else None
        inc(bi, "gp")
        bi = GP.memset(k_ones[:, :], 1.0) if GP else None
        inc(bi, "gp")
        bi = GP.memset(k_ones1[:, :], 1.0) if GP else None
        inc(bi, "gp")
        bi = GP.memset(S_state[:, :], 0.0) if GP else None
        inc(bi, "gcst")

        wt(SC, "cst", ncst)
        bi = SC.activation(c_aneg[:, :], c_alog, AF.Exp) if SC else None
        inc(bi, "scst")
        wt(VE, "scst", 1)
        bi = VE.tensor_scalar_mul(c_aneg[:, :], c_aneg[:, :], -1.0) if VE else None
        inc(bi, "vcst")

        wt(TE, "gcst", 1)
        wt(TE, "cst", ncst)
        wt(VE, "gcst", 1)
        wt(VE, "cst", ncst)
        wt(SC, "gcst", 1)

        for k in range(NK):
            bi = TE.matmul(PS[4][0:1, 0:8], k_ones[:, :], c_wrt[:, k * 8:(k + 1) * 8],
                           start=(k == 0), stop=(k == NK - 1)) if TE else None
            if k == NK - 1:
                inc(bi, "tp")
        wt(VE, "tp", cnt["tp"])
        bi = VE.tensor_copy(R[0:1, 7 * GT:7 * GT + 8], PS[4][0:1, 0:8]) if VE else None
        inc(bi, "vp")
        wt(TE, "vp", cnt["vp"])
        bi = TE.matmul(PS[4][:, 0:8], k_ones1[:, :], R[0:1, 7 * GT:7 * GT + 8],
                       start=True, stop=True) if TE else None
        inc(bi, "tp")
        wt(VE, "tp", cnt["tp"])
        bi = VE.tensor_copy(c_cswb[:, :], PS[4][:, 0:8]) if VE else None
        inc(bi, "vp")

        for k in range(NK):
            bi = SC.dma_start(out=s_u[:, k * GT:(k + 1) * GT],
                              in_=uT[k * 128:(k + 1) * 128, 0:GT]) if SC else None
            inc(bi, "uld", 16)
        for kk in range(2):
            bi = SC.dma_start(out=s_wfm[:, (kk % 2) * 512:(kk % 2) * 512 + 512],
                              in_=w_fmT[kk * 128:(kk + 1) * 128, :]) if SC else None
            inc(bi, FMLD[kk % 2], 16)
        strips_issue_upto(NSLOT)

        caps = {}

        for gi in range(NG):
            if gi > 0:
                wt(TE, "vpo", caps[("vpo", gi - 1)])
                wt(TE, "sp", caps[("sp", gi - 1)])

            for k in range(NK):
                if k == 0:
                    wt(TE, ULD[gi % 2], 256 * (gi // 2 + 1))
                bi = TE.matmul(PS[0][0:1, :], k_ones[:, :],
                               s_u[:, k * GT:(k + 1) * GT],
                               start=(k == 0), stop=(k == NK - 1)) if TE else None
                if k == NK - 1:
                    inc(bi, "tstat")
                if k == 0:
                    wt(SC, ULD[gi % 2], 256 * (gi // 2 + 1))
                    wt(SC, "vun", 16 * gi)
                wt(SC, "tsq", 16 * gi + k - 1)
                sbi = SC.activation(s_sq[:, (k % 2) * GT:(k % 2 + 1) * GT],
                                    s_u[:, k * GT:(k + 1) * GT],
                                    AF.Square) if SC else None
                inc(sbi, "csq")
                wt(TE, "csq", 16 * gi + k + 1)
                bi = TE.matmul(PS[1][0:1, :], k_ones[:, :],
                               s_sq[:, (k % 2) * GT:(k % 2 + 1) * GT],
                               start=(k == 0), stop=(k == NK - 1)) if TE else None
                inc(bi, "tsq")

            if gi > 0:
                wt(VE, "sfree", caps[("sfree", gi - 1)])
            wt(VE, "tstat", cnt["tstat"])
            wt(VE, "tsq", cnt["tsq"])
            bi = VE.tensor_scalar_mul(R[0:1, 0 * GT:1 * GT], PS[0][0:1, :], 1.0 / D) if VE else None
            inc(bi, "vp")
            bi = VE.tensor_scalar_mul(R[0:1, 1 * GT:2 * GT], PS[1][0:1, :], 1.0 / D) if VE else None
            inc(bi, "vp")
            wt(VE, "vp", cnt["vp"])
            bi = VE.tensor_tensor(R[0:1, 2 * GT:3 * GT], R[0:1, 0 * GT:1 * GT], R[0:1, 0 * GT:1 * GT], ALU.mult) if VE else None
            inc(bi, "vp")
            wt(VE, "vp", cnt["vp"])
            bi = VE.tensor_tensor(R[0:1, 1 * GT:2 * GT], R[0:1, 1 * GT:2 * GT], R[0:1, 2 * GT:3 * GT], ALU.subtract) if VE else None
            inc(bi, "vp")
            wt(VE, "vp", cnt["vp"])
            bi = VE.tensor_scalar_add(R[0:1, 2 * GT:3 * GT], R[0:1, 1 * GT:2 * GT], EPS) if VE else None
            inc(bi, "vp")
            wt(SC, "vp", cnt["vp"])
            bi = SC.activation(R[0:1, 3 * GT:4 * GT], R[0:1, 2 * GT:3 * GT], AF.Ln) if SC else None
            inc(bi, "sp")
            wt(SC, "sp", cnt["sp"])
            bi = SC.activation(R[0:1, 3 * GT:4 * GT], R[0:1, 3 * GT:4 * GT], AF.Exp, scale=-0.5) if SC else None
            inc(bi, "sp")
            wt(VE, "sp", cnt["sp"])
            bi = VE.reciprocal(R[0:1, 4 * GT:5 * GT], R[0:1, 2 * GT:3 * GT]) if VE else None
            inc(bi, "vp")
            wt(VE, "vp", cnt["vp"])
            bi = VE.tensor_tensor(R[0:1, 4 * GT:5 * GT], R[0:1, 4 * GT:5 * GT], R[0:1, 1 * GT:2 * GT], ALU.mult) if VE else None
            inc(bi, "vp")
            wt(VE, "vp", cnt["vp"])
            bi = VE.tensor_scalar_add(R[0:1, 4 * GT:5 * GT], R[0:1, 4 * GT:5 * GT], EPS) if VE else None
            inc(bi, "vp")
            wt(SC, "vp", cnt["vp"])
            bi = SC.activation(R[0:1, 4 * GT:5 * GT], R[0:1, 4 * GT:5 * GT], AF.Ln) if SC else None
            inc(bi, "sp")
            wt(SC, "sp", cnt["sp"])
            bi = SC.activation(R[0:1, 4 * GT:5 * GT], R[0:1, 4 * GT:5 * GT], AF.Exp, scale=-0.5) if SC else None
            inc(bi, "sp")
            wt(VE, "sp", cnt["sp"])
            bi = VE.tensor_tensor(R[0:1, 5 * GT:6 * GT], R[0:1, 3 * GT:4 * GT], R[0:1, 4 * GT:5 * GT], ALU.mult) if VE else None
            inc(bi, "vp")
            wt(VE, "vp", cnt["vp"])
            bi = VE.scalar_tensor_tensor(R[0:1, 6 * GT:7 * GT], R[0:1, 0 * GT:1 * GT], -1.0, R[0:1, 5 * GT:6 * GT],
                                         ALU.mult, ALU.mult) if VE else None
            inc(bi, "vp")

            cap_rows = cnt["vp"]
            if gi == 0:
                do_tap("u0", s_u[:, :], "vp", cap_rows)
                do_tap("rows", s_rows[:, :], "vp", cap_rows)

            wt(TE, "vp", cap_rows)
            bi = TE.matmul(PS[2][:, :], k_ones1[:, :], R[0:1, 5 * GT:6 * GT],
                           start=True, stop=True) if TE else None
            inc(bi, "tp")
            bi = TE.matmul(PS[3][:, :], k_ones1[:, :], R[0:1, 6 * GT:7 * GT],
                           start=True, stop=True) if TE else None
            inc(bi, "tp")
            cap_bc = cnt["tp"]

            wt(VE, "tp", cap_bc)
            bi = VE.tensor_copy(s_sq[:, 0:GT], PS[2][:, :]) if VE else None
            inc(bi, "vp")
            bi = VE.tensor_copy(s_sq[:, GT:2 * GT], PS[3][:, :]) if VE else None
            inc(bi, "vp")
            wt(VE, "vp", cnt["vp"])
            for k in range(NK):
                bi = VE.tensor_tensor(s_un[:, k * GT:(k + 1) * GT],
                                      s_u[:, k * GT:(k + 1) * GT],
                                      s_sq[:, 0:GT], ALU.mult) if VE else None
                inc(bi, "vp")
                wt(VE, "vp", cnt["vp"])
                bi = VE.tensor_tensor(s_un[:, k * GT:(k + 1) * GT],
                                      s_un[:, k * GT:(k + 1) * GT],
                                      s_sq[:, GT:2 * GT], ALU.add) if VE else None
                inc(bi, "ufree")
                wt(VE, "ufree", cnt["ufree"])
                bi = VE.tensor_scalar(s_un[:, k * GT:(k + 1) * GT],
                                      s_un[:, k * GT:(k + 1) * GT],
                                      c_blkt[:, k:k + 1], c_blkt[:, 16 + k:17 + k],
                                      ALU.mult, ALU.add) if VE else None
                inc(bi, "vun")
            if gi == 0:
                if "pbc" in tap_t:
                    wt(VE, "vun", 16)
                    bi = VE.tensor_copy(s_sq[:, 0:GT], PS[2][:, :]) if VE else None
                    inc(bi, "vp")
                    bi = VE.tensor_copy(s_sq[:, GT:2 * GT], PS[3][:, :]) if VE else None
                    inc(bi, "vp")
                    do_tap("pbc", s_sq[:, :], "vp", cnt["vp"])
                do_tap("un", s_un[:, :], "vun", 16)
                if "lnw" in tap_t:
                    bi = VE.tensor_copy(s_sq[:, 0:NK], c_lnw) if VE else None
                    inc(bi, "vp")
                    bi = VE.tensor_copy(s_sq[:, NK:2 * NK], c_lnb) if VE else None
                    inc(bi, "vp")
                    do_tap("lnw", s_sq[:, 0:2 * NK], "vp", cnt["vp"])
                if "all" in tap_t:
                    s_all = nc.alloc_sbuf_tensor_at("s_all", [128, 51200], F32,
                                                    offset=17024)
                    do_tap("all", s_all[:, :], "vun", 16)
                if "zx" in tap_t:
                    wt(VE, "vun", 16)
                    bi = VE.tensor_copy(s_zx[:, 0:GT], s_un[:, 8 * GT:9 * GT]) if VE else None
                    inc(bi, "vp")
                    bi = VE.tensor_copy(s_zx[:, GT:2 * GT], s_un[:, 0:GT]) if VE else None
                    inc(bi, "vp")
                    do_tap("zx", s_zx[:, :], "vp", cnt["vp"])
            if stop_after == "a5" and gi == 0:
                return cnt
            if gi + 1 < NG:
                for k in range(NK):
                    wt(SC, "ufree", 16 * gi + k + 1)
                    bi = SC.dma_start(
                        out=s_u[:, k * GT:(k + 1) * GT],
                        in_=uT[k * 128:(k + 1) * 128,
                               (gi + 1) * GT:(gi + 2) * GT]) if SC else None
                    inc(bi, ULD[(gi + 1) % 2], 16)

            for k in range(NK):
                gidx = 16 * gi + k
                wt(TE, FMLD[gidx % 2], 16 * (gidx // 2 + 1))
                wt(TE, "vun", 16 * gi + k + 1)
                slot = gidx % 2
                wf = s_wfm[:, slot * 512:slot * 512 + 512]
                bi = TE.matmul(PS[6][:, :], wf[:, 0:128],
                               s_un[:, k * GT:(k + 1) * GT],
                               start=(k == 0), stop=(k == NK - 1)) if TE else None
                if k == NK - 1:
                    inc(bi, "tfm")
                bi2 = TE.matmul(PS[7][:, :], wf[:, 128:256],
                                s_un[:, k * GT:(k + 1) * GT],
                                start=(k == 0), stop=(k == NK - 1)) if TE else None
                if k == NK - 1:
                    inc(bi2, "tfm")
                bi3 = TE.matmul(PS[0][0:HL, :], wf[:, 256:272],
                                s_un[:, k * GT:(k + 1) * GT],
                                start=(k == 0), stop=(k == NK - 1)) if TE else None
                inc(bi3, "fmfree")
                nxt = gidx + 2
                if nxt < 16 * NG:
                    wt(SC, "fmfree", nxt - 1)
                    bi = SC.dma_start(
                        out=s_wfm[:, (nxt % 2) * 512:(nxt % 2) * 512 + 512],
                        in_=w_fmT[(nxt % 16) * 128:(nxt % 16) * 128 + 128, :]) if SC else None
                    inc(bi, FMLD[nxt % 2], 16)

            wt(VE, "tfm", cnt["tfm"])
            bi = VE.tensor_copy(s_B[:, :], PS[6][:, :]) if VE else None
            inc(bi, "vp")
            bi = VE.tensor_copy(s_C[:, :], PS[7][:, :]) if VE else None
            inc(bi, "vp")
            cap_bc_evac = cnt["vp"]
            if gi == 0:
                do_tap("braw", s_B[:, :], "vp", cap_bc_evac)
            for X, cx in ((s_B, c_bnw), (s_C, c_cnw)):
                wt(SC, "vp", cap_bc_evac)
                sqslot = 0 if X is s_B else GT
                bi = SC.activation(s_sq[:, sqslot:sqslot + GT], X[:, :],
                                   AF.Square) if SC else None
                inc(bi, "sp")
                if gi == 0 and X is s_B:
                    do_tap("brow", R[0:1, 2 * GT:3 * GT], "sp", cnt["sp"])
                wt(TE, "sp", cnt["sp"])
                if X is s_C:
                    wt(TE, "vp", cnt["vp"])
                bi = TE.matmul(PS[2][0:1, :], k_ones[:, :], s_sq[:, sqslot:sqslot + GT],
                               start=True, stop=True) if TE else None
                inc(bi, "tp")
                wt(VE, "tp", cnt["tp"])
                bi = VE.tensor_scalar(R[0:1, 2 * GT:3 * GT], PS[2][0:1, :], 1.0 / DS, EPS,
                                      ALU.mult, ALU.add) if VE else None
                inc(bi, "vp")
                wt(SC, "vp", cnt["vp"])
                bi = SC.activation(R[0:1, 2 * GT:3 * GT], R[0:1, 2 * GT:3 * GT], AF.Ln) if SC else None
                inc(bi, "sp")
                wt(SC, "sp", cnt["sp"])
                bi = SC.activation(R[0:1, 2 * GT:3 * GT], R[0:1, 2 * GT:3 * GT], AF.Exp, scale=-0.5) if SC else None
                inc(bi, "sp")
                wt(TE, "sp", cnt["sp"])
                bi = TE.matmul(PS[3][:, :], k_ones1[:, :], R[0:1, 2 * GT:3 * GT],
                               start=True, stop=True) if TE else None
                inc(bi, "tp")
                wt(VE, "tp", cnt["tp"])
                bi = VE.tensor_tensor(X[:, :], X[:, :], PS[3][:, :], ALU.mult) if VE else None
                inc(bi, "vp")
                wt(VE, "vp", cnt["vp"])
                bi = VE.tensor_scalar_mul(X[:, :], X[:, :], cx) if VE else None
                inc(bi, "vp")
            cap_bcn = cnt["vp"]
            if stop_after == "a7" and gi == 0:
                do_tap("bn", s_B[:, :], "vp", cap_bcn)
                do_tap("cn", s_C[:, :], "vp", cap_bcn)
                return cnt
            wt(SC, "tfm", cnt["tfm"])
            bi = SC.activation(s_dts[:, :], PS[0][0:HL, :], AF.Exp,
                               bias=c_dtb) if SC else None
            inc(bi, "sp")
            wt(VE, "sp", cnt["sp"])
            bi = VE.tensor_scalar_add(s_dts[:, :], s_dts[:, :], 1.0) if VE else None
            inc(bi, "vp")
            wt(SC, "vp", cnt["vp"])
            bi = SC.activation(s_dts[:, :], s_dts[:, :], AF.Ln) if SC else None
            inc(bi, "sp")
            wt(VE, "sp", cnt["sp"])
            wt(VE, "vcst", 1)
            bi = VE.tensor_scalar_mul(s_dA[:, :], s_dts[:, :], c_aneg[:, :]) if VE else None
            inc(bi, "vp")
            cap_da = cnt["vp"]
            cap_sp_a7 = cnt["sp"]
            if stop_after == "a7b" and gi == 0:
                do_tap("bn", s_B[:, :], "vp", cap_da)
                do_tap("cn", s_C[:, :], "vp", cap_da)
                do_tap("da", s_dA[:, :], "vp", cap_da)
                return cnt

            if gi == 0:
                do_tap("bn", s_B[:, :], "vp", cap_bcn)
                do_tap("cn", s_C[:, :], "vp", cap_bcn)
                do_tap("da", s_dA[:, :], "vp", cap_da)

            for ps in range(2):
                wt(TE, "sp", cap_sp_a7)
                wt(TE, "vp", cap_bcn)
                wt(TE, "vssd", caps.get(("vssd_last",), 0))
                wt(TE, "vyf", cnt["vyf"])
                for k in range(NK):
                    si, soff = strip_take()
                    wt(TE, WLD[si % NSLOT], 16 * (si // NSLOT + 1))
                    last = None
                    for tt2 in range(2):
                        tglob = ps * 2 + tt2
                        for ns in range(4):
                            last = TE.matmul(
                                PS[tt2 * 4 + ns][:, :],
                                s_un[:, k * GT + tglob * 128:k * GT + tglob * 128 + 128],
                                r_w[:, soff + ns * 512:soff + (ns + 1) * 512],
                                start=(k == 0), stop=(k == NK - 1)) if TE else None
                    inc(last, "sfree")
                    rec_ev(f"sd:{si}", "sfree")
                cap_zx_mm = cnt["sfree"]
                wt(VE, "sfree", cap_zx_mm)
                for tt2 in range(2):
                    for ns in range(4):
                        bi = VE.tensor_copy(
                            s_zx[:, tt2 * 2048 + ns * 512:tt2 * 2048 + (ns + 1) * 512],
                            PS[tt2 * 4 + ns][:, :]) if VE else None
                        inc(bi, "vzx")
                cap_vzx = cnt["vzx"]
                if gi == 0 and ps == 0:
                    do_tap("zx", s_zx[:, :], "vzx", cap_vzx)

                for tt2 in range(2):
                    cloc = ps * 2 + tt2
                    zoff = tt2 * 2048
                    wt(VE, "vp", cap_da)
                    bi = VE.tensor_tensor_scan(
                        s_Acs[:, :], k_ones[0:HL, 0:1].broadcast_to([HL, CH]),
                        s_dA[:, cloc * CH:(cloc + 1) * CH], 0.0,
                        ALU.mult, ALU.add) if VE else None
                    inc(bi, "vp")
                    wt(VE, "vp", cnt["vp"])
                    bi = VE.tensor_tensor(s_mc[:, :], s_Acs[:, 0:1],
                                          s_Acs[:, CH - 1:CH], ALU.add) if VE else None
                    inc(bi, "vp")
                    wt(VE, "vp", cnt["vp"])
                    bi = VE.tensor_scalar_mul(s_mc[:, :], s_mc[:, :], 0.5) if VE else None
                    inc(bi, "vp")
                    cap_scan = cnt["vp"]

                    wt(TE, "vzx", cap_vzx)
                    wt(TE, "vp", cap_scan)
                    bi = TE.matmul(PS[0][:, 0:128], s_B[:, cloc * 128:(cloc + 1) * 128],
                                   s_C[:, cloc * 128:(cloc + 1) * 128],
                                   start=True, stop=True) if TE else None
                    inc(bi, "tp")
                    bi = TE.transpose(PS[1][:, 0:128],
                                      s_B[:, cloc * 128:(cloc + 1) * 128],
                                      k_id[:, :]) if TE else None
                    inc(bi, "tp")
                    bi = TE.transpose(PS[0][:, 128:128 + HL], s_Acs[:, :],
                                      k_id[0:HL, 0:HL]) if TE else None
                    inc(bi, "tp")
                    bi = TE.transpose(PS[0][0:1, 176:176 + HL], s_mc[:, :],
                                      k_id[0:HL, 0:HL]) if TE else None
                    inc(bi, "tp")
                    bi = TE.transpose(PS[0][0:1, 192:192 + HL],
                                      s_Acs[:, CH - 1:CH],
                                      k_id[0:HL, 0:HL]) if TE else None
                    inc(bi, "tp")
                    cap_t1 = cnt["tp"]
                    wt(VE, "tp", cap_t1)
                    bi = VE.tensor_tensor(s_scm[:, :], PS[0][:, 0:128],
                                          k_tril[:, :], ALU.mult) if VE else None
                    inc(bi, "vp")
                    bi = VE.tensor_copy(s_Btm[:, :], PS[1][:, 0:128]) if VE else None
                    inc(bi, "vp")
                    bi = VE.tensor_copy(s_AcsT[:, :], PS[0][:, 128:128 + HL]) if VE else None
                    inc(bi, "vp")
                    bi = VE.tensor_copy(s_mr[0:1, 0:HL], PS[0][0:1, 176:176 + HL]) if VE else None
                    inc(bi, "vp")
                    bi = VE.tensor_copy(s_mr[0:1, 16:16 + HL], PS[0][0:1, 192:192 + HL]) if VE else None
                    inc(bi, "vp")
                    cap_ev1 = cnt["vp"]
                    wt(TE, "vp", cap_ev1)
                    bi = TE.matmul(PS[0][:, 144:144 + HL], k_ones1[:, :],
                                   s_mr[0:1, 0:HL], start=True, stop=True) if TE else None
                    inc(bi, "tp")
                    bi = TE.matmul(PS[0][:, 160:160 + HL], k_ones1[:, :],
                                   s_mr[0:1, 16:16 + HL], start=True, stop=True) if TE else None
                    inc(bi, "tp")
                    cap_t2 = cnt["tp"]
                    wt(VE, "tp", cap_t2)
                    bi = VE.tensor_tensor(s_d1[:, :], PS[0][:, 144:144 + HL],
                                          s_AcsT[:, :], ALU.subtract) if VE else None
                    inc(bi, "vp")
                    bi = VE.tensor_tensor(s_d2[:, :], PS[0][:, 160:160 + HL],
                                          s_AcsT[:, :], ALU.subtract) if VE else None
                    inc(bi, "vp")
                    cap_d = cnt["vp"]
                    wt(SC, "vp", cap_d)
                    bi = SC.activation(s_f[:, :], s_d1[:, :], AF.Exp) if SC else None
                    inc(bi, "sp")
                    bi = SC.activation(s_e1[:, :], s_d1[:, :], AF.Exp, scale=-1.0) if SC else None
                    inc(bi, "sp")
                    bi = SC.activation(s_E2[:, :], s_AcsT[:, :], AF.Exp) if SC else None
                    inc(bi, "sp")
                    bi = SC.activation(s_ds[:, :], s_d2[:, :], AF.Exp) if SC else None
                    inc(bi, "sp")
                    bi = SC.activation(s_Dvb[:, :], PS[0][:, 160:160 + HL], AF.Exp) if SC else None
                    inc(bi, "sp")
                    cap_exp = cnt["sp"]
                    wt(VE, "sp", cap_exp)
                    bi = VE.tensor_tensor(
                        s_Mall[:, :].rearrange("p (h l) -> p h l", h=HL),
                        s_scm[:, :].unsqueeze(1).broadcast_to([128, HL, CH]),
                        s_f[:, :].unsqueeze(2).broadcast_to([128, HL, CH]),
                        ALU.mult) if VE else None
                    inc(bi, "vp")
                    bi = VE.tensor_tensor(
                        s_xd[:, :].rearrange("p (h q) -> p h q", h=HL),
                        s_zx[:, zoff + 1024:zoff + 2048].rearrange("p (h q) -> p h q", h=HL),
                        s_ds[:, :].unsqueeze(2).broadcast_to([128, HL, P]),
                        ALU.mult) if VE else None
                    inc(bi, "vp")
                    cap_mall = cnt["vp"]

                    wt(TE, "vp", cap_mall)
                    for h in range(HL):
                        bank = PS[2] if h < 8 else PS[3]
                        co = (h % 8) * 64
                        bi = TE.matmul(bank[:, co:co + 64],
                                       s_Mall[:, h * CH:(h + 1) * CH],
                                       s_zx[:, zoff + 1024 + h * P:zoff + 1024 + (h + 1) * P],
                                       start=True, stop=True) if TE else None
                        inc(bi, "tp")
                    for half in range(2):
                        bi = TE.matmul(PS[4 + half][:, :],
                                       s_C[:, cloc * 128:(cloc + 1) * 128],
                                       S_state[:, half * 512:(half + 1) * 512],
                                       start=True, stop=True) if TE else None
                        inc(bi, "tp")
                    bi = TE.matmul(PS[6][:, :], s_Btm[:, :],
                                   s_xd[:, 0:512], start=True, stop=True) if TE else None
                    inc(bi, "tp")
                    bi = TE.matmul(PS[7][:, :], s_Btm[:, :],
                                   s_xd[:, 512:1024], start=True, stop=True) if TE else None
                    cap_yd = inc(bi, "tyd")

                    wt(VE, "tyd", cap_yd)
                    bi = VE.tensor_tensor(
                        S_state[:, :].rearrange("p (h q) -> p h q", h=HL),
                        S_state[:, :].rearrange("p (h q) -> p h q", h=HL),
                        s_Dvb[:, :].unsqueeze(2).broadcast_to([128, HL, P]),
                        ALU.mult) if VE else None
                    inc(bi, "vp")
                    wt(VE, "vp", cnt["vp"])
                    for half in range(2):
                        bi = VE.tensor_tensor(S_state[:, half * 512:(half + 1) * 512],
                                              S_state[:, half * 512:(half + 1) * 512],
                                              PS[6 + half][:, :], ALU.add) if VE else None
                        inc(bi, "vp")
                    wt(VE, "vssd", caps.get(("vssd_last",), 0))
                    wt(VE, "vp", cnt["vp"])
                    for half in range(2):
                        bi = VE.tensor_tensor(
                            s_y[:, half * 512:(half + 1) * 512].rearrange(
                                "p (h q) -> p h q", h=8),
                            PS[2 + half][:, :].rearrange("p (h q) -> p h q", h=8),
                            s_e1[:, half * 8:(half + 1) * 8].unsqueeze(2).broadcast_to(
                                [128, 8, P]),
                            ALU.mult) if VE else None
                        inc(bi, "vp")
                        bi = VE.tensor_tensor(
                            s_g1[:, half * 512:(half + 1) * 512].rearrange(
                                "p (h q) -> p h q", h=8),
                            PS[4 + half][:, :].rearrange("p (h q) -> p h q", h=8),
                            s_E2[:, half * 8:(half + 1) * 8].unsqueeze(2).broadcast_to(
                                [128, 8, P]),
                            ALU.mult) if VE else None
                        inc(bi, "vp")
                    wt(VE, "vp", cnt["vp"])
                    bi = VE.tensor_tensor(s_y[:, :], s_y[:, :], s_g1[:, :], ALU.add) if VE else None
                    inc(bi, "vp")
                    wt(VE, "vp", cnt["vp"])
                    bi = VE.tensor_tensor(s_g1[:, :], s_zx[:, zoff + 1024:zoff + 2048],
                                          c_dfull[:, :], ALU.mult) if VE else None
                    inc(bi, "vp")
                    wt(VE, "vp", cnt["vp"])
                    bi = VE.tensor_tensor(s_y[:, :], s_y[:, :], s_g1[:, :], ALU.add) if VE else None
                    inc(bi, "vp")
                    cap_ypre = cnt["vp"]
                    wt(SC, "vzx", cap_vzx)
                    bi = SC.activation(s_sz[:, :], s_zx[:, zoff:zoff + 1024],
                                       AF.Silu) if SC else None
                    inc(bi, "sp")
                    cap_silu = cnt["sp"]
                    wt(VE, "sp", cap_silu)
                    wt(VE, "vp", cap_ypre)
                    bi = VE.tensor_tensor(s_y[:, :], s_y[:, :], s_sz[:, :], ALU.mult) if VE else None
                    vssd_c = inc(bi, "vssd")
                    caps[("vssd_last",)] = vssd_c
                    if gi == 0 and cloc == 0:
                        do_tap("y0", s_y[:, :], "vssd", vssd_c)
                        do_tap("sst", S_state[:, :], "vssd", vssd_c)

                    wt(TE, "vssd", vssd_c)
                    for e8 in range(8):
                        bank = PS[2] if e8 < 4 else PS[3]
                        co = (e8 % 4) * 128
                        bi = TE.transpose(bank[:, co:co + 128],
                                          s_y[:, e8 * 128:(e8 + 1) * 128],
                                          k_id[:, :]) if TE else None
                        inc(bi, "typ")
                    cap_typ = cnt["typ"]
                    wt(VE, "typ", cap_typ)
                    for e8 in range(8):
                        bank = PS[2] if e8 < 4 else PS[3]
                        co = (e8 % 4) * 128
                        bi = VE.tensor_copy(
                            s_yfm[:, e8 * GT + cloc * 128:e8 * GT + (cloc + 1) * 128],
                            bank[:, co:co + 128]) if VE else None
                        inc(bi, "vyf")

            wt(TE, "vyf", cnt["vyf"])
            wt(TE, "sp", cnt["sp"])
            q = gi // 2
            th = gi % 2
            for half in range(2):
                if gi > 0 or half > 0:
                    wt(TE, "vpo", caps.get(("vpo_half",), 0))
                for e in range(8):
                    si, soff = strip_take()
                    wt(TE, WLD[si % NSLOT], 16 * (si // NSLOT + 1))
                    last = None
                    for m in range(8):
                        mg = half * 8 + m
                        last = TE.matmul(PS[m][:, :],
                                         r_w[:, soff + mg * 128:soff + (mg + 1) * 128],
                                         s_yfm[:, e * GT:(e + 1) * GT],
                                         start=(e == 0), stop=(e == 7)) if TE else None
                    inc(last, "sfree")
                    rec_ev(f"sd:{si}", "sfree")
                cap_op = cnt["sfree"]
                wt(VE, "sfree", cap_op)
                wt(VE, "post", caps.get(("post_half",), 0))
                for m in range(8):
                    bi = VE.tensor_copy(s_po[:, m * GT:(m + 1) * GT],
                                        PS[m][:, :]) if VE else None
                    inc(bi, "vpo")
                caps[("vpo_half",)] = cnt["vpo"]
                wt(GP, "vpo", cnt["vpo"])
                for m in range(8):
                    mg = half * 8 + m
                    bi = GP.dma_start(
                        out=part_out[th][q, mg * 128:(mg + 1) * 128, :],
                        in_=s_po[:, m * GT:(m + 1) * GT]) if GP else None
                    inc(bi, "post", 16)
                caps[("post_half",)] = cnt["post"]
                if gi == 0 and half == 0:
                    do_tap("po", s_po[:, :], "vpo", caps[("vpo_half",)])
            if gi == 0:
                do_tap("yf", s_yfm[:, :], "vyf", cnt["vyf"])

            caps[("vpo", gi)] = cnt["vpo"]
            caps[("sp", gi)] = cnt["sp"]
            caps[("sfree", gi)] = cnt["sfree"]
            if gi == NG - 2:
                wt(GP, "post", cnt["post"])
                bi = GP.collective_compute(
                    "ReduceScatter", ALU.add,
                    replica_groups=[[0, 1, 2, 3], [4, 5, 6, 7]],
                    ins=[part_out[0].ap().opt()],
                    outs=[rs_out[0].ap().opt()]) if GP else None
                inc(bi, "cc")

        for Eng in (TE, VE, SC, GP, SY):
            wt(Eng, "cc", 1)
        wt(GP, "post", cnt["post"])
        bi = GP.iota(k_iota[:, :], pattern=[[1, 192]], base=0,
                     channel_multiplier=0,
                     allow_small_or_imprecise_dtypes=True) if GP else None
        inc(bi, "gp")
        for Eng in (TE, VE, SC):
            wt(Eng, "gp", cnt["gp"])
        bi = GP.collective_compute(
            "ReduceScatter", ALU.add,
            replica_groups=[[0, 1, 2, 3], [4, 5, 6, 7]],
            ins=[part_out[1].ap().opt()],
            outs=[rs_out[1].ap().opt()]) if GP else None
        inc(bi, "cc")

        for tg in range(2):
            if tg > 0:
                wt(SC, "sfree", caps[("c_sfree", 0)])
                wt(SC, "vx1", caps[("c_vx1", 0)])
                wt(SC, "trt", caps[("c_trt", 0)])
                wt(SC, "vfo", caps[("c_vfo", 0)])
            base_cld = cnt["cld"]
            if tg > 0:
                wt(SC, "cc", 2)
            for k in range(NK):
                bi = SC.dma_start(out=s_uq[:, k * GT:(k + 1) * GT],
                                  in_=uq[k * 128:(k + 1) * 128,
                                         tg * GT:(tg + 1) * GT]) if SC else None
                inc(bi, "cld", 16)
                bi = SC.dma_start(out=s_g[:, k * GT:(k + 1) * GT],
                                  in_=rs_out[tg][k * 128:(k + 1) * 128, :]) if SC else None
                inc(bi, "cld", 16)

            if tg > 0:
                wt(TE, "vfo", caps[("c_vfo", 0)])
                wt(TE, "vg", cnt["vg"])
            for k in range(NK):
                if k == 0:
                    wt(TE, "cld", base_cld + 32 * NK)
                bi = TE.matmul(PS[0][0:1, :], k_ones[:, :],
                               s_uq[:, k * GT:(k + 1) * GT],
                               start=(k == 0), stop=(k == NK - 1)) if TE else None
                if k == NK - 1:
                    inc(bi, "tstat")
                if k == 0:
                    wt(SC, "cld", base_cld + 32 * NK)
                wt(SC, "tsq2", cnt["tsq2"] - 1)
                sbi = SC.activation(s_sq2[:, (k % 2) * GT:(k % 2 + 1) * GT],
                                    s_uq[:, k * GT:(k + 1) * GT], AF.Square) if SC else None
                inc(sbi, "sq2")
                wt(TE, "sq2", cnt["sq2"])
                bi = TE.matmul(PS[1][0:1, :], k_ones[:, :],
                               s_sq2[:, (k % 2) * GT:(k % 2 + 1) * GT],
                               start=(k == 0), stop=(k == NK - 1)) if TE else None
                inc(bi, "tsq2")
            wt(VE, "tstat", cnt["tstat"])
            wt(VE, "tsq2", cnt["tsq2"])
            if tg > 0:
                wt(VE, "outd", caps[("c_outd", 0)])
                wt(VE, "sfree", caps[("c_sfree", 0)])
            bi = VE.tensor_scalar_mul(R[0:1, 0 * GT:1 * GT], PS[0][0:1, :], 1.0 / D) if VE else None
            inc(bi, "vp")
            bi = VE.tensor_scalar_mul(R[0:1, 1 * GT:2 * GT], PS[1][0:1, :], 1.0 / D) if VE else None
            inc(bi, "vp")
            wt(VE, "vp", cnt["vp"])
            bi = VE.tensor_tensor(R[0:1, 2 * GT:3 * GT], R[0:1, 0 * GT:1 * GT], R[0:1, 0 * GT:1 * GT], ALU.mult) if VE else None
            inc(bi, "vp")
            wt(VE, "vp", cnt["vp"])
            bi = VE.tensor_tensor(R[0:1, 1 * GT:2 * GT], R[0:1, 1 * GT:2 * GT], R[0:1, 2 * GT:3 * GT], ALU.subtract) if VE else None
            inc(bi, "vp")
            wt(VE, "vp", cnt["vp"])
            bi = VE.tensor_scalar_add(R[0:1, 2 * GT:3 * GT], R[0:1, 1 * GT:2 * GT], EPS) if VE else None
            inc(bi, "vp")
            wt(SC, "vp", cnt["vp"])
            bi = SC.activation(R[0:1, 3 * GT:4 * GT], R[0:1, 2 * GT:3 * GT], AF.Ln) if SC else None
            inc(bi, "sp")
            wt(SC, "sp", cnt["sp"])
            bi = SC.activation(R[0:1, 3 * GT:4 * GT], R[0:1, 3 * GT:4 * GT], AF.Exp, scale=-0.5) if SC else None
            inc(bi, "sp")
            wt(VE, "sp", cnt["sp"])
            bi = VE.tensor_scalar_add(R[0:1, 4 * GT:5 * GT], R[0:1, 3 * GT:4 * GT], 1.0) if VE else None
            inc(bi, "vp")
            bi = VE.scalar_tensor_tensor(R[0:1, 5 * GT:6 * GT], R[0:1, 0 * GT:1 * GT], -1.0, R[0:1, 3 * GT:4 * GT],
                                         ALU.mult, ALU.mult) if VE else None
            inc(bi, "vp")
            cap_r = cnt["vp"]
            wt(TE, "vp", cap_r)
            bi = TE.matmul(PS[2][:, :], k_ones1[:, :], R[0:1, 4 * GT:5 * GT],
                           start=True, stop=True) if TE else None
            inc(bi, "tp")
            bi = TE.matmul(PS[3][:, :], k_ones1[:, :], R[0:1, 5 * GT:6 * GT],
                           start=True, stop=True) if TE else None
            inc(bi, "tp")
            cap_bc2 = cnt["tp"]
            wt(VE, "tp", cap_bc2)
            for k in range(NK):
                bi = VE.tensor_tensor(s_fo[:, k * GT:(k + 1) * GT],
                                      s_uq[:, k * GT:(k + 1) * GT],
                                      PS[2][:, :], ALU.mult) if VE else None
                inc(bi, "vp")
                wt(VE, "vp", cnt["vp"])
                bi = VE.tensor_tensor(s_fo[:, k * GT:(k + 1) * GT],
                                      s_fo[:, k * GT:(k + 1) * GT],
                                      PS[3][:, :], ALU.add) if VE else None
                inc(bi, "vp")
                wt(VE, "vp", cnt["vp"])
                if k == 0:
                    wt(VE, "cld", base_cld + 32 * NK)
                bi = VE.tensor_tensor(s_fo[:, k * GT:(k + 1) * GT],
                                      s_fo[:, k * GT:(k + 1) * GT],
                                      s_g[:, k * GT:(k + 1) * GT], ALU.add) if VE else None
                inc(bi, "vx1")
            cap_x1 = cnt["vx1"]
            caps[("c_vx1", tg)] = cap_x1
            if tg == 0:
                do_tap("x1", s_fo[:, :], "vx1", cap_x1)

            wt(TE, "vx1", cap_x1)
            for k in range(NK):
                bi = TE.matmul(PS[0][0:1, :], k_ones[:, :],
                               s_fo[:, k * GT:(k + 1) * GT],
                               start=(k == 0), stop=(k == NK - 1)) if TE else None
                if k == NK - 1:
                    inc(bi, "tstat")
                wt(SC, "vx1", cap_x1)
                wt(SC, "tsq2", cnt["tsq2"] - 1)
                sbi = SC.activation(s_sq2[:, (k % 2) * GT:(k % 2 + 1) * GT],
                                    s_fo[:, k * GT:(k + 1) * GT], AF.Square) if SC else None
                inc(sbi, "sq2")
                wt(TE, "sq2", cnt["sq2"])
                bi = TE.matmul(PS[1][0:1, :], k_ones[:, :],
                               s_sq2[:, (k % 2) * GT:(k % 2 + 1) * GT],
                               start=(k == 0), stop=(k == NK - 1)) if TE else None
                inc(bi, "tsq2")
            wt(VE, "tstat", cnt["tstat"])
            wt(VE, "tsq2", cnt["tsq2"])
            bi = VE.tensor_scalar_mul(R[0:1, 0 * GT:1 * GT], PS[0][0:1, :], 1.0 / D) if VE else None
            inc(bi, "vp")
            bi = VE.tensor_scalar_mul(R[0:1, 1 * GT:2 * GT], PS[1][0:1, :], 1.0 / D) if VE else None
            inc(bi, "vp")
            wt(VE, "vp", cnt["vp"])
            bi = VE.tensor_tensor(R[0:1, 2 * GT:3 * GT], R[0:1, 0 * GT:1 * GT], R[0:1, 0 * GT:1 * GT], ALU.mult) if VE else None
            inc(bi, "vp")
            wt(VE, "vp", cnt["vp"])
            bi = VE.tensor_tensor(R[0:1, 1 * GT:2 * GT], R[0:1, 1 * GT:2 * GT], R[0:1, 2 * GT:3 * GT], ALU.subtract) if VE else None
            inc(bi, "vp")
            wt(VE, "vp", cnt["vp"])
            bi = VE.tensor_scalar_add(R[0:1, 2 * GT:3 * GT], R[0:1, 1 * GT:2 * GT], EPS) if VE else None
            inc(bi, "vp")
            wt(SC, "vp", cnt["vp"])
            bi = SC.activation(R[0:1, 3 * GT:4 * GT], R[0:1, 2 * GT:3 * GT], AF.Ln) if SC else None
            inc(bi, "sp")
            wt(SC, "sp", cnt["sp"])
            bi = SC.activation(R[0:1, 3 * GT:4 * GT], R[0:1, 3 * GT:4 * GT], AF.Exp, scale=-0.5) if SC else None
            inc(bi, "sp")
            wt(VE, "sp", cnt["sp"])
            bi = VE.scalar_tensor_tensor(R[0:1, 5 * GT:6 * GT], R[0:1, 0 * GT:1 * GT], -1.0, R[0:1, 3 * GT:4 * GT],
                                         ALU.mult, ALU.mult) if VE else None
            inc(bi, "vp")
            cap_r2 = cnt["vp"]

            wt(TE, "vp", cap_r2)
            for tc in range(4):
                bi = TE.transpose(PS[6][:, tc:tc + 1],
                                  R[0:1, 3 * GT + tc * 128:3 * GT + (tc + 1) * 128],
                                  k_id[0:1, 0:1]) if TE else None
                inc(bi, "tp")
                bi = TE.transpose(PS[6][:, 4 + tc:5 + tc],
                                  R[0:1, 5 * GT + tc * 128:5 * GT + (tc + 1) * 128],
                                  k_id[0:1, 0:1]) if TE else None
                inc(bi, "tp")
            cap_stc_t = cnt["tp"]
            wt(VE, "tp", cap_stc_t)
            bi = VE.tensor_copy(s_statc[:, :], PS[6][:, 0:8]) if VE else None
            inc(bi, "vrt")
            cap_statc = cnt["vrt"]

            for tc in range(4):
                for kb in range(2):
                    for j in range(8):
                        k = kb * 8 + j
                        bank = PS[4] if j < 4 else PS[5]
                        co = (j % 4) * 128
                        if j == 0:
                            wt(TE, "vrt", cnt["vrt"])
                        bi = TE.transpose(
                            bank[:, co:co + 128],
                            s_fo[:, k * GT + tc * 128:k * GT + (tc + 1) * 128],
                            k_id[:, :]) if TE else None
                        inc(bi, "tp")
                    wt(VE, "tp", cnt["tp"])
                    for j in range(8):
                        k = kb * 8 + j
                        bank = PS[4] if j < 4 else PS[5]
                        co = (j % 4) * 128
                        bi = VE.tensor_copy(
                            s_ln[:, tc * 2048 + k * 128:tc * 2048 + (k + 1) * 128],
                            bank[:, co:co + 128]) if VE else None
                        inc(bi, "vrt")
            cap_x1tm = cnt["vrt"]

            wt(VE, "vrt", cap_x1tm)
            for tc in range(4):
                bi = VE.tensor_scalar(s_ln[:, tc * 2048:(tc + 1) * 2048],
                                      s_ln[:, tc * 2048:(tc + 1) * 2048],
                                      s_statc[:, tc:tc + 1],
                                      s_statc[:, 4 + tc:5 + tc],
                                      ALU.mult, ALU.add) if VE else None
                inc(bi, "vrt")
            cap_lnd = cnt["vrt"]
            if tg == 0:
                do_tap("ln", s_ln[:, :], "vrt", cap_lnd)

            wt(TE, "vrt", cap_lnd)
            for k in range(NK):
                bi = TE.matmul(PS[4][0:8, :], c_wrt[:, k * 8:(k + 1) * 8],
                               s_fo[:, k * GT:(k + 1) * GT],
                               start=(k == 0), stop=(k == NK - 1)) if TE else None
                if k == NK - 1:
                    inc(bi, "trt")
            wt(VE, "trt", cnt["trt"])
            bi = VE.tensor_copy(s_lg[:, :], PS[4][0:8, :]) if VE else None
            inc(bi, "vrt")
            cap_lgev = cnt["vrt"]
            wt(TE, "vrt", cap_lgev)
            for tc in range(4):
                bi = TE.transpose(PS[5][:, tc * 8:(tc + 1) * 8],
                                  s_lg[:, tc * 128:(tc + 1) * 128],
                                  k_id[0:8, 0:8]) if TE else None
                inc(bi, "trt")
            wt(VE, "trt", cnt["trt"])
            bi = VE.tensor_copy(s_lgt[:, :], PS[5][:, 0:32]) if VE else None
            inc(bi, "vrt")
            for tc in range(4):
                wt(VE, "vrt", cnt["vrt"])
                bi = VE.tensor_scalar_mul(s_lgt[:, tc * 8:(tc + 1) * 8],
                                          s_lgt[:, tc * 8:(tc + 1) * 8],
                                          s_statc[:, tc:tc + 1]) if VE else None
                inc(bi, "vrt")
                bi = VE.tensor_scalar_mul(s_lg2[:, tc * 8:(tc + 1) * 8],
                                          c_cswb[:, :],
                                          s_statc[:, 4 + tc:5 + tc]) if VE else None
                inc(bi, "vrt")
                wt(VE, "vrt", cnt["vrt"])
                bi = VE.tensor_tensor(s_lgt[:, tc * 8:(tc + 1) * 8],
                                      s_lgt[:, tc * 8:(tc + 1) * 8],
                                      s_lg2[:, tc * 8:(tc + 1) * 8], ALU.add) if VE else None
                inc(bi, "vrt")
            for tc in range(4):
                blk = s_lgt[:, tc * 8:(tc + 1) * 8]
                wt(VE, "vrt", cnt["vrt"])
                bi = VE.reduce_max(s_m1[:, :], blk, axis=AX.X) if VE else None
                inc(bi, "vrt")
                wt(VE, "vrt", cnt["vrt"])
                bi = VE.tensor_scalar(s_lg2[:, tc * 8:(tc + 1) * 8], blk,
                                      s_m1[:, :], None, ALU.is_ge) if VE else None
                inc(bi, "vrt")
                wt(VE, "vrt", cnt["vrt"])
                bi = VE.scalar_tensor_tensor(s_wtk[:, tc * 8:(tc + 1) * 8],
                                             s_lg2[:, tc * 8:(tc + 1) * 8],
                                             -1e30, blk, ALU.mult, ALU.add) if VE else None
                inc(bi, "vrt")
                wt(VE, "vrt", cnt["vrt"])
                bi = VE.reduce_max(s_m2[:, :], s_wtk[:, tc * 8:(tc + 1) * 8],
                                   axis=AX.X) if VE else None
                inc(bi, "vrt")
                wt(VE, "vrt", cnt["vrt"])
                bi = VE.tensor_tensor(s_dd[:, :], s_m2[:, :], s_m1[:, :],
                                      ALU.subtract) if VE else None
                inc(bi, "vrt")
                wt(SC, "vrt", cnt["vrt"])
                bi = SC.activation(s_ee[:, :], s_dd[:, :], AF.Exp) if SC else None
                inc(bi, "sp")
                wt(VE, "sp", cnt["sp"])
                bi = VE.tensor_scalar_add(s_p1[:, :], s_ee[:, :], 1.0) if VE else None
                inc(bi, "vrt")
                wt(VE, "vrt", cnt["vrt"])
                bi = VE.reciprocal(s_p1[:, :], s_p1[:, :]) if VE else None
                inc(bi, "vrt")
                wt(VE, "vrt", cnt["vrt"])
                bi = VE.tensor_tensor(s_p2[:, :], s_p1[:, :], s_ee[:, :],
                                      ALU.mult) if VE else None
                inc(bi, "vrt")
                wt(VE, "vrt", cnt["vrt"])
                bi = VE.tensor_scalar(s_lg2[:, tc * 8:(tc + 1) * 8],
                                      s_wtk[:, tc * 8:(tc + 1) * 8],
                                      s_m2[:, :], s_p2[:, :], ALU.is_ge,
                                      ALU.mult) if VE else None
                inc(bi, "vrt")
                wt(VE, "vrt", cnt["vrt"])
                bi = VE.tensor_scalar(s_wtk[:, tc * 8:(tc + 1) * 8], blk,
                                      s_m1[:, :], s_p1[:, :], ALU.is_ge,
                                      ALU.mult) if VE else None
                inc(bi, "vrt")
                wt(VE, "vrt", cnt["vrt"])
                bi = VE.tensor_tensor(s_wtk[:, tc * 8:(tc + 1) * 8],
                                      s_wtk[:, tc * 8:(tc + 1) * 8],
                                      s_lg2[:, tc * 8:(tc + 1) * 8], ALU.add) if VE else None
                inc(bi, "vrt")
            cap_wtk = cnt["vrt"]
            if tg == 0:
                do_tap("wt", s_wtk[:, :], "vrt", cap_wtk)
                do_tap("lg", s_lgt[:, :], "vrt", cap_wtk)


            wt(TE, "vrt", cap_wtk)
            for tc in range(4):
                bi = TE.transpose(PS[5][0:8, tc * 128:(tc + 1) * 128],
                                  s_wtk[:, tc * 8:(tc + 1) * 8],
                                  k_id[:, :]) if TE else None
                inc(bi, "trt")
            wt(VE, "trt", cnt["trt"])
            for tc in range(4):
                bi = VE.tensor_copy(s_wfm8[:, tc * 128:(tc + 1) * 128],
                                    PS[5][0:8, tc * 128:(tc + 1) * 128]) if VE else None
                inc(bi, "vrt")
            cap_wfm8 = cnt["vrt"]

            wt(VE, "vrt", cap_wfm8)
            bi = VE.tensor_scalar(s_mrow[:, :], s_wfm8[:, :], 0.0,
                                  None, ALU.is_gt) if VE else None
            inc(bi, "vrt")
            if tg == 0:
                do_tap("msk", s_mrow[:, :], "vrt", cnt["vrt"])
            wt(VE, "vrt", cnt["vrt"])
            bi = VE.tensor_tensor_scan(
                s_rm8[:, :],
                k_ones[0:8, 0:1].broadcast_to([8, GT]),
                s_mrow[:, :], 0.0, ALU.mult, ALU.add) if VE else None
            inc(bi, "vrt")
            if tg == 0:
                do_tap("rmraw", s_rm8[:, :], "vrt", cnt["vrt"])
            wt(VE, "vrt", cnt["vrt"])
            bi = VE.tensor_tensor(s_rm8[:, :], s_rm8[:, :],
                                  s_mrow[:, :], ALU.mult) if VE else None
            inc(bi, "vrt")
            wt(VE, "vrt", cnt["vrt"])
            bi = VE.tensor_scalar_add(s_rm8[:, :], s_rm8[:, :], -1.0) if VE else None
            inc(bi, "vrt")
            cap_rm = cnt["vrt"]
            if tg == 0:
                do_tap("rm8", s_rm8[:, :], "vrt", cap_rm)
            wt(TE, "vrt", cap_rm)
            for tc in range(4):
                bi = TE.transpose(PS[6][:, 64 + tc * 8:64 + (tc + 1) * 8],
                                  s_rm8[:, tc * 128:(tc + 1) * 128],
                                  k_id[0:8, 0:8]) if TE else None
                inc(bi, "trt")
            wt(VE, "trt", cnt["trt"])
            bi = VE.tensor_copy(s_rmt[:, :], PS[6][:, 64:96]) if VE else None
            inc(bi, "vrt")
            cap_rmt = cnt["vrt"]
            if tg == 0:
                do_tap("rmt", s_rmt[:, :], "vrt", cap_rmt)

            for e in range(NE):
                wt(VE, "vrt", cap_rmt)
                wt(VE, "trt", cnt["trt"])
                for tc in range(4):
                    bi = VE.tensor_scalar(s_oh[:, tc * 192:(tc + 1) * 192],
                                          k_iota[:, :],
                                          s_rmt[:, tc * 8 + e:tc * 8 + e + 1],
                                          None, ALU.is_equal) if VE else None
                    inc(bi, "vg")
                cap_oh = cnt["vg"]
                if tg == 0 and e == 0:
                    do_tap("oh", s_oh[:, :], "vg", cap_oh)
                wt(TE, "vg", cap_oh)
                wt(TE, "vfo", cnt["vfo"])
                for sh in range(2):
                    if sh == 1:
                        wt(TE, "vg", cnt["vg"])
                    for tc in range(4):
                        bi = TE.transpose(
                            PS[6][0:96, tc * 128:(tc + 1) * 128],
                            s_oh[:, tc * 192 + sh * 96:tc * 192 + sh * 96 + 96],
                            k_id[:, :]) if TE else None
                        inc(bi, "tp")
                    wt(VE, "tp", cnt["tp"])
                    bi = VE.tensor_copy(s_ohT[0:96, sh * 512:(sh + 1) * 512],
                                        PS[6][0:96, 0:512]) if VE else None
                    inc(bi, "vg")
                for tc in range(4):
                    bi = TE.matmul(PS[7][0:1, 0:192],
                                   s_wtk[:, tc * 8 + e:tc * 8 + e + 1],
                                   s_oh[:, tc * 192:(tc + 1) * 192],
                                   start=(tc == 0), stop=(tc == 3)) if TE else None
                    if tc == 3:
                        inc(bi, "trt")
                wt(VE, "trt", cnt["trt"])
                bi = VE.tensor_copy(s_mrow[0:1, 0:192], PS[7][0:1, 0:192]) if VE else None
                inc(bi, "vg")
                wt(TE, "vg", cnt["vg"])
                bi = TE.matmul(PS[7][:, 256:448], k_ones1[:, :], s_mrow[0:1, 0:192],
                               start=True, stop=True) if TE else None
                inc(bi, "trt")
                wt(VE, "trt", cnt["trt"])
                bi = VE.tensor_copy(s_wb[:, :], PS[7][:, 256:448]) if VE else None
                inc(bi, "vg")
                cap_wb = cnt["vg"]
                if tg == 0 and e == 0:
                    do_tap("wb", s_wb[:, :], "vg", cap_wb)
                wt(TE, "vg", cap_wb)
                for m in range(NK):
                    bank = PS[m // 2]
                    co = (m % 2) * 256
                    for tc in range(4):
                        bi = TE.matmul(bank[:, co:co + 192],
                                       s_ln[:, tc * 2048 + m * 128:tc * 2048 + (m + 1) * 128],
                                       s_oh[:, tc * 192:(tc + 1) * 192],
                                       start=(tc == 0), stop=(tc == 3)) if TE else None
                    inc(bi, "trt")
                cap_gmm = cnt["trt"]
                wt(VE, "trt", cap_gmm)
                for m in range(NK):
                    bank = PS[m // 2]
                    co = (m % 2) * 256
                    bi = VE.tensor_copy(s_lnc[:, m * 192:(m + 1) * 192],
                                        bank[:, co:co + 192]) if VE else None
                    inc(bi, "vg")
                cap_lnc = cnt["vg"]
                if tg == 0 and e == 0:
                    do_tap("lnc", s_lnc[:, :], "vg", cap_lnc)
                for mh in range(2):
                    for k in range(NK):
                        si, soff = strip_take()
                        wt(TE, WLD[si % NSLOT], 16 * (si // NSLOT + 1))
                        if k == 0:
                            wt(TE, "vg", cap_lnc if mh == 0 else cnt["vg"])
                        last = None
                        for m8 in range(8):
                            last = TE.matmul(PS[m8][:, 0:192],
                                             r_w[:, soff + m8 * 128:soff + (m8 + 1) * 128],
                                             s_lnc[:, k * 192:(k + 1) * 192],
                                             start=(k == 0), stop=(k == NK - 1)) if TE else None
                        inc(last, "sfree")
                        rec_ev(f"sd:{si}", "sfree")
                    cap_fc = cnt["sfree"]
                    wt(VE, "sfree", cap_fc)
                    for m8 in range(8):
                        m = mh * 8 + m8
                        bank = PS[m8]
                        bi = VE.tensor_copy(s_sq2[:, 0:192], bank[:, 0:192]) if VE else None
                        inc(bi, "vp")
                        wt(VE, "vp", cnt["vp"])
                        bi = VE.scalar_tensor_tensor(s_gc[:, m * 192:(m + 1) * 192],
                                                     bank[:, 0:192], 0.0,
                                                     s_sq2[:, 0:192],
                                                     ALU.max, ALU.mult) if VE else None
                        inc(bi, "vp")
                        wt(VE, "vp", cnt["vp"])
                        bi = VE.tensor_tensor(s_gc[:, m * 192:(m + 1) * 192],
                                              s_gc[:, m * 192:(m + 1) * 192],
                                              s_wb[:, :], ALU.mult) if VE else None
                        inc(bi, "vg")
                cap_g = cnt["vg"]
                if tg == 0 and e == 0:
                    do_tap("gc", s_gc[:, :], "vg", cap_g)
                for k2 in range(NK):
                    si, soff = strip_take()
                    wt(TE, WLD[si % NSLOT], 16 * (si // NSLOT + 1))
                    if k2 == 0:
                        wt(TE, "vg", cap_g)
                    last = None
                    for sh in range(2):
                        for ns in range(4):
                            last = TE.matmul(
                                PS[sh * 4 + ns][0:96, :],
                                s_gc[:, k2 * 192 + sh * 96:k2 * 192 + sh * 96 + 96],
                                r_w[:, soff + ns * 512:soff + (ns + 1) * 512],
                                start=(k2 == 0), stop=(k2 == NK - 1)) if TE else None
                    inc(last, "sfree")
                    rec_ev(f"sd:{si}", "sfree")
                cap_pj = cnt["sfree"]
                wt(VE, "sfree", cap_pj)
                for sh in range(2):
                    for ns in range(4):
                        bi = VE.tensor_copy(
                            s_ytm[0:96, sh * 2048 + ns * 512:sh * 2048 + (ns + 1) * 512],
                            PS[sh * 4 + ns][0:96, :]) if VE else None
                        inc(bi, "vg")
                cap_y = cnt["vg"]
                if tg == 0 and e == 0:
                    do_tap("ytm", s_ytm[0:96, :], "vg", cap_y)
                for mh in range(2):
                    wt(TE, "vg", cap_y)
                    if mh == 1:
                        wt(TE, "vfo", cnt["vfo"])
                    for m in range(8):
                        mg = mh * 8 + m
                        bi = TE.matmul(PS[m][:, :],
                                       s_ytm[0:96, mg * 128:(mg + 1) * 128],
                                       s_ohT[0:96, 0:512],
                                       start=True, stop=False) if TE else None
                        bi = TE.matmul(PS[m][:, :],
                                       s_ytm[0:96, 2048 + mg * 128:2048 + (mg + 1) * 128],
                                       s_ohT[0:96, 512:1024],
                                       start=False, stop=True) if TE else None
                        inc(bi, "trt")
                    wt(VE, "trt", cnt["trt"])
                    for m in range(8):
                        mg = mh * 8 + m
                        bi = VE.tensor_tensor(s_fo[:, mg * GT:(mg + 1) * GT],
                                              s_fo[:, mg * GT:(mg + 1) * GT],
                                              PS[m][:, :], ALU.add) if VE else None
                        inc(bi, "vfo")
            caps[("c_vfo", tg)] = cnt["vfo"]
            if tg == 0:
                do_tap("fo", s_fo[:, :], "vfo", cnt["vfo"])
            caps[("c_sfree", tg)] = cnt["sfree"]
            caps[("c_trt", tg)] = cnt["trt"]
            wt(GP, "vfo", cnt["vfo"])
            for m in range(NK):
                bi = GP.dma_start(out=out[m * 128:(m + 1) * 128, tg * GT:(tg + 1) * GT],
                                  in_=s_fo[:, m * GT:(m + 1) * GT]) if GP else None
                inc(bi, "outd", 16)
            caps[("c_outd", tg)] = cnt["outd"]

        wt(GP, "outd", cnt["outd"])
        return cnt

    emit("pre", None)
    state["rec"] = False

    with nc.Block() as block:
        block.sync(lambda e: emit("sync", e))
        block.tensor(lambda e: emit("tensor", e))
        block.vector(lambda e: emit("vector", e))
        block.scalar(lambda e: emit("scalar", e))
        block.gpsimd(lambda e: emit("gpsimd", e))

    return nc


def prep_core_inputs(ins, c):
    g, j = c // 4, c % 4
    u = np.asarray(ins["L0_attn_u"][g])
    W = np.asarray(ins["L0_attn_in_proj_w"])
    Wout = np.asarray(ins["L0_attn_out_proj_w"])
    zr = W[j * 1024:(j + 1) * 1024]
    xr = W[4096 + j * 1024:4096 + (j + 1) * 1024]
    fm = np.concatenate([W[8192:8320], W[8320:8448],
                         W[8448 + 16 * j:8448 + 16 * (j + 1)]], 0)
    fmT = np.zeros((2048, 512), np.float32)
    fmT[:, 0:272] = fm.T
    cb = np.zeros((128, 512), np.float32)
    cb[:, 0:16] = np.asarray(ins["L0_attn_ln_w"]).reshape(16, 128).T
    cb[:, 16:32] = np.asarray(ins["L0_attn_ln_b"]).reshape(16, 128).T
    cb[:, 32] = np.asarray(ins["L0_attn_bnorm_w"])
    cb[:, 33] = np.asarray(ins["L0_attn_cnorm_w"])
    cb[0:16, 34] = np.asarray(ins["L0_attn_dt_bias"][16 * j:16 * (j + 1)])
    cb[0:16, 35] = np.asarray(ins["L0_attn_A_log"][16 * j:16 * (j + 1)])
    cb[:, 128:256] = (np.asarray(ins["L0_ffn_w_router"]).reshape(16, 128, 8)
                      .transpose(1, 0, 2).reshape(128, 128))
    dv = np.asarray(ins["L0_attn_D"][16 * j:16 * (j + 1)])
    return {
        "uT": np.ascontiguousarray(u.T),
        "uq": np.ascontiguousarray(u[j * 1024:(j + 1) * 1024].T),
        "w_tmT": np.ascontiguousarray(np.concatenate([zr, xr], 0).T),
        "w_fmT": fmT,
        "w_outT": np.ascontiguousarray(Wout.T[j * 1024:(j + 1) * 1024]),
        "w_fc": np.ascontiguousarray(ins["L0_ffn_w_fc"]),
        "w_pj": np.ascontiguousarray(ins["L0_ffn_w_proj"]),
        "cblk": cb,
        "dfull": np.ascontiguousarray(
            np.broadcast_to(np.repeat(dv, 64)[None, :], (128, 1024))),
    }


def assemble(results):
    out = np.empty((2, 4096, 2048), dtype=np.float32)
    for c in range(8):
        g, j = c // 4, c % 4
        out[g, j * 1024:(j + 1) * 1024, :] = results[c]["out"].T
    return out


def kernel(**inputs):
    nc = build()
    in_maps = [prep_core_inputs(inputs, c) for c in range(8)]
    res = run_bass_kernel_spmd(nc, in_maps, core_ids=list(range(8)))
    return assemble(res.results)
